# version 15
# baseline (speedup 1.0000x reference)
from contextlib import ExitStack

import numpy as np
import ml_dtypes
import concourse.bass as bass
import concourse.mybir as mybir
from concourse.bass_utils import run_bass_kernel_spmd

F32 = mybir.dt.float32
F16 = mybir.dt.float16
BF16 = mybir.dt.bfloat16
I32 = mybir.dt.int32
AF = mybir.ActivationFunctionType
ALU = mybir.AluOpType
AXX = mybir.AxisListType.X

D = 1024
T = 2048
NCH = 8
NKT = 16
QB = 512
NQB = T // QB
QB2 = 1024
EPS = 1e-5


def _load_act_table(nc, e, set_name):
    from concourse.hw_specs import get_activation_tables
    for idx, nm in enumerate(get_activation_tables(nc.m.arch)):
        if nm == set_name:
            return e.add_instruction(
                mybir.InstLoadActFuncSet(
                    name=nc.get_next_instruction_name(),
                    act_func_set_id=idx, ins=[], outs=[]))
    raise ValueError(set_name)


def build_program(debug=()):
    nc = bass.Bass("TRN2", target_bir_lowering=False,
                   detect_race_conditions=False)

    hT = nc.dram_tensor("hT", [D, T], F32, kind="ExternalInput").ap()
    wqT = nc.dram_tensor("wqT", [D, D], F32, kind="ExternalInput").ap()
    wkT = nc.dram_tensor("wkT", [D, 256], F32, kind="ExternalInput").ap()
    wvT = nc.dram_tensor("wvT", [D, 256], F32, kind="ExternalInput").ap()
    wfrT = nc.dram_tensor("wfrT", [D, D], F32, kind="ExternalInput").ap()
    wfiT = nc.dram_tensor("wfiT", [D, D], F32, kind="ExternalInput").ap()
    bq_in = nc.dram_tensor("bq_in", [128, 8], F32, kind="ExternalInput").ap()
    bk_in = nc.dram_tensor("bk_in", [128, 4], F32, kind="ExternalInput").ap()
    bv_in = nc.dram_tensor("bv_in", [128, 2], F32, kind="ExternalInput").ap()
    bf_in = nc.dram_tensor("bf_in", [128, 8], F32, kind="ExternalInput").ap()
    yT = nc.dram_tensor("yT", [D, T], F32, kind="ExternalOutput").ap()

    dbg = {}
    for name, shape, dt in debug:
        dbg[name] = nc.dram_tensor("dbg_" + name, shape, dt,
                                   kind="ExternalOutput").ap()

    iota_np = np.arange(D, dtype=np.float32)[None, :]
    ident32_np = np.eye(128, dtype=np.float32)
    identB_np = np.eye(128, dtype=np.float32).astype(ml_dtypes.bfloat16)
    selq_np = np.zeros((128, 128), dtype=np.float32)
    for c in range(8):
        selq_np[0:64, c * 16 + 2 * c] = 1.0
        selq_np[64:128, c * 16 + 2 * c + 1] = 1.0
    selq_np = selq_np.astype(np.float16)
    onestop_np = np.zeros((128, 1), dtype=np.float32)
    onestop_np[0:64, 0] = 1.0
    onestop_np = onestop_np.astype(np.float16)

    iota_d = nc.inline_tensor(iota_np, name="iota_c").ap()
    ident32_d = nc.inline_tensor(ident32_np, name="ident32_c").ap()
    identB_d = nc.inline_tensor(identB_np, name="identB_c").ap()
    selq_d = nc.inline_tensor(selq_np, name="selq_c").ap()
    onestop_d = nc.inline_tensor(onestop_np, name="onestop_c").ap()

    ctx = ExitStack()
    SEM = {}
    for s in ("sd", "sg", "st", "sv", "ss", "ph", "tok"):
        SEM[s] = ctx.enter_context(nc.semaphore(s))
    CNT = {k: 0 for k in SEM}
    PHN = [0]
    TOK = [0]

    def dma(e, out, in_, sem):
        e.dma_start(out=out, in_=in_).then_inc(SEM[sem], 16)
        CNT[sem] += 16

    def phase(name, sync=None, tensor=None, vector=None, scalar=None,
              gpsimd=None):
        k = PHN[0]

        def wrap(fn):
            def inner(e):
                e.wait_ge(SEM["ph"], 5 * k)
                if fn is not None:
                    fn(e)
                e.drain().then_inc(SEM["ph"], 1)
            return inner
        with nc.named_scope(name), nc.Block(name) as blk:
            blk.sync(wrap(sync))
            blk.tensor(wrap(tensor))
            blk.vector(wrap(vector))
            blk.scalar(wrap(scalar))
            blk.gpsimd(wrap(gpsimd))
        PHN[0] += 1

    class SP:

        def __init__(self):
            self.steps = []

        def step(self, eng, fn):
            self.steps.append((eng, fn))

        def emit(self, name):
            base = TOK[0]
            steps = list(self.steps)

            def mk(engname):
                def fn(e):
                    for i, (en, f) in enumerate(steps):
                        if en == engname:
                            e.wait_ge(SEM["tok"], base + i)
                            f(e)
                            e.drain().then_inc(SEM["tok"], 1)
                return fn
            phase(name, sync=mk("sync"), tensor=mk("tensor"),
                  vector=mk("vector"), scalar=mk("scalar"),
                  gpsimd=mk("gpsimd"))
            TOK[0] += len(steps)

    def pipe_phase(name, jobs, gpsimd=None, scalar=None, pre_tensor=None,
                   pre_vector=None, ps_shape=(128, QB), ps_dtype=F32):
        t0 = CNT["st"]
        v0 = CNT["sv"]
        pvi = 1 if pre_vector is not None else 0
        pp_ctx = ExitStack()
        pss = [pp_ctx.enter_context(
            nc.psum_tensor(f"pp_{name}{i}", list(ps_shape), ps_dtype))
            for i in range(2)]

        def ft(e):
            if pre_tensor is not None:
                pre_tensor(e)
            for n, (mm_fn, _) in enumerate(jobs):
                if n >= 2:
                    e.wait_ge(SEM["sv"], v0 + pvi + n - 1)
                elif pvi:
                    e.wait_ge(SEM["sv"], v0 + 1)
                mm_fn(e, pss[n % 2]).then_inc(SEM["st"], 1)

        def fv(e):
            if pre_vector is not None:
                pre_vector(e)
                e.drain().then_inc(SEM["sv"], 1)
            for n, (_, ev_fn) in enumerate(jobs):
                e.wait_ge(SEM["st"], t0 + n + 1)
                ev_fn(e, pss[n % 2]).then_inc(SEM["sv"], 1)

        CNT["st"] += len(jobs)
        CNT["sv"] += len(jobs) + pvi
        phase(name, tensor=ft, vector=fv, gpsimd=gpsimd, scalar=scalar)
        pp_ctx.close()

    xs = nc.alloc_sbuf_tensor("xs", [128, NCH * T], F16)
    M16 = nc.alloc_sbuf_tensor("M16", [128, NCH * D], F16)
    iota_sb = nc.alloc_sbuf_tensor("iota_sb", [1, D], F32)
    ident32 = nc.alloc_sbuf_tensor("ident32", [128, 128], F32)
    identB = nc.alloc_sbuf_tensor("identB", [128, 128], BF16)
    selq = nc.alloc_sbuf_tensor("selq", [128, 128], F16)
    onestop = nc.alloc_sbuf_tensor("onestop", [128, 1], F16)
    bq_sb = nc.alloc_sbuf_tensor("bq_sb", [128, 8], F32)
    bk_sb = nc.alloc_sbuf_tensor("bk_sb", [128, 4], F32)
    bv_sb = nc.alloc_sbuf_tensor("bv_sb", [128, 2], F32)
    bf_sb = nc.alloc_sbuf_tensor("bf_sb", [128, 8], F32)
    expbias = nc.alloc_sbuf_tensor("expbias", [128, 1], F32)
    negpi = nc.alloc_sbuf_tensor("negpi", [128, 1], F32)
    eps1 = nc.alloc_sbuf_tensor("eps1", [1, 1], F32)
    zero1 = nc.alloc_sbuf_tensor("zero1", [1, 1], F32)
    ones1_32 = nc.alloc_sbuf_tensor("ones1_32", [1, 128], F32)
    ones1_16 = nc.alloc_sbuf_tensor("ones1_16", [1, 128], F16)
    ones128_16 = nc.alloc_sbuf_tensor("ones128_16", [128, 1], F16)
    tmp32a = nc.alloc_sbuf_tensor("tmp32a", [128, QB2], F32)
    q2s = nc.alloc_sbuf_tensor("q2s", [128, NCH * QB], BF16)
    s_mean = nc.alloc_sbuf_tensor("s_mean", [1, QB], F32)
    s_msq = nc.alloc_sbuf_tensor("s_msq", [1, QB], F32)
    s_var = nc.alloc_sbuf_tensor("s_var", [1, QB], F32)
    s_inv = nc.alloc_sbuf_tensor("s_inv", [1, QB], F32)
    s_minv = nc.alloc_sbuf_tensor("s_minv", [1, QB], F32)
    r_inv = nc.alloc_sbuf_tensor("r_inv", [1, QB], F16)
    r_nminv = nc.alloc_sbuf_tensor("r_nminv", [1, QB], F16)
    qmax16 = nc.alloc_sbuf_tensor("qmax16", [16, 1], F32)
    rmax16 = nc.alloc_sbuf_tensor("rmax16", [16, 1], F32)
    kmax1 = nc.alloc_sbuf_tensor("kmax1", [1, 1], F32)
    krow1 = nc.alloc_sbuf_tensor("krow1", [1, 1], F32)
    qM = nc.alloc_sbuf_tensor("qM", [1, 1], F32)
    b2 = nc.alloc_sbuf_tensor("b2", [1, 1], F32)
    bnd = nc.alloc_sbuf_tensor("bnd", [1, 1], F32)
    bval = nc.alloc_sbuf_tensor("bval", [1, 1], F32)

    pre_ctx = ExitStack()
    C16 = pre_ctx.enter_context(nc.sbuf_tensor("C16", [128, NCH * D], F16))
    S16 = pre_ctx.enter_context(nc.sbuf_tensor("S16", [128, NCH * D], F16))
    Sn16 = pre_ctx.enter_context(nc.sbuf_tensor("Sn16", [128, NCH * D], F16))
    A_ctx = ExitStack()
    A16r = A_ctx.enter_context(nc.sbuf_tensor("A16r", [128, NCH * D], F16))
    A16i = A_ctx.enter_context(nc.sbuf_tensor("A16i", [128, NCH * D], F16))
    w_ctx = ExitStack()
    wr16 = w_ctx.enter_context(nc.sbuf_tensor("wr16", [128, NCH * D], F16))
    wi16 = w_ctx.enter_context(nc.sbuf_tensor("wi16", [128, NCH * D], F16))
    tw_ctx = ExitStack()
    ik32 = tw_ctx.enter_context(nc.sbuf_tensor("ik32", [128, D], I32))
    yci = tw_ctx.enter_context(nc.sbuf_tensor("yci", [128, D], I32))
    ycf = tw_ctx.enter_context(nc.sbuf_tensor("ycf", [128, D], F32))
    ysf = tw_ctx.enter_context(nc.sbuf_tensor("ysf", [128, D], F32))
    ps_tw_ctx = ExitStack()
    ps_dk = [ps_tw_ctx.enter_context(
        nc.psum_tensor(f"ps_dk{hh}", [128, QB], F32)) for hh in range(4)]

    sd0 = CNT["sd"]
    st0 = CNT["st"]
    sv0 = CNT["sv"]
    ss0 = CNT["ss"]

    def init_sync(e):
        dma(e, iota_sb[:, :], iota_d[:, :], "sd")
        dma(e, ident32[:, :], ident32_d[:, :], "sd")
        dma(e, identB[:, :], identB_d[:, :], "sd")
        dma(e, selq[:, :], selq_d[:, :], "sd")
        dma(e, onestop[:, :], onestop_d[:, :], "sd")
        dma(e, bq_sb[:, :], bq_in[:, :], "sd")
        dma(e, bk_sb[:, :], bk_in[:, :], "sd")
        dma(e, bv_sb[:, :], bv_in[:, :], "sd")
        dma(e, bf_sb[:, :], bf_in[:, :], "sd")

    def init_gpsimd(e):
        for c in range(NCH):
            dma(e, xs[:, c * T:(c + 1) * T], hT[c * 128:(c + 1) * 128, :],
                "sg")
        for c in range(NCH):
            dma(e, wr16[:, c * D:(c + 1) * D],
                wfrT[c * 128:(c + 1) * 128, :], "sg")
            dma(e, wi16[:, c * D:(c + 1) * D],
                wfiT[c * 128:(c + 1) * 128, :], "sg")

    def init_tensor(e):
        e.wait_ge(SEM["sd"], sd0 + 16)
        for c in range(NCH):
            if c >= 2:
                e.wait_ge(SEM["sv"], sv0 + c)
            for half in range(2):
                mm = e.matmul(ps_dk[(c % 2) * 2 + half][:, :],
                              iota_sb[0:1, c * 128:(c + 1) * 128],
                              iota_sb[0:1, half * QB:(half + 1) * QB],
                              start=True, stop=True)
            mm.then_inc(SEM["st"], 1)
    CNT["st"] += NCH

    def init_vector(e):
        e.memset(negpi[:, :], float(-np.pi))
        e.memset(eps1[:, :], float(EPS))
        e.memset(zero1[:, :], 0.0)
        e.memset(ones1_32[:, :], 1.0)
        e.memset(ones1_16[:, :], 1.0)
        e.memset(ones128_16[:, :], 1.0)
        e.memset(qmax16[:, :], 0.0)
        e.memset(kmax1[:, :], 0.0)
        e.drain().then_inc(SEM["sv"], 1)
        for c in range(NCH):
            e.wait_ge(SEM["st"], st0 + c + 1)
            if c >= 1:
                e.wait_ge(SEM["ss"], ss0 + c)
            e.tensor_copy(ik32[:, 0:QB], ps_dk[(c % 2) * 2][:, :])
            e.tensor_copy(ik32[:, QB:D], ps_dk[(c % 2) * 2 + 1][:, :])
            e.tensor_single_scalar(yci[:, :], ik32[:, :], 768, ALU.add)
            e.tensor_single_scalar(yci[:, :], yci[:, :], 1023, ALU.bitwise_and)
            e.tensor_copy(ycf[:, :], yci[:, :])
            e.tensor_single_scalar(yci[:, :], ik32[:, :], 512, ALU.add)
            e.tensor_single_scalar(yci[:, :], yci[:, :], 1023, ALU.bitwise_and)
            e.tensor_copy(ysf[:, :], yci[:, :])
            e.drain().then_inc(SEM["sv"], 1)
        e.wait_ge(SEM["ss"], ss0 + NCH)
        e.tensor_single_scalar(Sn16[:, :], S16[:, :], -1.0, ALU.mult)
    CNT["sv"] += NCH + 1

    def init_scalar(e):
        _load_act_table(nc, e, "trig_and_small")
        sc = float(2 * np.pi / D)
        for c in range(NCH):
            e.wait_ge(SEM["sv"], sv0 + c + 2)
            e.activation(out=C16[:, c * D:(c + 1) * D], in_=ycf[:, :],
                         func=AF.Sin, scale=sc, bias=negpi[:, 0:1])
            e.activation(out=S16[:, c * D:(c + 1) * D], in_=ysf[:, :],
                         func=AF.Sin, scale=sc,
                         bias=negpi[:, 0:1]).then_inc(SEM["ss"], 1)
        _load_act_table(nc, e, "natural_log_exp_and_others")
    CNT["ss"] += NCH

    phase("init", sync=init_sync, gpsimd=init_gpsimd, tensor=init_tensor,
          vector=init_vector, scalar=init_scalar)
    sd_base = CNT["sd"]
    sg_h0 = CNT["sg"]
    sg_h = sg_h0 - 16 * 16
    sg_w = CNT["sg"]
    ps_tw_ctx.close()
    tw_ctx.close()

    jobs = []
    for dst in range(2):
        for dchunk in range(NCH):
            for half in range(2):
                def mm_fn(e, ps, dst=dst, dchunk=dchunk, half=half,
                          first=(dst == 0 and dchunk == 0 and half == 0)):
                    if first:
                        e.wait_ge(SEM["sg"], sg_w)
                    t1 = C16 if dst == 0 else S16
                    t2 = Sn16 if dst == 0 else C16
                    for o in range(NCH):
                        e.matmul(ps[:, :],
                                 wr16[:, o * D + dchunk * 128:
                                      o * D + (dchunk + 1) * 128],
                                 t1[:, o * D + half * QB:
                                    o * D + (half + 1) * QB],
                                 start=(o == 0), stop=False)
                    for o in range(NCH):
                        mm = e.matmul(ps[:, :],
                                      wi16[:, o * D + dchunk * 128:
                                           o * D + (dchunk + 1) * 128],
                                      t2[:, o * D + half * QB:
                                         o * D + (half + 1) * QB],
                                      start=False, stop=(o == NCH - 1))
                    return mm

                def ev_fn(e, ps, dst=dst, dchunk=dchunk, half=half):
                    A = A16r if dst == 0 else A16i
                    return e.tensor_single_scalar(
                        A[:, dchunk * D + half * QB:
                          dchunk * D + (half + 1) * QB],
                        ps[:, :], float(1.0 / D), ALU.mult)
                jobs.append((mm_fn, ev_fn))
    pipe_phase("pre_A", jobs)
    w_ctx.close()

    jobs = []
    for achunk in range(NCH):
        for half in range(2):
            def mm_fn(e, ps, achunk=achunk, half=half):
                for dd in range(NCH):
                    e.matmul(ps[:, :],
                             C16[:, dd * D + achunk * 128:
                                 dd * D + (achunk + 1) * 128],
                             A16r[:, dd * D + half * QB:
                                  dd * D + (half + 1) * QB],
                             start=(dd == 0), stop=False)
                for dd in range(NCH):
                    mm = e.matmul(ps[:, :],
                                  S16[:, dd * D + achunk * 128:
                                      dd * D + (achunk + 1) * 128],
                                  A16i[:, dd * D + half * QB:
                                       dd * D + (half + 1) * QB],
                                  start=False, stop=(dd == NCH - 1))
                return mm

            def ev_fn(e, ps, achunk=achunk, half=half):
                return e.tensor_copy(
                    M16[:, achunk * D + half * QB:
                        achunk * D + (half + 1) * QB], ps[:, :])
            jobs.append((mm_fn, ev_fn))
    pipe_phase("pre_M", jobs)
    A_ctx.close()
    pre_ctx.close()

    attn_ctx = ExitStack()
    Q16 = attn_ctx.enter_context(nc.sbuf_tensor("Q16", [128, NCH * T], F16))
    K2 = attn_ctx.enter_context(nc.sbuf_tensor("K2", [128, 4 * T], F16))
    V2 = attn_ctx.enter_context(
        nc.sbuf_tensor("V2", [128, 4 * NKT * 128], BF16))
    PT = attn_ctx.enter_context(nc.sbuf_tensor("PT", [128, 8 * QB2], BF16))
    pw_ctx = ExitStack()
    wq16 = pw_ctx.enter_context(nc.sbuf_tensor("wq16", [128, NCH * D], F16))
    wk1 = pw_ctx.enter_context(nc.sbuf_tensor("wk1", [128, NCH * 256], F16))
    wk2 = pw_ctx.enter_context(nc.sbuf_tensor("wk2", [128, NCH * 512], F16))
    wv16 = pw_ctx.enter_context(nc.sbuf_tensor("wv16", [128, NCH * 256], BF16))
    V1 = pw_ctx.enter_context(nc.sbuf_tensor("V1", [128, 2 * T], BF16))

    sg_wl0 = CNT["sg"]

    def f_wl(e):
        for c in range(NCH):
            dma(e, wk1[:, c * 256:(c + 1) * 256],
                wkT[c * 128:(c + 1) * 128, :], "sg")
        for c in range(NCH):
            dma(e, wq16[:, c * D:(c + 1) * D],
                wqT[c * 128:(c + 1) * 128, :], "sg")
        for c in range(NCH):
            dma(e, wv16[:, c * 256:(c + 1) * 256],
                wvT[c * 128:(c + 1) * 128, :], "sg")

    def emit_dup(e, g):
        for f in range(NCH):
            src = wk1[:, f * 256 + g * 64:f * 256 + (g + 1) * 64]
            e.tensor_copy(
                wk2[:, f * 512 + g * 128:f * 512 + g * 128 + 64], src)
            last = e.tensor_copy(
                wk2[:, f * 512 + g * 128 + 64:f * 512 + (g + 1) * 128], src)
        return last

    def f_dup(e):
        e.wait_ge(SEM["sg"], sg_wl0 + 16 * 8)
        emit_dup(e, 0)

    jl = ([("k", g) for g in range(4)] + [("q", c) for c in range(NCH)]
          + [("v", vc) for vc in range(2)])
    jobs = []
    for kind, c in jl:
        for qb in range(NQB):
            def mm_fn(e, ps, kind=kind, c=c, qb=qb,
                      first=(kind == "k" and c == 0 and qb == 0)):
                if first:
                    e.wait_ge(SEM["sg"], sg_h)
                if kind == "q" and c == 0 and qb == 0:
                    e.wait_ge(SEM["sg"], sg_wl0 + 16 * 16)
                if kind == "v" and c == 0 and qb == 0:
                    e.wait_ge(SEM["sg"], sg_wl0 + 16 * 24)
                for f in range(NCH):
                    if kind == "q":
                        statn = wq16[:, f * D + c * 128:f * D + (c + 1) * 128]
                    elif kind == "k":
                        statn = wk2[:, f * 512 + c * 128:
                                    f * 512 + (c + 1) * 128]
                    else:
                        statn = wv16[:, f * 256 + c * 128:
                                     f * 256 + (c + 1) * 128]
                    mm = e.matmul(ps[:, :], statn,
                                  xs[:, f * T + qb * QB:f * T + (qb + 1) * QB],
                                  start=(f == 0), stop=(f == NCH - 1))
                return mm

            def ev_fn(e, ps, kind=kind, c=c, qb=qb):
                if kind == "q":
                    dst = Q16[:, c * T + qb * QB:c * T + (qb + 1) * QB]
                    bias = bq_sb[:, c:c + 1]
                elif kind == "k":
                    dst = K2[:, c * T + qb * QB:c * T + (qb + 1) * QB]
                    bias = bk_sb[:, c:c + 1]
                else:
                    dst = V1[:, c * T + qb * QB:c * T + (qb + 1) * QB]
                    bias = bv_sb[:, c:c + 1]
                last = e.tensor_scalar_add(dst, ps[:, :], bias)
                n = jl.index((kind, c)) * NQB + qb
                if n < 3:
                    last = emit_dup(e, n + 1)
                return last
            jobs.append((mm_fn, ev_fn))
    pipe_phase("proj", jobs, gpsimd=f_wl, pre_vector=f_dup)

    jobs = []
    for vc in range(2):
        for kt in range(NKT):
            def mm_fn(e, ps, vc=vc, kt=kt):
                return e.transpose(
                    ps[:, :],
                    V1[:, vc * T + kt * 128:vc * T + (kt + 1) * 128],
                    identB[:, :])

            def ev_fn(e, ps, vc=vc, kt=kt):
                for gg in range(2):
                    g = 2 * vc + gg
                    base = (g * NKT + kt) * 128
                    e.tensor_copy(V2[:, base:base + 64],
                                  ps[:, gg * 64:gg * 64 + 64])
                    ins = e.memset(V2[:, base + 64:base + 128], 1.0)
                return ins
            jobs.append((mm_fn, ev_fn))
    pipe_phase("vtrans", jobs, ps_shape=(128, 128), ps_dtype=BF16)
    pw_ctx.close()

    evac_ctx = ExitStack()
    rA = evac_ctx.enter_context(nc.sbuf_tensor("rA", [128, 2 * QB2], F32))
    rB = evac_ctx.enter_context(nc.sbuf_tensor("rB", [128, 2 * QB2], F32))
    rC = evac_ctx.enter_context(nc.sbuf_tensor("rC", [128, 2 * QB2], F32))

    ps5_ctx = ExitStack()
    ps_n = ps5_ctx.enter_context(nc.psum_tensor("ps_n", [16, QB], F32))
    ps_k = [ps5_ctx.enter_context(nc.psum_tensor(f"ps_k{i}", [1, QB], F32))
            for i in range(4)]
    ps_tq = ps5_ctx.enter_context(nc.psum_tensor("ps_tq", [16, 16], F32))
    ps_eb = ps5_ctx.enter_context(nc.psum_tensor("ps_eb", [128, 1], F32))
    st0 = CNT["st"]
    sv0 = CNT["sv"]
    ss0 = CNT["ss"]

    def bnd_tensor(e):
        for qb in range(NQB):
            e.wait_ge(SEM["sv"], sv0 + 2 * qb + 1)
            for c in range(NCH):
                mm = e.matmul(ps_n[:, :], selq[:, c * 16:(c + 1) * 16],
                              q2s[:, c * QB:(c + 1) * QB],
                              start=(c == 0), stop=(c == NCH - 1))
            mm.then_inc(SEM["st"], 1)
        for g in range(4):
            e.wait_ge(SEM["sv"], sv0 + 9 + 5 * g)
            for qb in range(NQB):
                mm = e.matmul(ps_k[qb][:, :], onestop[:, 0:1],
                              q2s[:, qb * QB:(qb + 1) * QB],
                              start=True, stop=True)
            mm.then_inc(SEM["st"], 1)
        e.wait_ge(SEM["sv"], sv0 + 8)
        e.transpose(ps_tq[0:1, 0:16], qmax16[:, 0:1],
                    ident32[0:16, 0:16]).then_inc(SEM["st"], 1)
        e.wait_ge(SEM["sv"], sv0 + 30)
        e.matmul(ps_eb[:, :], ones1_32[0:1, :], bval[0:1, 0:1],
                 start=True, stop=True).then_inc(SEM["st"], 1)
    CNT["st"] += 10

    def bnd_vector(e):
        for qb in range(NQB):
            if qb >= 1:
                e.wait_ge(SEM["st"], st0 + qb)
            for c in range(NCH):
                s_ = Q16[:, c * T + qb * QB:c * T + (qb + 1) * QB]
                mm = e.tensor_mul(q2s[:, c * QB:(c + 1) * QB], s_, s_)
            mm.then_inc(SEM["sv"], 1)
            e.wait_ge(SEM["st"], st0 + qb + 1)
            e.reduce_max(rmax16[:, 0:1], ps_n[:, :], axis=AXX)
            e.tensor_max(qmax16[:, 0:1], qmax16[:, 0:1],
                         rmax16[:, 0:1]).then_inc(SEM["sv"], 1)
        for g in range(4):
            e.wait_ge(SEM["st"], st0 + 4 + g)
            for qb in range(NQB):
                s_ = K2[:, g * T + qb * QB:g * T + (qb + 1) * QB]
                mm = e.tensor_mul(q2s[:, qb * QB:(qb + 1) * QB], s_, s_)
            mm.then_inc(SEM["sv"], 1)
            e.wait_ge(SEM["st"], st0 + 5 + g)
            for qb in range(NQB):
                e.reduce_max(krow1[:, 0:1], ps_k[qb][0:1, :], axis=AXX)
                mm = e.tensor_max(kmax1[:, 0:1], kmax1[:, 0:1], krow1[:, 0:1])
                mm.then_inc(SEM["sv"], 1)
        e.wait_ge(SEM["st"], st0 + 9)
        e.reduce_max(qM[0:1, 0:1], ps_tq[0:1, 0:16], axis=AXX)
        e.tensor_mul(b2[0:1, 0:1], qM[0:1, 0:1],
                     kmax1[0:1, 0:1]).then_inc(SEM["sv"], 1)
        e.wait_ge(SEM["ss"], ss0 + 1)
        e.tensor_single_scalar(bval[0:1, 0:1], bnd[0:1, 0:1],
                               float(-1.001 / 8.0), ALU.mult)
        e.tensor_single_scalar(bval[0:1, 0:1], bval[0:1, 0:1],
                               float(-0.001 / 8.0),
                               ALU.add).then_inc(SEM["sv"], 1)
        e.wait_ge(SEM["st"], st0 + 10)
        e.tensor_copy(expbias[:, 0:1], ps_eb[:, 0:1])
    CNT["sv"] += 30

    def bnd_scalar(e):
        e.wait_ge(SEM["sv"], sv0 + 29)
        e.activation(out=bnd[0:1, 0:1], in_=b2[0:1, 0:1], func=AF.Ln,
                     scale=1.0, bias=zero1[0:1, 0:1])
        e.activation(out=bnd[0:1, 0:1], in_=bnd[0:1, 0:1], func=AF.Exp,
                     scale=0.5, bias=zero1[0:1, 0:1]).then_inc(SEM["ss"], 1)
    CNT["ss"] += 1

    phase("bound", tensor=bnd_tensor, vector=bnd_vector, scalar=bnd_scalar)
    ps5_ctx.close()

    ps6_ctx = ExitStack()
    ps_s = [ps6_ctx.enter_context(nc.psum_tensor(f"ps_s{h}", [128, QB2], F32))
            for h in range(2)]
    ps_o = [ps6_ctx.enter_context(nc.psum_tensor(f"ps_o{h}", [128, QB2], F32))
            for h in range(2)]
    st0 = CNT["st"]
    sv0 = CNT["sv"]
    ss0 = CNT["ss"]
    sg0 = CNT["sg"]
    NIT = 16

    def attn_tensor(e):
        DL = 3
        for it in range(NIT):
            pr, j = it // 2, it % 2
            g = pr // 2
            itss = ss0 + it * 32
            for kt in range(NKT + DL):
                if kt < NKT:
                    def smm(e, h, half):
                        return e.matmul(
                            ps_s[h][:, half * QB:(half + 1) * QB],
                            K2[h * 64:(h + 1) * 64,
                               g * T + kt * 128:g * T + (kt + 1) * 128],
                            Q16[h * 64:(h + 1) * 64,
                                pr * T + j * QB2 + half * QB:
                                pr * T + j * QB2 + (half + 1) * QB],
                            start=True, stop=True,
                            tile_position=(h * 64, 0))
                    if kt > 0:
                        e.wait_ge(SEM["ss"], itss + 2 * (kt - 1) + 1)
                    else:
                        e.wait_ge(SEM["ss"], itss - 1 if it > 0 else 0)
                    smm(e, 0, 0)
                    smm(e, 0, 1).then_inc(SEM["st"], 1)
                    if kt > 0:
                        e.wait_ge(SEM["ss"], itss + 2 * (kt - 1) + 2)
                    else:
                        e.wait_ge(SEM["ss"], itss)
                    smm(e, 1, 0)
                    smm(e, 1, 1).then_inc(SEM["st"], 1)
                    CNT["st"] += 2
                if kt >= DL:
                    kc = kt - DL
                    e.wait_ge(SEM["ss"], itss + 2 * kc + 2)
                    vb = (g * NKT + kc) * 128
                    for h in range(2):
                        if kc == 0 and it > 0:
                            e.wait_ge(SEM["sv"], sv0 + 3 * it - 2 + h)
                        pt = PT[:, (h * 4 + kc % 4) * QB2:
                                (h * 4 + kc % 4 + 1) * QB2]
                        for half in range(2):
                            mm = e.matmul(
                                ps_o[h][:, half * QB:(half + 1) * QB],
                                V2[:, vb:vb + 128],
                                pt[:, half * QB:(half + 1) * QB],
                                start=(kc == 0), stop=(kc == NKT - 1))
                    mm.then_inc(SEM["st"], 1)
                    CNT["st"] += 1

    def attn_scalar(e):
        DL = 3
        for it in range(NIT):
            itst = st0 + it * 48
            for kt in range(NKT):
                for h in range(2):
                    if kt < DL:
                        e.wait_ge(SEM["st"], itst + 2 * kt + h + 1)
                    else:
                        e.wait_ge(SEM["st"], itst + 3 * kt - DL + h + 1)
                    base = (h * 4 + kt % 4) * QB2
                    e.activation(out=PT[:, base:base + QB2],
                                 in_=ps_s[h][:, :], func=AF.Exp,
                                 scale=0.125,
                                 bias=expbias[:, 0:1]).then_inc(SEM["ss"], 1)
    CNT["ss"] += NIT * 32

    def emit_final(e, it):
        pr, j = it // 2, it % 2
        sl = slice(pr * T + j * QB2, pr * T + (j + 1) * QB2)
        par = (it % 2) * QB2
        e.wait_ge(SEM["sg"], sg0 + 32 * (it + 1))
        e.tensor_mul(tmp32a[0:64, :], rA[0:64, par:par + QB2],
                     rC[0:64, par:par + QB2])
        e.tensor_add(xs[0:64, sl], tmp32a[0:64, :], xs[0:64, sl])
        e.tensor_mul(tmp32a[64:128, :], rC[64:128, par:par + QB2],
                     rB[64:128, par:par + QB2])
        e.tensor_add(xs[64:128, sl], tmp32a[64:128, :], xs[64:128, sl])

    def attn_vector(e):
        for it in range(NIT):
            par = (it % 2) * QB2
            e.wait_ge(SEM["st"], st0 + it * 48 + 48)
            e.tensor_copy(rA[:, par:par + QB2],
                          ps_o[0][:, :]).then_inc(SEM["sv"], 1)
            e.tensor_copy(rB[:, par:par + QB2],
                          ps_o[1][:, :]).then_inc(SEM["sv"], 1)
            e.reciprocal(rA[64:128, par:par + QB2], rA[64:128, par:par + QB2])
            e.reciprocal(rB[64:128, par:par + QB2], rB[64:128, par:par + QB2])
            e.drain().then_inc(SEM["sv"], 1)
            if it > 0:
                emit_final(e, it - 1)
        emit_final(e, NIT - 1)
    CNT["sv"] += NIT * 3

    def attn_gpsimd(e):
        for it in range(NIT):
            par = (it % 2) * QB2
            e.wait_ge(SEM["sv"], sv0 + 3 * it + 3)
            dma(e, rC[0:64, par:par + QB2],
                rA[64:128, par:par + QB2], "sg")
            dma(e, rC[64:128, par:par + QB2],
                rB[0:64, par:par + QB2], "sg")

    phase("attn", tensor=attn_tensor, scalar=attn_scalar, vector=attn_vector,
          gpsimd=attn_gpsimd)
    ps6_ctx.close()
    evac_ctx.close()
    attn_ctx.close()

    post_ctx = ExitStack()
    out32 = post_ctx.enter_context(
        nc.sbuf_tensor("out32", [128, NCH * T], F32))
    xsq = post_ctx.enter_context(
        nc.sbuf_tensor("xsq", [128, NCH * T], BF16))

    def emit_ln(tag, src16, srcapply, dstget, store=False):
        lnctx = ExitStack()
        ps_a = lnctx.enter_context(nc.psum_tensor(f"ps_{tag}a", [1, QB], F32))
        ps_b = lnctx.enter_context(nc.psum_tensor(f"ps_{tag}b", [1, QB], F32))
        ps_i = lnctx.enter_context(
            nc.psum_tensor(f"ps_{tag}i", [128, QB], F32))
        ps_m = lnctx.enter_context(
            nc.psum_tensor(f"ps_{tag}m", [128, QB], F32))
        t0 = CNT["st"]
        v0 = CNT["sv"]
        s0 = CNT["ss"]

        def f_tensor(e):
            for qb in range(NQB):
                e.wait_ge(SEM["sv"], v0 + (1 if qb == 0 else 3 * qb - 1))
                for c in range(NCH):
                    e.matmul(ps_a[:, :], ones128_16[:, 0:1], src16(c, qb),
                             start=(c == 0), stop=(c == NCH - 1))
                for c in range(NCH):
                    mm = e.matmul(ps_b[:, :], ones128_16[:, 0:1],
                                  xsq[:, c * T + qb * QB:c * T + (qb + 1) * QB],
                                  start=(c == 0), stop=(c == NCH - 1))
                mm.then_inc(SEM["st"], 1)
                e.wait_ge(SEM["sv"], v0 + 3 * qb + 3)
                e.matmul(ps_i[:, :], ones1_16[0:1, :], r_inv[0:1, :],
                         start=True, stop=True)
                e.matmul(ps_m[:, :], ones1_16[0:1, :], r_nminv[0:1, :],
                         start=True, stop=True).then_inc(SEM["st"], 1)
        CNT["st"] += 2 * NQB

        def f_vector(e):
            for c in range(NCH):
                for qb in range(NQB):
                    s = src16(c, qb)
                    e.tensor_mul(
                        xsq[:, c * T + qb * QB:c * T + (qb + 1) * QB], s, s)
            e.drain().then_inc(SEM["sv"], 1)
            for qb in range(NQB):
                e.wait_ge(SEM["st"], t0 + 2 * qb + 1)
                e.tensor_single_scalar(s_mean[0:1, :], ps_a[0:1, :],
                                       float(1.0 / D), ALU.mult)
                e.tensor_mul(s_msq[0:1, :], s_mean[0:1, :], s_mean[0:1, :])
                e.tensor_single_scalar(s_var[0:1, :], ps_b[0:1, :],
                                       float(1.0 / D), ALU.mult)
                e.tensor_sub(s_var[0:1, :], s_var[0:1, :],
                             s_msq[0:1, :]).then_inc(SEM["sv"], 1)
                e.wait_ge(SEM["ss"], s0 + qb + 1)
                e.tensor_mul(s_minv[0:1, :], s_mean[0:1, :], s_inv[0:1, :])
                e.tensor_copy(r_inv[0:1, :], s_inv[0:1, :])
                e.tensor_single_scalar(
                    r_nminv[0:1, :], s_minv[0:1, :], -1.0,
                    ALU.mult).then_inc(SEM["sv"], 1)
                e.wait_ge(SEM["st"], t0 + 2 * qb + 2)
                last = None
                for c in range(NCH):
                    e.tensor_mul(tmp32a[:, 0:QB], srcapply(c, qb), ps_i[:, :])
                    last = e.tensor_add(dstget(c, qb), tmp32a[:, 0:QB],
                                        ps_m[:, :])
                last.then_inc(SEM["sv"], 1)
        CNT["sv"] += 1 + 3 * NQB

        def f_scalar(e):
            for qb in range(NQB):
                e.wait_ge(SEM["sv"], v0 + 3 * qb + 2)
                e.activation(out=s_inv[0:1, :], in_=s_var[0:1, :], func=AF.Ln,
                             scale=1.0, bias=eps1[0:1, 0:1])
                e.activation(out=s_inv[0:1, :], in_=s_inv[0:1, :],
                             func=AF.Exp, scale=-0.5,
                             bias=zero1[0:1, 0:1]).then_inc(SEM["ss"], 1)
        CNT["ss"] += NQB

        f_sync = None
        if store:
            def f_sync(e):
                for qb in range(NQB):
                    e.wait_ge(SEM["sv"], v0 + 3 * qb + 4)
                    for c in range(NCH):
                        dma(e, yT[c * 128:(c + 1) * 128,
                                  qb * QB:(qb + 1) * QB],
                            out32[:, c * T + qb * QB:c * T + (qb + 1) * QB],
                            "sd")
        phase(f"ln_{tag}", tensor=f_tensor, vector=f_vector, scalar=f_scalar,
              sync=f_sync)
        lnctx.close()

    emit_ln("ln1",
            lambda c, qb: xs[:, c * T + qb * QB:c * T + (qb + 1) * QB],
            lambda c, qb: xs[:, c * T + qb * QB:c * T + (qb + 1) * QB],
            lambda c, qb: xs[:, c * T + qb * QB:c * T + (qb + 1) * QB])

    jobs = []
    for b in range(NCH):
        for qb in range(NQB):
            def mm_fn(e, ps, b=b, qb=qb):
                for a in range(NCH):
                    mm = e.matmul(
                        ps[:, :],
                        M16[:, a * D + b * 128:a * D + (b + 1) * 128],
                        xs[:, a * T + qb * QB:a * T + (qb + 1) * QB],
                        start=(a == 0), stop=(a == NCH - 1))
                return mm

            def ev_fn(e, ps, b=b, qb=qb):
                sl = slice(b * T + qb * QB, b * T + (qb + 1) * QB)
                e.tensor_scalar_add(out32[:, sl], ps[:, :], bf_sb[:, b:b + 1])
                return e.tensor_add(out32[:, sl], out32[:, sl], xs[:, sl])
            jobs.append((mm_fn, ev_fn))
    pipe_phase("ffn", jobs)

    p = SP()

    def f_cast(e):
        for c in range(NCH):
            e.tensor_copy(xs[:, c * T:(c + 1) * T],
                          out32[:, c * T:(c + 1) * T])
    p.step("vector", f_cast)
    p.emit("cast2")

    emit_ln("ln2",
            lambda c, qb: xs[:, c * T + qb * QB:c * T + (qb + 1) * QB],
            lambda c, qb: out32[:, c * T + qb * QB:c * T + (qb + 1) * QB],
            lambda c, qb: out32[:, c * T + qb * QB:c * T + (qb + 1) * QB],
            store=True)

    if dbg:
        p = SP()

        def f_dbg(e):
            if "M16" in dbg:
                dma(e, dbg["M16"][:, :], M16[:, :], "sg")
            if "xs" in dbg:
                dma(e, dbg["xs"][:, :], xs[:, :], "sg")
            if "expbias" in dbg:
                dma(e, dbg["expbias"][:, :], expbias[:, :], "sg")
        p.step("gpsimd", f_dbg)
        p.emit("dbg")

    p = SP()
    p.step("sync", lambda e: e.wait_ge(SEM["sd"], CNT["sd"]))
    p.emit("fin")

    return nc


_NC_CACHE = {}


def kernel(L0_attn_h, L0_attn_wq_w, L0_attn_wq_b, L0_attn_wk_w, L0_attn_wk_b,
           L0_attn_wv_w, L0_attn_wv_b, L0_ffn_w_freq_real, L0_ffn_w_freq_imag,
           L0_ffn_bias):
    h = np.asarray(L0_attn_h, dtype=np.float32)
    wq = np.asarray(L0_attn_wq_w, dtype=np.float32)
    wk = np.asarray(L0_attn_wk_w, dtype=np.float32)
    wv = np.asarray(L0_attn_wv_w, dtype=np.float32)
    wfr = np.asarray(L0_ffn_w_freq_real, dtype=np.float32)
    wfi = np.asarray(L0_ffn_w_freq_imag, dtype=np.float32)
    bq = np.asarray(L0_attn_wq_b, dtype=np.float32)
    bk = np.asarray(L0_attn_wk_b, dtype=np.float32)
    bv = np.asarray(L0_attn_wv_b, dtype=np.float32)
    bf = np.asarray(L0_ffn_bias, dtype=np.float32)

    wqT = np.ascontiguousarray(wq.T)
    wkT = np.ascontiguousarray(wk.T)
    wvT = np.ascontiguousarray(wv.T)
    wfrT = np.ascontiguousarray(wfr.T)
    wfiT = np.ascontiguousarray(wfi.T)
    bq2 = np.ascontiguousarray(bq.reshape(8, 128).T)
    bk2 = np.ascontiguousarray(np.tile(bk.reshape(4, 64), (1, 2)).T)
    bv2 = np.ascontiguousarray(bv.reshape(2, 128).T)
    bf2 = np.ascontiguousarray(bf.reshape(8, 128).T)

    if "nc" not in _NC_CACHE:
        _NC_CACHE["nc"] = build_program()
    nc = _NC_CACHE["nc"]

    in_maps = []
    for b in range(8):
        in_maps.append({
            "hT": np.ascontiguousarray(h[b].T),
            "wqT": wqT, "wkT": wkT, "wvT": wvT,
            "wfrT": wfrT, "wfiT": wfiT,
            "bq_in": bq2, "bk_in": bk2, "bv_in": bv2, "bf_in": bf2,
        })
    res = run_bass_kernel_spmd(nc, in_maps, core_ids=list(range(8)))
    out = np.stack([np.asarray(r["yT"]).T for r in res.results], axis=0)
    return np.ascontiguousarray(out.astype(np.float32))


# version 16
# speedup vs baseline: 1.0173x; 1.0173x over previous
from contextlib import ExitStack

import numpy as np
import ml_dtypes
import concourse.bass as bass
import concourse.mybir as mybir
from concourse.bass_utils import run_bass_kernel_spmd

F32 = mybir.dt.float32
F16 = mybir.dt.float16
BF16 = mybir.dt.bfloat16
I32 = mybir.dt.int32
AF = mybir.ActivationFunctionType
ALU = mybir.AluOpType
AXX = mybir.AxisListType.X

D = 1024
T = 2048
NCH = 8
NKT = 16
QB = 512
NQB = T // QB
QB2 = 1024
EPS = 1e-5


def _load_act_table(nc, e, set_name):
    from concourse.hw_specs import get_activation_tables
    for idx, nm in enumerate(get_activation_tables(nc.m.arch)):
        if nm == set_name:
            return e.add_instruction(
                mybir.InstLoadActFuncSet(
                    name=nc.get_next_instruction_name(),
                    act_func_set_id=idx, ins=[], outs=[]))
    raise ValueError(set_name)


def build_program(debug=()):
    nc = bass.Bass("TRN2", target_bir_lowering=False,
                   detect_race_conditions=False)

    hT = nc.dram_tensor("hT", [D, T], F32, kind="ExternalInput").ap()
    wqT = nc.dram_tensor("wqT", [D, D], F32, kind="ExternalInput").ap()
    wkT = nc.dram_tensor("wkT", [D, 256], F32, kind="ExternalInput").ap()
    wvT = nc.dram_tensor("wvT", [D, 256], F32, kind="ExternalInput").ap()
    wfrT = nc.dram_tensor("wfrT", [D, D], F32, kind="ExternalInput").ap()
    wfiT = nc.dram_tensor("wfiT", [D, D], F32, kind="ExternalInput").ap()
    bq_in = nc.dram_tensor("bq_in", [128, 8], F32, kind="ExternalInput").ap()
    bk_in = nc.dram_tensor("bk_in", [128, 4], F32, kind="ExternalInput").ap()
    bv_in = nc.dram_tensor("bv_in", [128, 2], F32, kind="ExternalInput").ap()
    bf_in = nc.dram_tensor("bf_in", [128, 8], F32, kind="ExternalInput").ap()
    yT = nc.dram_tensor("yT", [D, T], F32, kind="ExternalOutput").ap()

    dbg = {}
    for name, shape, dt in debug:
        dbg[name] = nc.dram_tensor("dbg_" + name, shape, dt,
                                   kind="ExternalOutput").ap()

    iota_np = np.arange(D, dtype=np.float32)[None, :]
    ident32_np = np.eye(128, dtype=np.float32)
    identB_np = np.eye(128, dtype=np.float32).astype(ml_dtypes.bfloat16)
    selq_np = np.zeros((128, 128), dtype=np.float32)
    for c in range(8):
        selq_np[0:64, c * 16 + 2 * c] = 1.0
        selq_np[64:128, c * 16 + 2 * c + 1] = 1.0
    selq_np = selq_np.astype(np.float16)
    onestop_np = np.zeros((128, 1), dtype=np.float32)
    onestop_np[0:64, 0] = 1.0
    onestop_np = onestop_np.astype(np.float16)

    iota_d = nc.inline_tensor(iota_np, name="iota_c").ap()
    ident32_d = nc.inline_tensor(ident32_np, name="ident32_c").ap()
    identB_d = nc.inline_tensor(identB_np, name="identB_c").ap()
    selq_d = nc.inline_tensor(selq_np, name="selq_c").ap()
    onestop_d = nc.inline_tensor(onestop_np, name="onestop_c").ap()

    ctx = ExitStack()
    SEM = {}
    for s in ("sd", "sg", "st", "sv", "ss", "ph", "tok"):
        SEM[s] = ctx.enter_context(nc.semaphore(s))
    CNT = {k: 0 for k in SEM}
    PHN = [0]
    TOK = [0]

    def dma(e, out, in_, sem):
        e.dma_start(out=out, in_=in_).then_inc(SEM[sem], 16)
        CNT[sem] += 16

    def phase(name, sync=None, tensor=None, vector=None, scalar=None,
              gpsimd=None):
        k = PHN[0]

        def wrap(fn):
            def inner(e):
                e.wait_ge(SEM["ph"], 5 * k)
                if fn is not None:
                    fn(e)
                e.drain().then_inc(SEM["ph"], 1)
            return inner
        with nc.named_scope(name), nc.Block(name) as blk:
            blk.sync(wrap(sync))
            blk.tensor(wrap(tensor))
            blk.vector(wrap(vector))
            blk.scalar(wrap(scalar))
            blk.gpsimd(wrap(gpsimd))
        PHN[0] += 1

    class SP:

        def __init__(self):
            self.steps = []

        def step(self, eng, fn):
            self.steps.append((eng, fn))

        def emit(self, name):
            base = TOK[0]
            steps = list(self.steps)

            def mk(engname):
                def fn(e):
                    for i, (en, f) in enumerate(steps):
                        if en == engname:
                            e.wait_ge(SEM["tok"], base + i)
                            f(e)
                            e.drain().then_inc(SEM["tok"], 1)
                return fn
            phase(name, sync=mk("sync"), tensor=mk("tensor"),
                  vector=mk("vector"), scalar=mk("scalar"),
                  gpsimd=mk("gpsimd"))
            TOK[0] += len(steps)

    def pipe_phase(name, jobs, gpsimd=None, scalar=None, pre_tensor=None,
                   pre_vector=None, ps_shape=(128, QB), ps_dtype=F32):
        t0 = CNT["st"]
        v0 = CNT["sv"]
        pvi = 1 if pre_vector is not None else 0
        pp_ctx = ExitStack()
        pss = [pp_ctx.enter_context(
            nc.psum_tensor(f"pp_{name}{i}", list(ps_shape), ps_dtype))
            for i in range(2)]

        def ft(e):
            if pre_tensor is not None:
                pre_tensor(e)
            for n, (mm_fn, _) in enumerate(jobs):
                if n >= 2:
                    e.wait_ge(SEM["sv"], v0 + pvi + n - 1)
                elif pvi:
                    e.wait_ge(SEM["sv"], v0 + 1)
                mm_fn(e, pss[n % 2]).then_inc(SEM["st"], 1)

        def fv(e):
            if pre_vector is not None:
                pre_vector(e)
                e.drain().then_inc(SEM["sv"], 1)
            for n, (_, ev_fn) in enumerate(jobs):
                e.wait_ge(SEM["st"], t0 + n + 1)
                ev_fn(e, pss[n % 2]).then_inc(SEM["sv"], 1)

        CNT["st"] += len(jobs)
        CNT["sv"] += len(jobs) + pvi
        phase(name, tensor=ft, vector=fv, gpsimd=gpsimd, scalar=scalar)
        pp_ctx.close()

    xs = nc.alloc_sbuf_tensor("xs", [128, NCH * T], F16)
    M16 = nc.alloc_sbuf_tensor("M16", [128, NCH * D], F16)
    iota_sb = nc.alloc_sbuf_tensor("iota_sb", [1, D], F32)
    ident32 = nc.alloc_sbuf_tensor("ident32", [128, 128], F32)
    identB = nc.alloc_sbuf_tensor("identB", [128, 128], BF16)
    selq = nc.alloc_sbuf_tensor("selq", [128, 128], F16)
    onestop = nc.alloc_sbuf_tensor("onestop", [128, 1], F16)
    bq_sb = nc.alloc_sbuf_tensor("bq_sb", [128, 8], F32)
    bk_sb = nc.alloc_sbuf_tensor("bk_sb", [128, 4], F32)
    bv_sb = nc.alloc_sbuf_tensor("bv_sb", [128, 2], F32)
    bf_sb = nc.alloc_sbuf_tensor("bf_sb", [128, 8], F32)
    expbias = nc.alloc_sbuf_tensor("expbias", [128, 1], F32)
    negpi = nc.alloc_sbuf_tensor("negpi", [128, 1], F32)
    eps1 = nc.alloc_sbuf_tensor("eps1", [1, 1], F32)
    zero1 = nc.alloc_sbuf_tensor("zero1", [1, 1], F32)
    ones1_32 = nc.alloc_sbuf_tensor("ones1_32", [1, 128], F32)
    ones1_16 = nc.alloc_sbuf_tensor("ones1_16", [1, 128], F16)
    ones128_16 = nc.alloc_sbuf_tensor("ones128_16", [128, 1], F16)
    tmp32a = nc.alloc_sbuf_tensor("tmp32a", [128, QB2], F32)
    q2s = nc.alloc_sbuf_tensor("q2s", [128, NCH * QB], BF16)
    s_mean = nc.alloc_sbuf_tensor("s_mean", [1, QB], F32)
    s_var = nc.alloc_sbuf_tensor("s_var", [1, QB], F32)
    s_msq = tmp32a[0:1, 0:QB]
    s_minv = tmp32a[0:1, QB:2 * QB]
    s_inv = nc.alloc_sbuf_tensor("s_inv", [1, QB], F32)
    r_inv = nc.alloc_sbuf_tensor("r_inv", [1, QB], F16)
    r_nminv = nc.alloc_sbuf_tensor("r_nminv", [1, QB], F16)
    r_inv2 = nc.alloc_sbuf_tensor("r_inv2", [1, QB], F16)
    r_nminv2 = nc.alloc_sbuf_tensor("r_nminv2", [1, QB], F16)
    qmax16 = nc.alloc_sbuf_tensor("qmax16", [16, 1], F32)
    rmax16 = nc.alloc_sbuf_tensor("rmax16", [16, 1], F32)
    kmax1 = nc.alloc_sbuf_tensor("kmax1", [1, 1], F32)
    krow1 = nc.alloc_sbuf_tensor("krow1", [1, 1], F32)
    qM = nc.alloc_sbuf_tensor("qM", [1, 1], F32)
    b2 = nc.alloc_sbuf_tensor("b2", [1, 1], F32)
    bnd = nc.alloc_sbuf_tensor("bnd", [1, 1], F32)
    bval = nc.alloc_sbuf_tensor("bval", [1, 1], F32)

    pre_ctx = ExitStack()
    C16 = pre_ctx.enter_context(nc.sbuf_tensor("C16", [128, NCH * D], F16))
    S16 = pre_ctx.enter_context(nc.sbuf_tensor("S16", [128, NCH * D], F16))
    Sn16 = pre_ctx.enter_context(nc.sbuf_tensor("Sn16", [128, NCH * D], F16))
    A_ctx = ExitStack()
    A16r = A_ctx.enter_context(nc.sbuf_tensor("A16r", [128, NCH * D], F16))
    A16i = A_ctx.enter_context(nc.sbuf_tensor("A16i", [128, NCH * D], F16))
    w_ctx = ExitStack()
    wr16 = w_ctx.enter_context(nc.sbuf_tensor("wr16", [128, NCH * D], F16))
    wi16 = w_ctx.enter_context(nc.sbuf_tensor("wi16", [128, NCH * D], F16))
    tw_ctx = ExitStack()
    ik32 = tw_ctx.enter_context(nc.sbuf_tensor("ik32", [128, D], I32))
    yci = tw_ctx.enter_context(nc.sbuf_tensor("yci", [128, D], I32))
    ycf = tw_ctx.enter_context(nc.sbuf_tensor("ycf", [128, D], F32))
    ysf = tw_ctx.enter_context(nc.sbuf_tensor("ysf", [128, D], F32))
    ps_tw_ctx = ExitStack()
    ps_dk = [ps_tw_ctx.enter_context(
        nc.psum_tensor(f"ps_dk{hh}", [128, QB], F32)) for hh in range(4)]

    sd0 = CNT["sd"]
    st0 = CNT["st"]
    sv0 = CNT["sv"]
    ss0 = CNT["ss"]

    def init_sync(e):
        dma(e, iota_sb[:, :], iota_d[:, :], "sd")
        dma(e, ident32[:, :], ident32_d[:, :], "sd")
        dma(e, identB[:, :], identB_d[:, :], "sd")
        dma(e, selq[:, :], selq_d[:, :], "sd")
        dma(e, onestop[:, :], onestop_d[:, :], "sd")
        dma(e, bq_sb[:, :], bq_in[:, :], "sd")
        dma(e, bk_sb[:, :], bk_in[:, :], "sd")
        dma(e, bv_sb[:, :], bv_in[:, :], "sd")
        dma(e, bf_sb[:, :], bf_in[:, :], "sd")

    def init_gpsimd(e):
        for c in range(NCH):
            dma(e, xs[:, c * T:(c + 1) * T], hT[c * 128:(c + 1) * 128, :],
                "sg")
        for c in range(NCH):
            dma(e, wr16[:, c * D:(c + 1) * D],
                wfrT[c * 128:(c + 1) * 128, :], "sg")
            dma(e, wi16[:, c * D:(c + 1) * D],
                wfiT[c * 128:(c + 1) * 128, :], "sg")

    def init_tensor(e):
        e.wait_ge(SEM["sd"], sd0 + 16)
        for c in range(NCH):
            if c >= 2:
                e.wait_ge(SEM["sv"], sv0 + c)
            for half in range(2):
                mm = e.matmul(ps_dk[(c % 2) * 2 + half][:, :],
                              iota_sb[0:1, c * 128:(c + 1) * 128],
                              iota_sb[0:1, half * QB:(half + 1) * QB],
                              start=True, stop=True)
            mm.then_inc(SEM["st"], 1)
    CNT["st"] += NCH

    def init_vector(e):
        e.memset(negpi[:, :], float(-np.pi))
        e.memset(eps1[:, :], float(EPS))
        e.memset(zero1[:, :], 0.0)
        e.memset(ones1_32[:, :], 1.0)
        e.memset(ones1_16[:, :], 1.0)
        e.memset(ones128_16[:, :], 1.0)
        e.memset(qmax16[:, :], 0.0)
        e.memset(kmax1[:, :], 0.0)
        e.drain().then_inc(SEM["sv"], 1)
        for c in range(NCH):
            e.wait_ge(SEM["st"], st0 + c + 1)
            if c >= 1:
                e.wait_ge(SEM["ss"], ss0 + c)
            e.tensor_copy(ik32[:, 0:QB], ps_dk[(c % 2) * 2][:, :])
            e.tensor_copy(ik32[:, QB:D], ps_dk[(c % 2) * 2 + 1][:, :])
            e.tensor_single_scalar(yci[:, :], ik32[:, :], 768, ALU.add)
            e.tensor_single_scalar(yci[:, :], yci[:, :], 1023, ALU.bitwise_and)
            e.tensor_copy(ycf[:, :], yci[:, :])
            e.tensor_single_scalar(yci[:, :], ik32[:, :], 512, ALU.add)
            e.tensor_single_scalar(yci[:, :], yci[:, :], 1023, ALU.bitwise_and)
            e.tensor_copy(ysf[:, :], yci[:, :])
            e.drain().then_inc(SEM["sv"], 1)
        e.wait_ge(SEM["ss"], ss0 + NCH)
        e.tensor_single_scalar(Sn16[:, :], S16[:, :], -1.0, ALU.mult)
    CNT["sv"] += NCH + 1

    def init_scalar(e):
        _load_act_table(nc, e, "trig_and_small")
        sc = float(2 * np.pi / D)
        for c in range(NCH):
            e.wait_ge(SEM["sv"], sv0 + c + 2)
            e.activation(out=C16[:, c * D:(c + 1) * D], in_=ycf[:, :],
                         func=AF.Sin, scale=sc, bias=negpi[:, 0:1])
            e.activation(out=S16[:, c * D:(c + 1) * D], in_=ysf[:, :],
                         func=AF.Sin, scale=sc,
                         bias=negpi[:, 0:1]).then_inc(SEM["ss"], 1)
        _load_act_table(nc, e, "natural_log_exp_and_others")
    CNT["ss"] += NCH

    phase("init", sync=init_sync, gpsimd=init_gpsimd, tensor=init_tensor,
          vector=init_vector, scalar=init_scalar)
    sd_base = CNT["sd"]
    sg_h0 = CNT["sg"]
    sg_h = sg_h0 - 16 * 16
    sg_w = CNT["sg"]
    ps_tw_ctx.close()
    tw_ctx.close()

    jobs = []
    for dst in range(2):
        for dchunk in range(NCH):
            for half in range(2):
                def mm_fn(e, ps, dst=dst, dchunk=dchunk, half=half,
                          first=(dst == 0 and dchunk == 0 and half == 0)):
                    if first:
                        e.wait_ge(SEM["sg"], sg_w)
                    t1 = C16 if dst == 0 else S16
                    t2 = Sn16 if dst == 0 else C16
                    for o in range(NCH):
                        e.matmul(ps[:, :],
                                 wr16[:, o * D + dchunk * 128:
                                      o * D + (dchunk + 1) * 128],
                                 t1[:, o * D + half * QB:
                                    o * D + (half + 1) * QB],
                                 start=(o == 0), stop=False)
                    for o in range(NCH):
                        mm = e.matmul(ps[:, :],
                                      wi16[:, o * D + dchunk * 128:
                                           o * D + (dchunk + 1) * 128],
                                      t2[:, o * D + half * QB:
                                         o * D + (half + 1) * QB],
                                      start=False, stop=(o == NCH - 1))
                    return mm

                def ev_fn(e, ps, dst=dst, dchunk=dchunk, half=half):
                    A = A16r if dst == 0 else A16i
                    return e.tensor_single_scalar(
                        A[:, dchunk * D + half * QB:
                          dchunk * D + (half + 1) * QB],
                        ps[:, :], float(1.0 / D), ALU.mult)
                jobs.append((mm_fn, ev_fn))
    pipe_phase("pre_A", jobs)
    w_ctx.close()

    jobs = []
    for achunk in range(NCH):
        for half in range(2):
            def mm_fn(e, ps, achunk=achunk, half=half):
                for dd in range(NCH):
                    e.matmul(ps[:, :],
                             C16[:, dd * D + achunk * 128:
                                 dd * D + (achunk + 1) * 128],
                             A16r[:, dd * D + half * QB:
                                  dd * D + (half + 1) * QB],
                             start=(dd == 0), stop=False)
                for dd in range(NCH):
                    mm = e.matmul(ps[:, :],
                                  S16[:, dd * D + achunk * 128:
                                      dd * D + (achunk + 1) * 128],
                                  A16i[:, dd * D + half * QB:
                                       dd * D + (half + 1) * QB],
                                  start=False, stop=(dd == NCH - 1))
                return mm

            def ev_fn(e, ps, achunk=achunk, half=half):
                last = e.tensor_copy(
                    M16[:, achunk * D + half * QB:
                        achunk * D + (half + 1) * QB], ps[:, :])
                if achunk // 4 == half:
                    off = achunk * D + achunk * 128
                    last = e.tensor_add(M16[:, off:off + 128],
                                        M16[:, off:off + 128], ident32[:, :])
                return last
            jobs.append((mm_fn, ev_fn))
    pipe_phase("pre_M", jobs)
    A_ctx.close()
    pre_ctx.close()

    attn_ctx = ExitStack()
    Q16 = attn_ctx.enter_context(nc.sbuf_tensor("Q16", [128, NCH * T], F16))
    K2 = attn_ctx.enter_context(nc.sbuf_tensor("K2", [128, 4 * T], F16))
    V2 = attn_ctx.enter_context(
        nc.sbuf_tensor("V2", [128, 4 * NKT * 128], BF16))
    PT = attn_ctx.enter_context(nc.sbuf_tensor("PT", [128, 8 * QB2], BF16))
    pw_ctx = ExitStack()
    wq16 = pw_ctx.enter_context(nc.sbuf_tensor("wq16", [128, NCH * D], F16))
    wk1 = pw_ctx.enter_context(nc.sbuf_tensor("wk1", [128, NCH * 256], F16))
    wk2 = pw_ctx.enter_context(nc.sbuf_tensor("wk2", [128, NCH * 512], F16))
    wv16 = pw_ctx.enter_context(nc.sbuf_tensor("wv16", [128, NCH * 256], BF16))
    V1 = pw_ctx.enter_context(nc.sbuf_tensor("V1", [128, 2 * T], BF16))

    sg_wl0 = CNT["sg"]

    def f_wl(e):
        for c in range(NCH):
            dma(e, wk1[:, c * 256:(c + 1) * 256],
                wkT[c * 128:(c + 1) * 128, :], "sg")
        for c in range(NCH):
            dma(e, wq16[:, c * D:(c + 1) * D],
                wqT[c * 128:(c + 1) * 128, :], "sg")
        for c in range(NCH):
            dma(e, wv16[:, c * 256:(c + 1) * 256],
                wvT[c * 128:(c + 1) * 128, :], "sg")

    def emit_dup(e, g):
        for f in range(NCH):
            src = wk1[:, f * 256 + g * 64:f * 256 + (g + 1) * 64]
            e.tensor_copy(
                wk2[:, f * 512 + g * 128:f * 512 + g * 128 + 64], src)
            last = e.tensor_copy(
                wk2[:, f * 512 + g * 128 + 64:f * 512 + (g + 1) * 128], src)
        return last

    def f_dup(e):
        e.wait_ge(SEM["sg"], sg_wl0 + 16 * 8)
        emit_dup(e, 0)

    jl = ([("k", g) for g in range(4)] + [("q", c) for c in range(NCH)]
          + [("v", vc) for vc in range(2)])
    jobs = []
    for kind, c in jl:
        for qb in range(NQB):
            def mm_fn(e, ps, kind=kind, c=c, qb=qb,
                      first=(kind == "k" and c == 0 and qb == 0)):
                if first:
                    e.wait_ge(SEM["sg"], sg_h)
                if kind == "q" and c == 0 and qb == 0:
                    e.wait_ge(SEM["sg"], sg_wl0 + 16 * 16)
                if kind == "v" and c == 0 and qb == 0:
                    e.wait_ge(SEM["sg"], sg_wl0 + 16 * 24)
                for f in range(NCH):
                    if kind == "q":
                        statn = wq16[:, f * D + c * 128:f * D + (c + 1) * 128]
                    elif kind == "k":
                        statn = wk2[:, f * 512 + c * 128:
                                    f * 512 + (c + 1) * 128]
                    else:
                        statn = wv16[:, f * 256 + c * 128:
                                     f * 256 + (c + 1) * 128]
                    mm = e.matmul(ps[:, :], statn,
                                  xs[:, f * T + qb * QB:f * T + (qb + 1) * QB],
                                  start=(f == 0), stop=(f == NCH - 1))
                return mm

            def ev_fn(e, ps, kind=kind, c=c, qb=qb):
                if kind == "q":
                    dst = Q16[:, c * T + qb * QB:c * T + (qb + 1) * QB]
                    bias = bq_sb[:, c:c + 1]
                elif kind == "k":
                    dst = K2[:, c * T + qb * QB:c * T + (qb + 1) * QB]
                    bias = bk_sb[:, c:c + 1]
                else:
                    dst = V1[:, c * T + qb * QB:c * T + (qb + 1) * QB]
                    bias = bv_sb[:, c:c + 1]
                last = e.tensor_scalar_add(dst, ps[:, :], bias)
                n = jl.index((kind, c)) * NQB + qb
                if n < 3:
                    last = emit_dup(e, n + 1)
                return last
            jobs.append((mm_fn, ev_fn))
    pipe_phase("proj", jobs, gpsimd=f_wl, pre_vector=f_dup)

    jobs = []
    for vc in range(2):
        for kt in range(NKT):
            def mm_fn(e, ps, vc=vc, kt=kt):
                return e.transpose(
                    ps[:, :],
                    V1[:, vc * T + kt * 128:vc * T + (kt + 1) * 128],
                    identB[:, :])

            def ev_fn(e, ps, vc=vc, kt=kt):
                for gg in range(2):
                    g = 2 * vc + gg
                    base = (g * NKT + kt) * 128
                    e.tensor_copy(V2[:, base:base + 64],
                                  ps[:, gg * 64:gg * 64 + 64])
                    ins = e.memset(V2[:, base + 64:base + 128], 1.0)
                return ins
            jobs.append((mm_fn, ev_fn))
    pipe_phase("vtrans", jobs, ps_shape=(128, 128), ps_dtype=BF16)
    pw_ctx.close()

    evac_ctx = ExitStack()
    rA = evac_ctx.enter_context(nc.sbuf_tensor("rA", [128, 2 * QB2], F32))
    rB = evac_ctx.enter_context(nc.sbuf_tensor("rB", [128, 2 * QB2], F32))
    rC = evac_ctx.enter_context(nc.sbuf_tensor("rC", [128, 2 * QB2], F32))

    ps5_ctx = ExitStack()
    ps_n = ps5_ctx.enter_context(nc.psum_tensor("ps_n", [16, QB], F32))
    ps_k = [ps5_ctx.enter_context(nc.psum_tensor(f"ps_k{i}", [1, QB], F32))
            for i in range(4)]
    ps_tq = ps5_ctx.enter_context(nc.psum_tensor("ps_tq", [16, 16], F32))
    ps_eb = ps5_ctx.enter_context(nc.psum_tensor("ps_eb", [128, 1], F32))
    st0 = CNT["st"]
    sv0 = CNT["sv"]
    ss0 = CNT["ss"]

    def bnd_tensor(e):
        for qb in range(NQB):
            e.wait_ge(SEM["sv"], sv0 + 2 * qb + 1)
            for c in range(NCH):
                mm = e.matmul(ps_n[:, :], selq[:, c * 16:(c + 1) * 16],
                              q2s[:, c * QB:(c + 1) * QB],
                              start=(c == 0), stop=(c == NCH - 1))
            mm.then_inc(SEM["st"], 1)
        for g in range(4):
            e.wait_ge(SEM["sv"], sv0 + 9 + 5 * g)
            for qb in range(NQB):
                mm = e.matmul(ps_k[qb][:, :], onestop[:, 0:1],
                              q2s[:, qb * QB:(qb + 1) * QB],
                              start=True, stop=True)
            mm.then_inc(SEM["st"], 1)
        e.wait_ge(SEM["sv"], sv0 + 8)
        e.transpose(ps_tq[0:1, 0:16], qmax16[:, 0:1],
                    ident32[0:16, 0:16]).then_inc(SEM["st"], 1)
        e.wait_ge(SEM["sv"], sv0 + 30)
        e.matmul(ps_eb[:, :], ones1_32[0:1, :], bval[0:1, 0:1],
                 start=True, stop=True).then_inc(SEM["st"], 1)
    CNT["st"] += 10

    def bnd_vector(e):
        for qb in range(NQB):
            if qb >= 1:
                e.wait_ge(SEM["st"], st0 + qb)
            for c in range(NCH):
                s_ = Q16[:, c * T + qb * QB:c * T + (qb + 1) * QB]
                mm = e.tensor_mul(q2s[:, c * QB:(c + 1) * QB], s_, s_)
            mm.then_inc(SEM["sv"], 1)
            e.wait_ge(SEM["st"], st0 + qb + 1)
            e.reduce_max(rmax16[:, 0:1], ps_n[:, :], axis=AXX)
            e.tensor_max(qmax16[:, 0:1], qmax16[:, 0:1],
                         rmax16[:, 0:1]).then_inc(SEM["sv"], 1)
        for g in range(4):
            e.wait_ge(SEM["st"], st0 + 4 + g)
            for qb in range(NQB):
                s_ = K2[:, g * T + qb * QB:g * T + (qb + 1) * QB]
                mm = e.tensor_mul(q2s[:, qb * QB:(qb + 1) * QB], s_, s_)
            mm.then_inc(SEM["sv"], 1)
            e.wait_ge(SEM["st"], st0 + 5 + g)
            for qb in range(NQB):
                e.reduce_max(krow1[:, 0:1], ps_k[qb][0:1, :], axis=AXX)
                mm = e.tensor_max(kmax1[:, 0:1], kmax1[:, 0:1], krow1[:, 0:1])
                mm.then_inc(SEM["sv"], 1)
        e.wait_ge(SEM["st"], st0 + 9)
        e.reduce_max(qM[0:1, 0:1], ps_tq[0:1, 0:16], axis=AXX)
        e.tensor_mul(b2[0:1, 0:1], qM[0:1, 0:1],
                     kmax1[0:1, 0:1]).then_inc(SEM["sv"], 1)
        e.wait_ge(SEM["ss"], ss0 + 1)
        e.tensor_single_scalar(bval[0:1, 0:1], bnd[0:1, 0:1],
                               float(-1.001 / 8.0), ALU.mult)
        e.tensor_single_scalar(bval[0:1, 0:1], bval[0:1, 0:1],
                               float(-0.001 / 8.0),
                               ALU.add).then_inc(SEM["sv"], 1)
        e.wait_ge(SEM["st"], st0 + 10)
        e.tensor_copy(expbias[:, 0:1], ps_eb[:, 0:1])
    CNT["sv"] += 30

    def bnd_scalar(e):
        e.wait_ge(SEM["sv"], sv0 + 29)
        e.activation(out=bnd[0:1, 0:1], in_=b2[0:1, 0:1], func=AF.Ln,
                     scale=1.0, bias=zero1[0:1, 0:1])
        e.activation(out=bnd[0:1, 0:1], in_=bnd[0:1, 0:1], func=AF.Exp,
                     scale=0.5, bias=zero1[0:1, 0:1]).then_inc(SEM["ss"], 1)
    CNT["ss"] += 1

    phase("bound", tensor=bnd_tensor, vector=bnd_vector, scalar=bnd_scalar)
    ps5_ctx.close()

    ps6_ctx = ExitStack()
    ps_s = [ps6_ctx.enter_context(nc.psum_tensor(f"ps_s{h}", [128, QB2], F32))
            for h in range(2)]
    ps_o = [ps6_ctx.enter_context(nc.psum_tensor(f"ps_o{h}", [128, QB2], F32))
            for h in range(2)]
    st0 = CNT["st"]
    sv0 = CNT["sv"]
    ss0 = CNT["ss"]
    sg0 = CNT["sg"]
    NIT = 16

    def attn_tensor(e):
        DL = 3
        for it in range(NIT):
            pr, j = it // 2, it % 2
            g = pr // 2
            itss = ss0 + it * 32
            for kt in range(NKT + DL):
                if kt < NKT:
                    def smm(e, h, half):
                        return e.matmul(
                            ps_s[h][:, half * QB:(half + 1) * QB],
                            K2[h * 64:(h + 1) * 64,
                               g * T + kt * 128:g * T + (kt + 1) * 128],
                            Q16[h * 64:(h + 1) * 64,
                                pr * T + j * QB2 + half * QB:
                                pr * T + j * QB2 + (half + 1) * QB],
                            start=True, stop=True,
                            tile_position=(h * 64, 0))
                    if kt > 0:
                        e.wait_ge(SEM["ss"], itss + 2 * (kt - 1) + 1)
                    else:
                        e.wait_ge(SEM["ss"], itss - 1 if it > 0 else 0)
                    smm(e, 0, 0)
                    smm(e, 0, 1).then_inc(SEM["st"], 1)
                    if kt > 0:
                        e.wait_ge(SEM["ss"], itss + 2 * (kt - 1) + 2)
                    else:
                        e.wait_ge(SEM["ss"], itss)
                    smm(e, 1, 0)
                    smm(e, 1, 1).then_inc(SEM["st"], 1)
                    CNT["st"] += 2
                if kt >= DL:
                    kc = kt - DL
                    e.wait_ge(SEM["ss"], itss + 2 * kc + 2)
                    vb = (g * NKT + kc) * 128
                    for h in range(2):
                        if kc == 0 and it > 0:
                            e.wait_ge(SEM["sv"], sv0 + 3 * it - 2 + h)
                        pt = PT[:, (h * 4 + kc % 4) * QB2:
                                (h * 4 + kc % 4 + 1) * QB2]
                        for half in range(2):
                            mm = e.matmul(
                                ps_o[h][:, half * QB:(half + 1) * QB],
                                V2[:, vb:vb + 128],
                                pt[:, half * QB:(half + 1) * QB],
                                start=(kc == 0), stop=(kc == NKT - 1))
                    mm.then_inc(SEM["st"], 1)
                    CNT["st"] += 1

    def attn_scalar(e):
        DL = 3
        for it in range(NIT):
            itst = st0 + it * 48
            for kt in range(NKT):
                for h in range(2):
                    if kt < DL:
                        e.wait_ge(SEM["st"], itst + 2 * kt + h + 1)
                    else:
                        e.wait_ge(SEM["st"], itst + 3 * kt - DL + h + 1)
                    base = (h * 4 + kt % 4) * QB2
                    e.activation(out=PT[:, base:base + QB2],
                                 in_=ps_s[h][:, :], func=AF.Exp,
                                 scale=0.125,
                                 bias=expbias[:, 0:1]).then_inc(SEM["ss"], 1)
    CNT["ss"] += NIT * 32

    def emit_final(e, it):
        pr, j = it // 2, it % 2
        sl = slice(pr * T + j * QB2, pr * T + (j + 1) * QB2)
        par = (it % 2) * QB2
        e.wait_ge(SEM["sg"], sg0 + 32 * (it + 1))
        e.tensor_mul(tmp32a[0:64, :], rA[0:64, par:par + QB2],
                     rC[0:64, par:par + QB2])
        e.tensor_add(xs[0:64, sl], tmp32a[0:64, :], xs[0:64, sl])
        e.tensor_mul(tmp32a[64:128, :], rC[64:128, par:par + QB2],
                     rB[64:128, par:par + QB2])
        e.tensor_add(xs[64:128, sl], tmp32a[64:128, :], xs[64:128, sl])

    def attn_vector(e):
        for it in range(NIT):
            par = (it % 2) * QB2
            e.wait_ge(SEM["st"], st0 + it * 48 + 48)
            e.tensor_copy(rA[:, par:par + QB2],
                          ps_o[0][:, :]).then_inc(SEM["sv"], 1)
            e.tensor_copy(rB[:, par:par + QB2],
                          ps_o[1][:, :]).then_inc(SEM["sv"], 1)
            e.reciprocal(rA[64:128, par:par + QB2], rA[64:128, par:par + QB2])
            e.reciprocal(rB[64:128, par:par + QB2], rB[64:128, par:par + QB2])
            e.drain().then_inc(SEM["sv"], 1)
            if it > 0:
                emit_final(e, it - 1)
        emit_final(e, NIT - 1)
    CNT["sv"] += NIT * 3

    def attn_gpsimd(e):
        for it in range(NIT):
            par = (it % 2) * QB2
            e.wait_ge(SEM["sv"], sv0 + 3 * it + 3)
            dma(e, rC[0:64, par:par + QB2],
                rA[64:128, par:par + QB2], "sg")
            dma(e, rC[64:128, par:par + QB2],
                rB[0:64, par:par + QB2], "sg")

    phase("attn", tensor=attn_tensor, scalar=attn_scalar, vector=attn_vector,
          gpsimd=attn_gpsimd)
    ps6_ctx.close()
    evac_ctx.close()
    attn_ctx.close()

    post_ctx = ExitStack()
    out32 = post_ctx.enter_context(
        nc.sbuf_tensor("out32", [128, NCH * T], F32))
    xsq = post_ctx.enter_context(
        nc.sbuf_tensor("xsq", [128, NCH * T], BF16))

    def emit_ln(tag, src16, srcapply, dstget, store=False):
        lnctx = ExitStack()
        ps_a = lnctx.enter_context(nc.psum_tensor(f"ps_{tag}a", [1, QB], F32))
        ps_b = lnctx.enter_context(nc.psum_tensor(f"ps_{tag}b", [1, QB], F32))
        ps_i = lnctx.enter_context(
            nc.psum_tensor(f"ps_{tag}i", [128, QB], F32))
        ps_m = lnctx.enter_context(
            nc.psum_tensor(f"ps_{tag}m", [128, QB], F32))
        t0 = CNT["st"]
        v0 = CNT["sv"]
        s0 = CNT["ss"]

        def f_tensor(e):
            for qb in range(NQB):
                e.wait_ge(SEM["sv"], v0 + (1 if qb == 0 else 3 * qb - 1))
                for c in range(NCH):
                    e.matmul(ps_a[:, :], ones128_16[:, 0:1], src16(c, qb),
                             start=(c == 0), stop=(c == NCH - 1))
                for c in range(NCH):
                    mm = e.matmul(ps_b[:, :], ones128_16[:, 0:1],
                                  xsq[:, c * T + qb * QB:c * T + (qb + 1) * QB],
                                  start=(c == 0), stop=(c == NCH - 1))
                mm.then_inc(SEM["st"], 1)
                e.wait_ge(SEM["sv"], v0 + 3 * qb + 3)
                e.matmul(ps_i[:, :], ones1_16[0:1, :], r_inv[0:1, :],
                         start=True, stop=True)
                e.matmul(ps_m[:, :], ones1_16[0:1, :], r_nminv[0:1, :],
                         start=True, stop=True).then_inc(SEM["st"], 1)
        CNT["st"] += 2 * NQB

        def f_vector(e):
            for c in range(NCH):
                for qb in range(NQB):
                    s = src16(c, qb)
                    e.tensor_mul(
                        xsq[:, c * T + qb * QB:c * T + (qb + 1) * QB], s, s)
            e.drain().then_inc(SEM["sv"], 1)
            for qb in range(NQB):
                e.wait_ge(SEM["st"], t0 + 2 * qb + 1)
                e.tensor_single_scalar(s_mean[0:1, :], ps_a[0:1, :],
                                       float(1.0 / D), ALU.mult)
                e.tensor_mul(s_msq[0:1, :], s_mean[0:1, :], s_mean[0:1, :])
                e.tensor_single_scalar(s_var[0:1, :], ps_b[0:1, :],
                                       float(1.0 / D), ALU.mult)
                e.tensor_sub(s_var[0:1, :], s_var[0:1, :],
                             s_msq[0:1, :]).then_inc(SEM["sv"], 1)
                e.wait_ge(SEM["ss"], s0 + qb + 1)
                e.tensor_mul(s_minv[0:1, :], s_mean[0:1, :], s_inv[0:1, :])
                e.tensor_copy(r_inv[0:1, :], s_inv[0:1, :])
                e.tensor_single_scalar(
                    r_nminv[0:1, :], s_minv[0:1, :], -1.0,
                    ALU.mult).then_inc(SEM["sv"], 1)
                e.wait_ge(SEM["st"], t0 + 2 * qb + 2)
                last = None
                for c in range(NCH):
                    e.tensor_mul(tmp32a[:, 0:QB], srcapply(c, qb), ps_i[:, :])
                    last = e.tensor_add(dstget(c, qb), tmp32a[:, 0:QB],
                                        ps_m[:, :])
                last.then_inc(SEM["sv"], 1)
        CNT["sv"] += 1 + 3 * NQB

        def f_scalar(e):
            for qb in range(NQB):
                e.wait_ge(SEM["sv"], v0 + 3 * qb + 2)
                e.activation(out=s_inv[0:1, :], in_=s_var[0:1, :], func=AF.Ln,
                             scale=1.0, bias=eps1[0:1, 0:1])
                e.activation(out=s_inv[0:1, :], in_=s_inv[0:1, :],
                             func=AF.Exp, scale=-0.5,
                             bias=zero1[0:1, 0:1]).then_inc(SEM["ss"], 1)
        CNT["ss"] += NQB

        f_sync = None
        if store:
            def f_sync(e):
                for qb in range(NQB):
                    e.wait_ge(SEM["sv"], v0 + 3 * qb + 4)
                    for c in range(NCH):
                        dma(e, yT[c * 128:(c + 1) * 128,
                                  qb * QB:(qb + 1) * QB],
                            out32[:, c * T + qb * QB:c * T + (qb + 1) * QB],
                            "sd")
        phase(f"ln_{tag}", tensor=f_tensor, vector=f_vector, scalar=f_scalar,
              sync=f_sync)
        lnctx.close()

    lf_ctx = ExitStack()
    ps_a = lf_ctx.enter_context(nc.psum_tensor("ps_lfa", [1, QB], F32))
    ps_b = lf_ctx.enter_context(nc.psum_tensor("ps_lfb", [1, QB], F32))
    ps_i = lf_ctx.enter_context(nc.psum_tensor("ps_lfi", [128, QB], F32))
    ps_m = lf_ctx.enter_context(nc.psum_tensor("ps_lfm", [128, QB], F32))
    ps_i2 = lf_ctx.enter_context(nc.psum_tensor("ps_lfi2", [128, QB], F32))
    ps_m2 = lf_ctx.enter_context(nc.psum_tensor("ps_lfm2", [128, QB], F32))
    pp = [lf_ctx.enter_context(nc.psum_tensor(f"pp_lf{i}", [128, QB], F32))
          for i in range(2)]
    t0 = CNT["st"]
    v0 = CNT["sv"]
    s0 = CNT["ss"]

    TORD = (["st1_0", "st1_1", "bc1_0", "bc1_1"]
            + [f"ffn_0_{b}" for b in range(NCH)]
            + ["st1_2", "bc1_2"]
            + [f"ffn_1_{b}" for b in range(NCH)]
            + ["st1_3", "bc1_3", "st2_0", "bc2_0"]
            + [f"ffn_2_{b}" for b in range(NCH)]
            + ["st2_1", "bc2_1"]
            + [f"ffn_3_{b}" for b in range(NCH)]
            + ["st2_2", "bc2_2", "st2_3", "bc2_3"])
    VORD = ["sq1_0", "v1a_0", "v2a_0", "sq1_1", "ap1_0", "v1a_1", "v2a_1",
            "sq1_2", "ap1_1", "v1a_2", "v2a_2", "cast_0", "sq1_3", "ap1_2",
            "v1a_3", "v2a_3", "sq2_0", "v1b_0", "v2b_0", "cast_1", "ap1_3",
            "sq2_1", "v1b_1", "v2b_1", "ap2_0", "cast_2", "sq2_2", "v1b_2",
            "v2b_2", "ap2_1", "cast_3", "sq2_3", "v1b_3", "v2b_3", "ap2_2",
            "ap2_3"]
    SORD = (["rs1_0", "rs1_1"] + [f"sf_0_{b}" for b in range(NCH)]
            + ["rs1_2"] + [f"sf_1_{b}" for b in range(NCH)]
            + ["rs1_3", "rs2_0"] + [f"sf_2_{b}" for b in range(NCH)]
            + ["rs2_1"] + [f"sf_3_{b}" for b in range(NCH)]
            + ["rs2_2", "rs2_3"])
    TP = {nm: t0 + i + 1 for i, nm in enumerate(TORD)}
    VP = {nm: v0 + i + 1 for i, nm in enumerate(VORD)}
    SP_ = {nm: s0 + i + 1 for i, nm in enumerate(SORD)}
    AB_READER = {"st1_1": "v1a_0", "st1_2": "v1a_1", "st1_3": "v1a_2",
                 "st2_0": "v1a_3", "st2_1": "v1b_0", "st2_2": "v1b_1",
                 "st2_3": "v1b_2"}

    def xsl(c, qb):
        return xs[:, c * T + qb * QB:c * T + (qb + 1) * QB]

    def osl(c, qb):
        return out32[:, c * T + qb * QB:c * T + (qb + 1) * QB]

    def lf_tensor(e):
        ffjobs = []
        for nm in TORD:
            kind, q = nm.rsplit("_", 1)[0].split("_")[0], nm.split("_")[1]
            q = int(q)
            if kind in ("st1", "st2"):
                e.wait_ge(SEM["sv"], VP[("sq1" if kind == "st1" else "sq2")
                                        + f"_{q}"])
                if nm in AB_READER:
                    e.wait_ge(SEM["sv"], VP[AB_READER[nm]])
                for c in range(NCH):
                    e.matmul(ps_a[:, :], ones128_16[:, 0:1], xsl(c, q),
                             start=(c == 0), stop=(c == NCH - 1))
                for c in range(NCH):
                    mm = e.matmul(ps_b[:, :], ones128_16[:, 0:1],
                                  xsq[:, c * T + q * QB:c * T + (q + 1) * QB],
                                  start=(c == 0), stop=(c == NCH - 1))
                mm.then_inc(SEM["st"], 1)
            elif kind in ("bc1", "bc2"):
                e.wait_ge(SEM["sv"], VP[("v2a" if kind == "bc1" else "v2b")
                                        + f"_{q}"])
                if q > 0:
                    e.wait_ge(SEM["sv"], VP[("ap1" if kind == "bc1"
                                             else "ap2") + f"_{q - 1}"])
                pi, pm = (ps_i, ps_m) if kind == "bc1" else (ps_i2, ps_m2)
                ri, rn = ((r_inv, r_nminv) if kind == "bc1"
                          else (r_inv2, r_nminv2))
                e.matmul(pi[:, :], ones1_16[0:1, :], ri[0:1, :],
                         start=True, stop=True)
                e.matmul(pm[:, :], ones1_16[0:1, :], rn[0:1, :],
                         start=True, stop=True).then_inc(SEM["st"], 1)
            else:
                b = int(nm.split("_")[2])
                j = len(ffjobs)
                ffjobs.append(nm)
                if b == 0:
                    e.wait_ge(SEM["sv"], VP[f"ap1_{q}"])
                if j >= 2:
                    e.wait_ge(SEM["ss"], SP_[ffjobs[j - 2].replace(
                        "ffn", "sf")])
                for a in range(NCH):
                    mm = e.matmul(
                        pp[j % 2][:, :],
                        M16[:, a * D + b * 128:a * D + (b + 1) * 128],
                        xsl(a, q),
                        start=(a == 0), stop=(a == NCH - 1))
                mm.then_inc(SEM["st"], 1)
    CNT["st"] += len(TORD)

    def lf_vector(e):
        for nm in VORD:
            kind, q = nm.rsplit("_", 1)
            q = int(q)
            if kind in ("sq1", "sq2"):
                for c in range(NCH):
                    s = xsl(c, q)
                    mm = e.tensor_mul(
                        xsq[:, c * T + q * QB:c * T + (q + 1) * QB], s, s)
                mm.then_inc(SEM["sv"], 1)
            elif kind == "cast":
                e.wait_ge(SEM["ss"], SP_[f"sf_{q}_7"])
                for c in range(NCH):
                    mm = e.tensor_copy(xsl(c, q), osl(c, q))
                mm.then_inc(SEM["sv"], 1)
            elif kind in ("v1a", "v1b"):
                e.wait_ge(SEM["st"], TP[("st1" if kind == "v1a" else "st2")
                                        + f"_{q}"])
                e.tensor_single_scalar(s_mean[0:1, :], ps_a[0:1, :],
                                       float(1.0 / D), ALU.mult)
                e.tensor_mul(s_msq[0:1, :], s_mean[0:1, :], s_mean[0:1, :])
                e.tensor_single_scalar(s_var[0:1, :], ps_b[0:1, :],
                                       float(1.0 / D), ALU.mult)
                e.tensor_sub(s_var[0:1, :], s_var[0:1, :],
                             s_msq[0:1, :]).then_inc(SEM["sv"], 1)
            elif kind in ("v2a", "v2b"):
                e.wait_ge(SEM["ss"], SP_[("rs1" if kind == "v2a" else "rs2")
                                         + f"_{q}"])
                ri, rn = ((r_inv, r_nminv) if kind == "v2a"
                          else (r_inv2, r_nminv2))
                e.tensor_mul(s_minv[0:1, :], s_mean[0:1, :], s_inv[0:1, :])
                e.tensor_copy(ri[0:1, :], s_inv[0:1, :])
                e.tensor_single_scalar(
                    rn[0:1, :], s_minv[0:1, :], -1.0,
                    ALU.mult).then_inc(SEM["sv"], 1)
            elif kind == "ap1":
                e.wait_ge(SEM["st"], TP[f"bc1_{q}"])
                for c in range(NCH):
                    e.tensor_mul(tmp32a[:, 0:QB], xsl(c, q), ps_i[:, :])
                    last = e.tensor_add(xsl(c, q), tmp32a[:, 0:QB],
                                        ps_m[:, :])
                last.then_inc(SEM["sv"], 1)
            elif kind == "ap2":
                e.wait_ge(SEM["st"], TP[f"bc2_{q}"])
                for c in range(NCH):
                    e.tensor_mul(tmp32a[:, 0:QB], osl(c, q), ps_i2[:, :])
                    last = e.tensor_add(osl(c, q), tmp32a[:, 0:QB],
                                        ps_m2[:, :])
                last.then_inc(SEM["sv"], 1)
    CNT["sv"] += len(VORD)

    def lf_scalar(e):
        for nm in SORD:
            kind, q = nm.rsplit("_", 1)
            if kind.startswith("sf"):
                q, b = int(nm.split("_")[1]), int(nm.split("_")[2])
                e.wait_ge(SEM["st"], TP[nm.replace("sf", "ffn")])
                j = TORD.index(nm.replace("sf", "ffn"))
                jj = [x for x in TORD if x.startswith("ffn")].index(
                    nm.replace("sf", "ffn"))
                e.activation(out=osl(b, q), in_=pp[jj % 2][:, :],
                             func=AF.Identity, scale=1.0,
                             bias=bf_sb[:, b:b + 1]).then_inc(SEM["ss"], 1)
            else:
                q = int(q)
                e.wait_ge(SEM["sv"], VP[("v1a" if kind == "rs1" else "v1b")
                                        + f"_{q}"])
                e.activation(out=s_inv[0:1, :], in_=s_var[0:1, :],
                             func=AF.Ln, scale=1.0, bias=eps1[0:1, 0:1])
                e.activation(out=s_inv[0:1, :], in_=s_inv[0:1, :],
                             func=AF.Exp, scale=-0.5,
                             bias=zero1[0:1, 0:1]).then_inc(SEM["ss"], 1)
    CNT["ss"] += len(SORD)

    def lf_sync(e):
        for q in range(NQB):
            e.wait_ge(SEM["sv"], VP[f"ap2_{q}"])
            for c in range(NCH):
                dma(e, yT[c * 128:(c + 1) * 128, q * QB:(q + 1) * QB],
                    out32[:, c * T + q * QB:c * T + (q + 1) * QB], "sd")

    phase("post", tensor=lf_tensor, vector=lf_vector, scalar=lf_scalar,
          sync=lf_sync)
    lf_ctx.close()

    if dbg:
        p = SP()

        def f_dbg(e):
            if "M16" in dbg:
                dma(e, dbg["M16"][:, :], M16[:, :], "sg")
            if "xs" in dbg:
                dma(e, dbg["xs"][:, :], xs[:, :], "sg")
            if "expbias" in dbg:
                dma(e, dbg["expbias"][:, :], expbias[:, :], "sg")
        p.step("gpsimd", f_dbg)
        p.emit("dbg")

    p = SP()
    p.step("sync", lambda e: e.wait_ge(SEM["sd"], CNT["sd"]))
    p.emit("fin")

    return nc


_NC_CACHE = {}


def kernel(L0_attn_h, L0_attn_wq_w, L0_attn_wq_b, L0_attn_wk_w, L0_attn_wk_b,
           L0_attn_wv_w, L0_attn_wv_b, L0_ffn_w_freq_real, L0_ffn_w_freq_imag,
           L0_ffn_bias):
    h = np.asarray(L0_attn_h, dtype=np.float32)
    wq = np.asarray(L0_attn_wq_w, dtype=np.float32)
    wk = np.asarray(L0_attn_wk_w, dtype=np.float32)
    wv = np.asarray(L0_attn_wv_w, dtype=np.float32)
    wfr = np.asarray(L0_ffn_w_freq_real, dtype=np.float32)
    wfi = np.asarray(L0_ffn_w_freq_imag, dtype=np.float32)
    bq = np.asarray(L0_attn_wq_b, dtype=np.float32)
    bk = np.asarray(L0_attn_wk_b, dtype=np.float32)
    bv = np.asarray(L0_attn_wv_b, dtype=np.float32)
    bf = np.asarray(L0_ffn_bias, dtype=np.float32)

    wqT = np.ascontiguousarray(wq.T)
    wkT = np.ascontiguousarray(wk.T)
    wvT = np.ascontiguousarray(wv.T)
    wfrT = np.ascontiguousarray(wfr.T)
    wfiT = np.ascontiguousarray(wfi.T)
    bq2 = np.ascontiguousarray(bq.reshape(8, 128).T)
    bk2 = np.ascontiguousarray(np.tile(bk.reshape(4, 64), (1, 2)).T)
    bv2 = np.ascontiguousarray(bv.reshape(2, 128).T)
    bf2 = np.ascontiguousarray(bf.reshape(8, 128).T)

    if "nc" not in _NC_CACHE:
        _NC_CACHE["nc"] = build_program()
    nc = _NC_CACHE["nc"]

    in_maps = []
    for b in range(8):
        in_maps.append({
            "hT": np.ascontiguousarray(h[b].T),
            "wqT": wqT, "wkT": wkT, "wvT": wvT,
            "wfrT": wfrT, "wfiT": wfiT,
            "bq_in": bq2, "bk_in": bk2, "bv_in": bv2, "bf_in": bf2,
        })
    res = run_bass_kernel_spmd(nc, in_maps, core_ids=list(range(8)))
    out = np.stack([np.asarray(r["yT"]).T for r in res.results], axis=0)
    return np.ascontiguousarray(out.astype(np.float32))


# version 17
# speedup vs baseline: 1.0379x; 1.0202x over previous
from contextlib import ExitStack

import numpy as np
import ml_dtypes
import concourse.bass as bass
import concourse.mybir as mybir
from concourse.bass_utils import run_bass_kernel_spmd

F32 = mybir.dt.float32
F16 = mybir.dt.float16
BF16 = mybir.dt.bfloat16
I32 = mybir.dt.int32
AF = mybir.ActivationFunctionType
ALU = mybir.AluOpType
AXX = mybir.AxisListType.X

D = 1024
T = 2048
NCH = 8
NKT = 16
QB = 512
NQB = T // QB
QB2 = 1024
EPS = 1e-5


def _load_act_table(nc, e, set_name):
    from concourse.hw_specs import get_activation_tables
    for idx, nm in enumerate(get_activation_tables(nc.m.arch)):
        if nm == set_name:
            return e.add_instruction(
                mybir.InstLoadActFuncSet(
                    name=nc.get_next_instruction_name(),
                    act_func_set_id=idx, ins=[], outs=[]))
    raise ValueError(set_name)


def build_program(debug=()):
    nc = bass.Bass("TRN2", target_bir_lowering=False,
                   detect_race_conditions=False)

    hT = nc.dram_tensor("hT", [D, T], F32, kind="ExternalInput").ap()
    wqT = nc.dram_tensor("wqT", [D, D], F32, kind="ExternalInput").ap()
    wkT = nc.dram_tensor("wkT", [D, 256], F32, kind="ExternalInput").ap()
    wvT = nc.dram_tensor("wvT", [D, 256], F32, kind="ExternalInput").ap()
    wfrT = nc.dram_tensor("wfrT", [D, D], F32, kind="ExternalInput").ap()
    wfiT = nc.dram_tensor("wfiT", [D, D], F32, kind="ExternalInput").ap()
    bq_in = nc.dram_tensor("bq_in", [128, 8], F32, kind="ExternalInput").ap()
    bk_in = nc.dram_tensor("bk_in", [128, 4], F32, kind="ExternalInput").ap()
    bv_in = nc.dram_tensor("bv_in", [128, 2], F32, kind="ExternalInput").ap()
    bf_in = nc.dram_tensor("bf_in", [128, 8], F32, kind="ExternalInput").ap()
    yT = nc.dram_tensor("yT", [D, T], F32, kind="ExternalOutput").ap()

    dbg = {}
    for name, shape, dt in debug:
        dbg[name] = nc.dram_tensor("dbg_" + name, shape, dt,
                                   kind="ExternalOutput").ap()

    iota_np = np.arange(D, dtype=np.float32)[None, :]
    ident32_np = np.eye(128, dtype=np.float32)
    identB_np = np.eye(128, dtype=np.float32).astype(ml_dtypes.bfloat16)
    selq_np = np.zeros((128, 128), dtype=np.float32)
    for c in range(8):
        selq_np[0:64, c * 16 + 2 * c] = 1.0
        selq_np[64:128, c * 16 + 2 * c + 1] = 1.0
    selq_np = selq_np.astype(np.float16)
    onestop_np = np.zeros((128, 1), dtype=np.float32)
    onestop_np[0:64, 0] = 1.0
    onestop_np = onestop_np.astype(np.float16)

    iota_d = nc.inline_tensor(iota_np, name="iota_c").ap()
    ident32_d = nc.inline_tensor(ident32_np, name="ident32_c").ap()
    identB_d = nc.inline_tensor(identB_np, name="identB_c").ap()
    selq_d = nc.inline_tensor(selq_np, name="selq_c").ap()
    onestop_d = nc.inline_tensor(onestop_np, name="onestop_c").ap()

    ctx = ExitStack()
    SEM = {}
    for s in ("sd", "sg", "st", "sv", "ss", "ph", "tok"):
        SEM[s] = ctx.enter_context(nc.semaphore(s))
    CNT = {k: 0 for k in SEM}
    PHN = [0]
    TOK = [0]

    def dma(e, out, in_, sem):
        e.dma_start(out=out, in_=in_).then_inc(SEM[sem], 16)
        CNT[sem] += 16

    def phase(name, sync=None, tensor=None, vector=None, scalar=None,
              gpsimd=None):
        k = PHN[0]

        def wrap(fn):
            def inner(e):
                e.wait_ge(SEM["ph"], 5 * k)
                if fn is not None:
                    fn(e)
                e.drain().then_inc(SEM["ph"], 1)
            return inner
        with nc.named_scope(name), nc.Block(name) as blk:
            blk.sync(wrap(sync))
            blk.tensor(wrap(tensor))
            blk.vector(wrap(vector))
            blk.scalar(wrap(scalar))
            blk.gpsimd(wrap(gpsimd))
        PHN[0] += 1

    class SP:

        def __init__(self):
            self.steps = []

        def step(self, eng, fn):
            self.steps.append((eng, fn))

        def emit(self, name):
            base = TOK[0]
            steps = list(self.steps)

            def mk(engname):
                def fn(e):
                    for i, (en, f) in enumerate(steps):
                        if en == engname:
                            e.wait_ge(SEM["tok"], base + i)
                            f(e)
                            e.drain().then_inc(SEM["tok"], 1)
                return fn
            phase(name, sync=mk("sync"), tensor=mk("tensor"),
                  vector=mk("vector"), scalar=mk("scalar"),
                  gpsimd=mk("gpsimd"))
            TOK[0] += len(steps)

    def pipe_phase(name, jobs, gpsimd=None, scalar=None, pre_tensor=None,
                   pre_vector=None, ps_shape=(128, QB), ps_dtype=F32):
        t0 = CNT["st"]
        v0 = CNT["sv"]
        pvi = 1 if pre_vector is not None else 0
        pp_ctx = ExitStack()
        pss = [pp_ctx.enter_context(
            nc.psum_tensor(f"pp_{name}{i}", list(ps_shape), ps_dtype))
            for i in range(2)]

        def ft(e):
            if pre_tensor is not None:
                pre_tensor(e)
            for n, (mm_fn, _) in enumerate(jobs):
                if n >= 2:
                    e.wait_ge(SEM["sv"], v0 + pvi + n - 1)
                elif pvi:
                    e.wait_ge(SEM["sv"], v0 + 1)
                mm_fn(e, pss[n % 2]).then_inc(SEM["st"], 1)

        def fv(e):
            if pre_vector is not None:
                pre_vector(e)
                e.drain().then_inc(SEM["sv"], 1)
            for n, (_, ev_fn) in enumerate(jobs):
                e.wait_ge(SEM["st"], t0 + n + 1)
                ev_fn(e, pss[n % 2]).then_inc(SEM["sv"], 1)

        CNT["st"] += len(jobs)
        CNT["sv"] += len(jobs) + pvi
        phase(name, tensor=ft, vector=fv, gpsimd=gpsimd, scalar=scalar)
        pp_ctx.close()

    xs = nc.alloc_sbuf_tensor("xs", [128, NCH * T], F16)
    M16 = nc.alloc_sbuf_tensor("M16", [128, NCH * D], F16)
    iota_sb = nc.alloc_sbuf_tensor("iota_sb", [1, D], F32)
    ident32 = nc.alloc_sbuf_tensor("ident32", [128, 128], F32)
    identB = nc.alloc_sbuf_tensor("identB", [128, 128], BF16)
    selq = nc.alloc_sbuf_tensor("selq", [128, 128], F16)
    onestop = nc.alloc_sbuf_tensor("onestop", [128, 1], F16)
    bq_sb = nc.alloc_sbuf_tensor("bq_sb", [128, 8], F32)
    bk_sb = nc.alloc_sbuf_tensor("bk_sb", [128, 4], F32)
    bv_sb = nc.alloc_sbuf_tensor("bv_sb", [128, 2], F32)
    bf_sb = nc.alloc_sbuf_tensor("bf_sb", [128, 8], F32)
    expbias = nc.alloc_sbuf_tensor("expbias", [128, 1], F32)
    negpi = nc.alloc_sbuf_tensor("negpi", [128, 1], F32)
    eps1 = nc.alloc_sbuf_tensor("eps1", [1, 1], F32)
    zero1 = nc.alloc_sbuf_tensor("zero1", [1, 1], F32)
    ones1_32 = nc.alloc_sbuf_tensor("ones1_32", [1, 128], F32)
    ones1_16 = nc.alloc_sbuf_tensor("ones1_16", [1, 128], F16)
    ones128_16 = nc.alloc_sbuf_tensor("ones128_16", [128, 1], F16)
    tmp32a = nc.alloc_sbuf_tensor("tmp32a", [128, QB2], F32)
    q2s = nc.alloc_sbuf_tensor("q2s", [128, NCH * QB], BF16)
    s_mean = nc.alloc_sbuf_tensor("s_mean", [1, QB], F32)
    s_var = nc.alloc_sbuf_tensor("s_var", [1, QB], F32)
    s_msq = tmp32a[0:1, 0:QB]
    s_minv = tmp32a[0:1, QB:2 * QB]
    s_inv = nc.alloc_sbuf_tensor("s_inv", [1, QB], F32)
    r_inv = nc.alloc_sbuf_tensor("r_inv", [1, QB], F16)
    r_nminv = nc.alloc_sbuf_tensor("r_nminv", [1, QB], F16)
    r_inv2 = nc.alloc_sbuf_tensor("r_inv2", [1, QB], F16)
    r_nminv2 = nc.alloc_sbuf_tensor("r_nminv2", [1, QB], F16)
    qmax16 = nc.alloc_sbuf_tensor("qmax16", [16, 1], F32)
    rmax16 = nc.alloc_sbuf_tensor("rmax16", [16, 1], F32)
    kmax1 = nc.alloc_sbuf_tensor("kmax1", [1, 1], F32)
    krow1 = nc.alloc_sbuf_tensor("krow1", [1, 1], F32)
    qM = nc.alloc_sbuf_tensor("qM", [1, 1], F32)
    b2 = nc.alloc_sbuf_tensor("b2", [1, 1], F32)
    bnd = nc.alloc_sbuf_tensor("bnd", [1, 1], F32)
    bval = nc.alloc_sbuf_tensor("bval", [1, 1], F32)

    pre_ctx = ExitStack()
    C16 = pre_ctx.enter_context(nc.sbuf_tensor("C16", [128, NCH * D], F16))
    S16 = pre_ctx.enter_context(nc.sbuf_tensor("S16", [128, NCH * D], F16))
    Sn16 = pre_ctx.enter_context(nc.sbuf_tensor("Sn16", [128, NCH * D], F16))
    A_ctx = ExitStack()
    A16r = A_ctx.enter_context(nc.sbuf_tensor("A16r", [128, NCH * D], F16))
    A16i = A_ctx.enter_context(nc.sbuf_tensor("A16i", [128, NCH * D], F16))
    w_ctx = ExitStack()
    wr16 = w_ctx.enter_context(nc.sbuf_tensor("wr16", [128, NCH * D], F16))
    wi16 = w_ctx.enter_context(nc.sbuf_tensor("wi16", [128, NCH * D], F16))
    tw_ctx = ExitStack()
    ik32 = tw_ctx.enter_context(nc.sbuf_tensor("ik32", [128, D], I32))
    yci = tw_ctx.enter_context(nc.sbuf_tensor("yci", [128, D], I32))
    ycf = tw_ctx.enter_context(nc.sbuf_tensor("ycf", [128, D], F32))
    ysf = tw_ctx.enter_context(nc.sbuf_tensor("ysf", [128, D], F32))
    ps_tw_ctx = ExitStack()
    ps_dk = [ps_tw_ctx.enter_context(
        nc.psum_tensor(f"ps_dk{hh}", [128, QB], F32)) for hh in range(4)]

    sd0 = CNT["sd"]
    st0 = CNT["st"]
    sv0 = CNT["sv"]
    ss0 = CNT["ss"]

    def init_sync(e):
        dma(e, iota_sb[:, :], iota_d[:, :], "sd")
        dma(e, ident32[:, :], ident32_d[:, :], "sd")
        dma(e, identB[:, :], identB_d[:, :], "sd")
        dma(e, selq[:, :], selq_d[:, :], "sd")
        dma(e, onestop[:, :], onestop_d[:, :], "sd")
        dma(e, bq_sb[:, :], bq_in[:, :], "sd")
        dma(e, bk_sb[:, :], bk_in[:, :], "sd")
        dma(e, bv_sb[:, :], bv_in[:, :], "sd")
        dma(e, bf_sb[:, :], bf_in[:, :], "sd")

    def init_gpsimd(e):
        for c in range(NCH):
            dma(e, xs[:, c * T:(c + 1) * T], hT[c * 128:(c + 1) * 128, :],
                "sg")
        for c in range(NCH):
            dma(e, wr16[:, c * D:(c + 1) * D],
                wfrT[c * 128:(c + 1) * 128, :], "sg")
            dma(e, wi16[:, c * D:(c + 1) * D],
                wfiT[c * 128:(c + 1) * 128, :], "sg")

    def init_tensor(e):
        e.wait_ge(SEM["sd"], sd0 + 16)
        for c in range(NCH):
            if c >= 2:
                e.wait_ge(SEM["sv"], sv0 + c)
            for half in range(2):
                mm = e.matmul(ps_dk[(c % 2) * 2 + half][:, :],
                              iota_sb[0:1, c * 128:(c + 1) * 128],
                              iota_sb[0:1, half * QB:(half + 1) * QB],
                              start=True, stop=True)
            mm.then_inc(SEM["st"], 1)
    CNT["st"] += NCH

    def init_vector(e):
        e.memset(negpi[:, :], float(-np.pi))
        e.memset(eps1[:, :], float(EPS))
        e.memset(zero1[:, :], 0.0)
        e.memset(ones1_32[:, :], 1.0)
        e.memset(ones1_16[:, :], 1.0)
        e.memset(ones128_16[:, :], 1.0)
        e.memset(qmax16[:, :], 0.0)
        e.memset(kmax1[:, :], 0.0)
        e.drain().then_inc(SEM["sv"], 1)
        for c in range(NCH):
            e.wait_ge(SEM["st"], st0 + c + 1)
            if c >= 1:
                e.wait_ge(SEM["ss"], ss0 + c)
            e.tensor_copy(ik32[:, 0:QB], ps_dk[(c % 2) * 2][:, :])
            e.tensor_copy(ik32[:, QB:D], ps_dk[(c % 2) * 2 + 1][:, :])
            e.tensor_single_scalar(yci[:, :], ik32[:, :], 768, ALU.add)
            e.tensor_single_scalar(yci[:, :], yci[:, :], 1023, ALU.bitwise_and)
            e.tensor_copy(ycf[:, :], yci[:, :])
            e.tensor_single_scalar(yci[:, :], ik32[:, :], 512, ALU.add)
            e.tensor_single_scalar(yci[:, :], yci[:, :], 1023, ALU.bitwise_and)
            e.tensor_copy(ysf[:, :], yci[:, :])
            e.drain().then_inc(SEM["sv"], 1)
        e.wait_ge(SEM["ss"], ss0 + NCH)
        e.tensor_single_scalar(Sn16[:, :], S16[:, :], -1.0, ALU.mult)
    CNT["sv"] += NCH + 1

    def init_scalar(e):
        _load_act_table(nc, e, "trig_and_small")
        sc = float(2 * np.pi / D)
        for c in range(NCH):
            e.wait_ge(SEM["sv"], sv0 + c + 2)
            e.activation(out=C16[:, c * D:(c + 1) * D], in_=ycf[:, :],
                         func=AF.Sin, scale=sc, bias=negpi[:, 0:1])
            e.activation(out=S16[:, c * D:(c + 1) * D], in_=ysf[:, :],
                         func=AF.Sin, scale=sc,
                         bias=negpi[:, 0:1]).then_inc(SEM["ss"], 1)
        _load_act_table(nc, e, "natural_log_exp_and_others")
    CNT["ss"] += NCH

    phase("init", sync=init_sync, gpsimd=init_gpsimd, tensor=init_tensor,
          vector=init_vector, scalar=init_scalar)
    sd_base = CNT["sd"]
    sg_h0 = CNT["sg"]
    sg_h = sg_h0 - 16 * 16
    sg_w = CNT["sg"]
    ps_tw_ctx.close()
    tw_ctx.close()

    jobs = []
    for dst in range(2):
        for dchunk in range(NCH):
            for half in range(2):
                def mm_fn(e, ps, dst=dst, dchunk=dchunk, half=half,
                          first=(dst == 0 and dchunk == 0 and half == 0)):
                    if first:
                        e.wait_ge(SEM["sg"], sg_w)
                    t1 = C16 if dst == 0 else S16
                    t2 = Sn16 if dst == 0 else C16
                    for o in range(NCH):
                        e.matmul(ps[:, :],
                                 wr16[:, o * D + dchunk * 128:
                                      o * D + (dchunk + 1) * 128],
                                 t1[:, o * D + half * QB:
                                    o * D + (half + 1) * QB],
                                 start=(o == 0), stop=False)
                    for o in range(NCH):
                        mm = e.matmul(ps[:, :],
                                      wi16[:, o * D + dchunk * 128:
                                           o * D + (dchunk + 1) * 128],
                                      t2[:, o * D + half * QB:
                                         o * D + (half + 1) * QB],
                                      start=False, stop=(o == NCH - 1))
                    return mm

                def ev_fn(e, ps, dst=dst, dchunk=dchunk, half=half):
                    A = A16r if dst == 0 else A16i
                    return e.tensor_single_scalar(
                        A[:, dchunk * D + half * QB:
                          dchunk * D + (half + 1) * QB],
                        ps[:, :], float(1.0 / D), ALU.mult)
                jobs.append((mm_fn, ev_fn))
    pipe_phase("pre_A", jobs)
    w_ctx.close()

    jobs = []
    for achunk in range(NCH):
        for half in range(2):
            def mm_fn(e, ps, achunk=achunk, half=half):
                for dd in range(NCH):
                    e.matmul(ps[:, :],
                             C16[:, dd * D + achunk * 128:
                                 dd * D + (achunk + 1) * 128],
                             A16r[:, dd * D + half * QB:
                                  dd * D + (half + 1) * QB],
                             start=(dd == 0), stop=False)
                for dd in range(NCH):
                    mm = e.matmul(ps[:, :],
                                  S16[:, dd * D + achunk * 128:
                                      dd * D + (achunk + 1) * 128],
                                  A16i[:, dd * D + half * QB:
                                       dd * D + (half + 1) * QB],
                                  start=False, stop=(dd == NCH - 1))
                return mm

            def ev_fn(e, ps, achunk=achunk, half=half):
                last = e.tensor_copy(
                    M16[:, achunk * D + half * QB:
                        achunk * D + (half + 1) * QB], ps[:, :])
                if achunk // 4 == half:
                    off = achunk * D + achunk * 128
                    last = e.tensor_add(M16[:, off:off + 128],
                                        M16[:, off:off + 128], ident32[:, :])
                return last
            jobs.append((mm_fn, ev_fn))
    pipe_phase("pre_M", jobs)
    A_ctx.close()
    pre_ctx.close()

    attn_ctx = ExitStack()
    Q16 = attn_ctx.enter_context(nc.sbuf_tensor("Q16", [128, NCH * T], F16))
    K2 = attn_ctx.enter_context(nc.sbuf_tensor("K2", [128, 4 * T], F16))
    V2 = attn_ctx.enter_context(
        nc.sbuf_tensor("V2", [128, 4 * NKT * 128], BF16))
    PT = attn_ctx.enter_context(nc.sbuf_tensor("PT", [128, 8 * QB2], BF16))
    pw_ctx = ExitStack()
    wq16 = pw_ctx.enter_context(nc.sbuf_tensor("wq16", [128, NCH * D], F16))
    wk1 = pw_ctx.enter_context(nc.sbuf_tensor("wk1", [128, NCH * 256], F16))
    wk2 = pw_ctx.enter_context(nc.sbuf_tensor("wk2", [128, NCH * 512], F16))
    wv16 = pw_ctx.enter_context(nc.sbuf_tensor("wv16", [128, NCH * 256], BF16))
    V1 = pw_ctx.enter_context(nc.sbuf_tensor("V1", [128, 2 * T], BF16))

    ps5_ctx = ExitStack()
    ps_n = ps5_ctx.enter_context(nc.psum_tensor("ps_n", [16, QB], F32))
    ps_k2 = [ps5_ctx.enter_context(nc.psum_tensor(f"ps_k{i}", [1, QB], F32))
             for i in range(2)]
    ps_tqeb = ps5_ctx.enter_context(nc.psum_tensor("ps_tqeb", [128, 32], F32))
    sg_wl0 = CNT["sg"]
    t0p = CNT["st"]
    v0p = CNT["sv"]
    s0p = CNT["ss"]

    def f_wl(e):
        for c in range(NCH):
            dma(e, wk1[:, c * 256:(c + 1) * 256],
                wkT[c * 128:(c + 1) * 128, :], "sg")
        for c in range(NCH):
            dma(e, wq16[:, c * D:(c + 1) * D],
                wqT[c * 128:(c + 1) * 128, :], "sg")
        for c in range(NCH):
            dma(e, wv16[:, c * 256:(c + 1) * 256],
                wvT[c * 128:(c + 1) * 128, :], "sg")

    def emit_dup(e, g):
        for f in range(NCH):
            src = wk1[:, f * 256 + g * 64:f * 256 + (g + 1) * 64]
            e.tensor_copy(
                wk2[:, f * 512 + g * 128:f * 512 + g * 128 + 64], src)
            last = e.tensor_copy(
                wk2[:, f * 512 + g * 128 + 64:f * 512 + (g + 1) * 128], src)
        return last

    def f_dup(e):
        e.wait_ge(SEM["sg"], sg_wl0 + 16 * 8)
        emit_dup(e, 0)

    jspec = ([("k", g, qb) for g in range(4) for qb in range(NQB)]
             + [("q", c, qb) for qb in range(NQB) for c in range(NCH)]
             + [("v", vc, qb) for vc in range(2) for qb in range(NQB)])

    jobs = []
    for n, (kind, c, qb) in enumerate(jspec):
        def mm_fn(e, ps, n=n, kind=kind, c=c, qb=qb):
            if n == 0:
                e.wait_ge(SEM["sg"], sg_h)
            if n == 16:
                e.wait_ge(SEM["sg"], sg_wl0 + 16 * 16)
            if n == 48:
                e.wait_ge(SEM["sg"], sg_wl0 + 16 * 24)
            for g in range(4):
                qb2 = n - (4 * g + 5)
                if 0 <= qb2 <= 3:
                    sl = (g % 2) * 4 + qb2
                    e.matmul(ps_k2[qb2 % 2][:, :], onestop[:, 0:1],
                             q2s[:, sl * QB:(sl + 1) * QB],
                             start=True, stop=True)
            if n >= 25 and (n - 25) % 8 == 0 and (n - 25) // 8 <= 3:
                qb3 = (n - 25) // 8
                e.wait_ge(SEM["sv"], v0p + 2 + 24 + 8 * qb3)
                for c2 in range(NCH):
                    e.matmul(ps_n[:, :], selq[:, c2 * 16:(c2 + 1) * 16],
                             q2s[:, c2 * QB:(c2 + 1) * QB],
                             start=(c2 == 0), stop=(c2 == NCH - 1))
            if n == 51:
                e.transpose(ps_tqeb[0:1, 0:16], qmax16[:, 0:1],
                            ident32[0:16, 0:16])
            if n == 53:
                e.wait_ge(SEM["sv"], v0p + 2 + 52)
                e.matmul(ps_tqeb[:, 16:17], ones1_32[0:1, :], bval[0:1, 0:1],
                         start=True, stop=True)
            for f in range(NCH):
                if kind == "q":
                    statn = wq16[:, f * D + c * 128:f * D + (c + 1) * 128]
                elif kind == "k":
                    statn = wk2[:, f * 512 + c * 128:
                                f * 512 + (c + 1) * 128]
                else:
                    statn = wv16[:, f * 256 + c * 128:
                                 f * 256 + (c + 1) * 128]
                mm = e.matmul(ps[:, :], statn,
                              xs[:, f * T + qb * QB:f * T + (qb + 1) * QB],
                              start=(f == 0), stop=(f == NCH - 1))
            return mm

        def ev_fn(e, ps, n=n, kind=kind, c=c, qb=qb):
            if kind == "q":
                dst = Q16[:, c * T + qb * QB:c * T + (qb + 1) * QB]
                bias = bq_sb[:, c:c + 1]
            elif kind == "k":
                dst = K2[:, c * T + qb * QB:c * T + (qb + 1) * QB]
                bias = bk_sb[:, c:c + 1]
            else:
                dst = V1[:, c * T + qb * QB:c * T + (qb + 1) * QB]
                bias = bv_sb[:, c:c + 1]
            last = e.tensor_scalar_add(dst, ps[:, :], bias)
            if n < 3:
                last = emit_dup(e, n + 1)
            for g in range(4):
                if n == 4 * g + 3:
                    for j in range(NQB):
                        s_ = K2[:, g * T + j * QB:g * T + (j + 1) * QB]
                        sl = (g % 2) * 4 + j
                        last = e.tensor_mul(q2s[:, sl * QB:(sl + 1) * QB],
                                            s_, s_)
                qb2 = n - (4 * g + 5)
                if 0 <= qb2 <= 3:
                    e.reduce_max(krow1[:, 0:1], ps_k2[qb2 % 2][0:1, :],
                                 axis=AXX)
                    last = e.tensor_max(kmax1[:, 0:1], kmax1[:, 0:1],
                                        krow1[:, 0:1])
            if 17 <= n <= 48:
                kp, cp, qp = jspec[n - 1]
                if kp == "q":
                    s_ = Q16[:, cp * T + qp * QB:cp * T + (qp + 1) * QB]
                    last = e.tensor_mul(q2s[:, cp * QB:(cp + 1) * QB],
                                        s_, s_)
            if n >= 25 and (n - 25) % 8 == 0 and (n - 25) // 8 <= 3:
                e.reduce_max(rmax16[:, 0:1], ps_n[:, :], axis=AXX)
                last = e.tensor_max(qmax16[:, 0:1], qmax16[:, 0:1],
                                    rmax16[:, 0:1])
            if n == 51:
                e.reduce_max(qM[0:1, 0:1], ps_tqeb[0:1, 0:16], axis=AXX)
                last = e.tensor_mul(b2[0:1, 0:1], qM[0:1, 0:1],
                                    kmax1[0:1, 0:1])
            if n == 52:
                e.wait_ge(SEM["ss"], s0p + 1)
                e.tensor_single_scalar(bval[0:1, 0:1], bnd[0:1, 0:1],
                                       float(-1.001 / 8.0), ALU.mult)
                last = e.tensor_single_scalar(bval[0:1, 0:1], bval[0:1, 0:1],
                                              float(-0.001 / 8.0), ALU.add)
            if n == 53:
                last = e.tensor_copy(expbias[:, 0:1], ps_tqeb[:, 16:17])
            return last
        jobs.append((mm_fn, ev_fn))

    def f_bnd(e):
        e.wait_ge(SEM["sv"], v0p + 2 + 51)
        e.activation(out=bnd[0:1, 0:1], in_=b2[0:1, 0:1], func=AF.Ln,
                     scale=1.0, bias=zero1[0:1, 0:1])
        e.activation(out=bnd[0:1, 0:1], in_=bnd[0:1, 0:1], func=AF.Exp,
                     scale=0.5, bias=zero1[0:1, 0:1]).then_inc(SEM["ss"], 1)
    CNT["ss"] += 1

    pipe_phase("proj", jobs, gpsimd=f_wl, pre_vector=f_dup, scalar=f_bnd)
    ps5_ctx.close()

    jobs = []
    for vc in range(2):
        for kt in range(NKT):
            def mm_fn(e, ps, vc=vc, kt=kt):
                return e.transpose(
                    ps[:, :],
                    V1[:, vc * T + kt * 128:vc * T + (kt + 1) * 128],
                    identB[:, :])

            def ev_fn(e, ps, vc=vc, kt=kt):
                for gg in range(2):
                    g = 2 * vc + gg
                    base = (g * NKT + kt) * 128
                    e.tensor_copy(V2[:, base:base + 64],
                                  ps[:, gg * 64:gg * 64 + 64])
                    ins = e.memset(V2[:, base + 64:base + 128], 1.0)
                return ins
            jobs.append((mm_fn, ev_fn))
    pipe_phase("vtrans", jobs, ps_shape=(128, 128), ps_dtype=BF16)
    pw_ctx.close()

    evac_ctx = ExitStack()
    rA = evac_ctx.enter_context(nc.sbuf_tensor("rA", [128, 2 * QB2], F32))
    rB = evac_ctx.enter_context(nc.sbuf_tensor("rB", [128, 2 * QB2], F32))
    rC = evac_ctx.enter_context(nc.sbuf_tensor("rC", [128, 2 * QB2], F32))


    ps6_ctx = ExitStack()
    ps_s = [ps6_ctx.enter_context(nc.psum_tensor(f"ps_s{h}", [128, QB2], F32))
            for h in range(2)]
    ps_o = [ps6_ctx.enter_context(nc.psum_tensor(f"ps_o{h}", [128, QB2], F32))
            for h in range(2)]
    st0 = CNT["st"]
    sv0 = CNT["sv"]
    ss0 = CNT["ss"]
    sg0 = CNT["sg"]
    NIT = 16

    def attn_tensor(e):
        DL = 3
        for it in range(NIT):
            pr, j = it // 2, it % 2
            g = pr // 2
            itss = ss0 + it * 32
            for kt in range(NKT + DL):
                if kt < NKT:
                    def smm(e, h, half):
                        return e.matmul(
                            ps_s[h][:, half * QB:(half + 1) * QB],
                            K2[h * 64:(h + 1) * 64,
                               g * T + kt * 128:g * T + (kt + 1) * 128],
                            Q16[h * 64:(h + 1) * 64,
                                pr * T + j * QB2 + half * QB:
                                pr * T + j * QB2 + (half + 1) * QB],
                            start=True, stop=True,
                            tile_position=(h * 64, 0))
                    if kt > 0:
                        e.wait_ge(SEM["ss"], itss + 2 * (kt - 1) + 1)
                    else:
                        e.wait_ge(SEM["ss"], itss - 1 if it > 0 else 0)
                    smm(e, 0, 0)
                    smm(e, 0, 1).then_inc(SEM["st"], 1)
                    if kt > 0:
                        e.wait_ge(SEM["ss"], itss + 2 * (kt - 1) + 2)
                    else:
                        e.wait_ge(SEM["ss"], itss)
                    smm(e, 1, 0)
                    smm(e, 1, 1).then_inc(SEM["st"], 1)
                    CNT["st"] += 2
                if kt >= DL:
                    kc = kt - DL
                    e.wait_ge(SEM["ss"], itss + 2 * kc + 2)
                    vb = (g * NKT + kc) * 128
                    for h in range(2):
                        if kc == 0 and it > 0:
                            e.wait_ge(SEM["sv"], sv0 + 3 * it - 2 + h)
                        pt = PT[:, (h * 4 + kc % 4) * QB2:
                                (h * 4 + kc % 4 + 1) * QB2]
                        for half in range(2):
                            mm = e.matmul(
                                ps_o[h][:, half * QB:(half + 1) * QB],
                                V2[:, vb:vb + 128],
                                pt[:, half * QB:(half + 1) * QB],
                                start=(kc == 0), stop=(kc == NKT - 1))
                    mm.then_inc(SEM["st"], 1)
                    CNT["st"] += 1

    def attn_scalar(e):
        DL = 3
        for it in range(NIT):
            itst = st0 + it * 48
            for kt in range(NKT):
                for h in range(2):
                    if kt < DL:
                        e.wait_ge(SEM["st"], itst + 2 * kt + h + 1)
                    else:
                        e.wait_ge(SEM["st"], itst + 3 * kt - DL + h + 1)
                    base = (h * 4 + kt % 4) * QB2
                    e.activation(out=PT[:, base:base + QB2],
                                 in_=ps_s[h][:, :], func=AF.Exp,
                                 scale=0.125,
                                 bias=expbias[:, 0:1]).then_inc(SEM["ss"], 1)
    CNT["ss"] += NIT * 32

    def emit_final(e, it):
        pr, j = it // 2, it % 2
        sl = slice(pr * T + j * QB2, pr * T + (j + 1) * QB2)
        par = (it % 2) * QB2
        e.wait_ge(SEM["sg"], sg0 + 32 * (it + 1))
        e.tensor_mul(tmp32a[0:64, :], rA[0:64, par:par + QB2],
                     rC[0:64, par:par + QB2])
        e.tensor_add(xs[0:64, sl], tmp32a[0:64, :], xs[0:64, sl])
        e.tensor_mul(tmp32a[64:128, :], rC[64:128, par:par + QB2],
                     rB[64:128, par:par + QB2])
        e.tensor_add(xs[64:128, sl], tmp32a[64:128, :], xs[64:128, sl])

    def attn_vector(e):
        for it in range(NIT):
            par = (it % 2) * QB2
            e.wait_ge(SEM["st"], st0 + it * 48 + 48)
            e.tensor_copy(rA[:, par:par + QB2],
                          ps_o[0][:, :]).then_inc(SEM["sv"], 1)
            e.tensor_copy(rB[:, par:par + QB2],
                          ps_o[1][:, :]).then_inc(SEM["sv"], 1)
            e.reciprocal(rA[64:128, par:par + QB2], rA[64:128, par:par + QB2])
            e.reciprocal(rB[64:128, par:par + QB2], rB[64:128, par:par + QB2])
            e.drain().then_inc(SEM["sv"], 1)
            if it > 0:
                emit_final(e, it - 1)
        emit_final(e, NIT - 1)
    CNT["sv"] += NIT * 3

    def attn_gpsimd(e):
        for it in range(NIT):
            par = (it % 2) * QB2
            e.wait_ge(SEM["sv"], sv0 + 3 * it + 3)
            dma(e, rC[0:64, par:par + QB2],
                rA[64:128, par:par + QB2], "sg")
            dma(e, rC[64:128, par:par + QB2],
                rB[0:64, par:par + QB2], "sg")

    phase("attn", tensor=attn_tensor, scalar=attn_scalar, vector=attn_vector,
          gpsimd=attn_gpsimd)
    ps6_ctx.close()
    evac_ctx.close()
    attn_ctx.close()

    post_ctx = ExitStack()
    out32 = post_ctx.enter_context(
        nc.sbuf_tensor("out32", [128, NCH * T], F32))
    xsq = post_ctx.enter_context(
        nc.sbuf_tensor("xsq", [128, NCH * T], BF16))

    def emit_ln(tag, src16, srcapply, dstget, store=False):
        lnctx = ExitStack()
        ps_a = lnctx.enter_context(nc.psum_tensor(f"ps_{tag}a", [1, QB], F32))
        ps_b = lnctx.enter_context(nc.psum_tensor(f"ps_{tag}b", [1, QB], F32))
        ps_i = lnctx.enter_context(
            nc.psum_tensor(f"ps_{tag}i", [128, QB], F32))
        ps_m = lnctx.enter_context(
            nc.psum_tensor(f"ps_{tag}m", [128, QB], F32))
        t0 = CNT["st"]
        v0 = CNT["sv"]
        s0 = CNT["ss"]

        def f_tensor(e):
            for qb in range(NQB):
                e.wait_ge(SEM["sv"], v0 + (1 if qb == 0 else 3 * qb - 1))
                for c in range(NCH):
                    e.matmul(ps_a[:, :], ones128_16[:, 0:1], src16(c, qb),
                             start=(c == 0), stop=(c == NCH - 1))
                for c in range(NCH):
                    mm = e.matmul(ps_b[:, :], ones128_16[:, 0:1],
                                  xsq[:, c * T + qb * QB:c * T + (qb + 1) * QB],
                                  start=(c == 0), stop=(c == NCH - 1))
                mm.then_inc(SEM["st"], 1)
                e.wait_ge(SEM["sv"], v0 + 3 * qb + 3)
                e.matmul(ps_i[:, :], ones1_16[0:1, :], r_inv[0:1, :],
                         start=True, stop=True)
                e.matmul(ps_m[:, :], ones1_16[0:1, :], r_nminv[0:1, :],
                         start=True, stop=True).then_inc(SEM["st"], 1)
        CNT["st"] += 2 * NQB

        def f_vector(e):
            for c in range(NCH):
                for qb in range(NQB):
                    s = src16(c, qb)
                    e.tensor_mul(
                        xsq[:, c * T + qb * QB:c * T + (qb + 1) * QB], s, s)
            e.drain().then_inc(SEM["sv"], 1)
            for qb in range(NQB):
                e.wait_ge(SEM["st"], t0 + 2 * qb + 1)
                e.tensor_single_scalar(s_mean[0:1, :], ps_a[0:1, :],
                                       float(1.0 / D), ALU.mult)
                e.tensor_mul(s_msq[0:1, :], s_mean[0:1, :], s_mean[0:1, :])
                e.tensor_single_scalar(s_var[0:1, :], ps_b[0:1, :],
                                       float(1.0 / D), ALU.mult)
                e.tensor_sub(s_var[0:1, :], s_var[0:1, :],
                             s_msq[0:1, :]).then_inc(SEM["sv"], 1)
                e.wait_ge(SEM["ss"], s0 + qb + 1)
                e.tensor_mul(s_minv[0:1, :], s_mean[0:1, :], s_inv[0:1, :])
                e.tensor_copy(r_inv[0:1, :], s_inv[0:1, :])
                e.tensor_single_scalar(
                    r_nminv[0:1, :], s_minv[0:1, :], -1.0,
                    ALU.mult).then_inc(SEM["sv"], 1)
                e.wait_ge(SEM["st"], t0 + 2 * qb + 2)
                last = None
                for c in range(NCH):
                    e.tensor_mul(tmp32a[:, 0:QB], srcapply(c, qb), ps_i[:, :])
                    last = e.tensor_add(dstget(c, qb), tmp32a[:, 0:QB],
                                        ps_m[:, :])
                last.then_inc(SEM["sv"], 1)
        CNT["sv"] += 1 + 3 * NQB

        def f_scalar(e):
            for qb in range(NQB):
                e.wait_ge(SEM["sv"], v0 + 3 * qb + 2)
                e.activation(out=s_inv[0:1, :], in_=s_var[0:1, :], func=AF.Ln,
                             scale=1.0, bias=eps1[0:1, 0:1])
                e.activation(out=s_inv[0:1, :], in_=s_inv[0:1, :],
                             func=AF.Exp, scale=-0.5,
                             bias=zero1[0:1, 0:1]).then_inc(SEM["ss"], 1)
        CNT["ss"] += NQB

        f_sync = None
        if store:
            def f_sync(e):
                for qb in range(NQB):
                    e.wait_ge(SEM["sv"], v0 + 3 * qb + 4)
                    for c in range(NCH):
                        dma(e, yT[c * 128:(c + 1) * 128,
                                  qb * QB:(qb + 1) * QB],
                            out32[:, c * T + qb * QB:c * T + (qb + 1) * QB],
                            "sd")
        phase(f"ln_{tag}", tensor=f_tensor, vector=f_vector, scalar=f_scalar,
              sync=f_sync)
        lnctx.close()

    lf_ctx = ExitStack()
    ps_a = lf_ctx.enter_context(nc.psum_tensor("ps_lfa", [1, QB], F32))
    ps_b = lf_ctx.enter_context(nc.psum_tensor("ps_lfb", [1, QB], F32))
    ps_i = lf_ctx.enter_context(nc.psum_tensor("ps_lfi", [128, QB], F32))
    ps_m = lf_ctx.enter_context(nc.psum_tensor("ps_lfm", [128, QB], F32))
    ps_i2 = lf_ctx.enter_context(nc.psum_tensor("ps_lfi2", [128, QB], F32))
    ps_m2 = lf_ctx.enter_context(nc.psum_tensor("ps_lfm2", [128, QB], F32))
    pp = [lf_ctx.enter_context(nc.psum_tensor(f"pp_lf{i}", [128, QB], F32))
          for i in range(2)]
    t0 = CNT["st"]
    v0 = CNT["sv"]
    s0 = CNT["ss"]

    TORD = (["st1_0", "st1_1", "bc1_0", "bc1_1"]
            + [f"ffn_0_{b}" for b in range(NCH)]
            + ["st1_2", "bc1_2"]
            + [f"ffn_1_{b}" for b in range(NCH)]
            + ["st1_3", "bc1_3", "st2_0", "bc2_0"]
            + [f"ffn_2_{b}" for b in range(NCH)]
            + ["st2_1", "bc2_1"]
            + [f"ffn_3_{b}" for b in range(NCH)]
            + ["st2_2", "bc2_2", "st2_3", "bc2_3"])
    VORD = ["sq1_0", "v1a_0", "v2a_0", "sq1_1", "ap1_0", "v1a_1", "v2a_1",
            "sq1_2", "ap1_1", "v1a_2", "v2a_2", "cast_0", "sq1_3", "ap1_2",
            "v1a_3", "v2a_3", "sq2_0", "v1b_0", "v2b_0", "cast_1", "ap1_3",
            "sq2_1", "v1b_1", "v2b_1", "ap2_0", "cast_2", "sq2_2", "v1b_2",
            "v2b_2", "ap2_1", "cast_3", "sq2_3", "v1b_3", "v2b_3", "ap2_2",
            "ap2_3"]
    SORD = (["rs1_0", "rs1_1"] + [f"sf_0_{b}" for b in range(NCH)]
            + ["rs1_2"] + [f"sf_1_{b}" for b in range(NCH)]
            + ["rs1_3", "rs2_0"] + [f"sf_2_{b}" for b in range(NCH)]
            + ["rs2_1"] + [f"sf_3_{b}" for b in range(NCH)]
            + ["rs2_2", "rs2_3"])
    TP = {nm: t0 + i + 1 for i, nm in enumerate(TORD)}
    VP = {nm: v0 + i + 1 for i, nm in enumerate(VORD)}
    SP_ = {nm: s0 + i + 1 for i, nm in enumerate(SORD)}
    AB_READER = {"st1_1": "v1a_0", "st1_2": "v1a_1", "st1_3": "v1a_2",
                 "st2_0": "v1a_3", "st2_1": "v1b_0", "st2_2": "v1b_1",
                 "st2_3": "v1b_2"}

    def xsl(c, qb):
        return xs[:, c * T + qb * QB:c * T + (qb + 1) * QB]

    def osl(c, qb):
        return out32[:, c * T + qb * QB:c * T + (qb + 1) * QB]

    def lf_tensor(e):
        ffjobs = []
        for nm in TORD:
            kind, q = nm.rsplit("_", 1)[0].split("_")[0], nm.split("_")[1]
            q = int(q)
            if kind in ("st1", "st2"):
                e.wait_ge(SEM["sv"], VP[("sq1" if kind == "st1" else "sq2")
                                        + f"_{q}"])
                if nm in AB_READER:
                    e.wait_ge(SEM["sv"], VP[AB_READER[nm]])
                for c in range(NCH):
                    e.matmul(ps_a[:, :], ones128_16[:, 0:1], xsl(c, q),
                             start=(c == 0), stop=(c == NCH - 1))
                for c in range(NCH):
                    mm = e.matmul(ps_b[:, :], ones128_16[:, 0:1],
                                  xsq[:, c * T + q * QB:c * T + (q + 1) * QB],
                                  start=(c == 0), stop=(c == NCH - 1))
                mm.then_inc(SEM["st"], 1)
            elif kind in ("bc1", "bc2"):
                e.wait_ge(SEM["sv"], VP[("v2a" if kind == "bc1" else "v2b")
                                        + f"_{q}"])
                if q > 0:
                    e.wait_ge(SEM["sv"], VP[("ap1" if kind == "bc1"
                                             else "ap2") + f"_{q - 1}"])
                pi, pm = (ps_i, ps_m) if kind == "bc1" else (ps_i2, ps_m2)
                ri, rn = ((r_inv, r_nminv) if kind == "bc1"
                          else (r_inv2, r_nminv2))
                e.matmul(pi[:, :], ones1_16[0:1, :], ri[0:1, :],
                         start=True, stop=True)
                e.matmul(pm[:, :], ones1_16[0:1, :], rn[0:1, :],
                         start=True, stop=True).then_inc(SEM["st"], 1)
            else:
                b = int(nm.split("_")[2])
                j = len(ffjobs)
                ffjobs.append(nm)
                if b == 0:
                    e.wait_ge(SEM["sv"], VP[f"ap1_{q}"])
                if j >= 2:
                    e.wait_ge(SEM["ss"], SP_[ffjobs[j - 2].replace(
                        "ffn", "sf")])
                for a in range(NCH):
                    mm = e.matmul(
                        pp[j % 2][:, :],
                        M16[:, a * D + b * 128:a * D + (b + 1) * 128],
                        xsl(a, q),
                        start=(a == 0), stop=(a == NCH - 1))
                mm.then_inc(SEM["st"], 1)
    CNT["st"] += len(TORD)

    def lf_vector(e):
        for nm in VORD:
            kind, q = nm.rsplit("_", 1)
            q = int(q)
            if kind in ("sq1", "sq2"):
                for c in range(NCH):
                    s = xsl(c, q)
                    mm = e.tensor_mul(
                        xsq[:, c * T + q * QB:c * T + (q + 1) * QB], s, s)
                mm.then_inc(SEM["sv"], 1)
            elif kind == "cast":
                e.wait_ge(SEM["ss"], SP_[f"sf_{q}_7"])
                for c in range(NCH):
                    mm = e.tensor_copy(xsl(c, q), osl(c, q))
                mm.then_inc(SEM["sv"], 1)
            elif kind in ("v1a", "v1b"):
                e.wait_ge(SEM["st"], TP[("st1" if kind == "v1a" else "st2")
                                        + f"_{q}"])
                e.tensor_single_scalar(s_mean[0:1, :], ps_a[0:1, :],
                                       float(1.0 / D), ALU.mult)
                e.tensor_mul(s_msq[0:1, :], s_mean[0:1, :], s_mean[0:1, :])
                e.tensor_single_scalar(s_var[0:1, :], ps_b[0:1, :],
                                       float(1.0 / D), ALU.mult)
                e.tensor_sub(s_var[0:1, :], s_var[0:1, :],
                             s_msq[0:1, :]).then_inc(SEM["sv"], 1)
            elif kind in ("v2a", "v2b"):
                e.wait_ge(SEM["ss"], SP_[("rs1" if kind == "v2a" else "rs2")
                                         + f"_{q}"])
                ri, rn = ((r_inv, r_nminv) if kind == "v2a"
                          else (r_inv2, r_nminv2))
                e.tensor_mul(s_minv[0:1, :], s_mean[0:1, :], s_inv[0:1, :])
                e.tensor_copy(ri[0:1, :], s_inv[0:1, :])
                e.tensor_single_scalar(
                    rn[0:1, :], s_minv[0:1, :], -1.0,
                    ALU.mult).then_inc(SEM["sv"], 1)
            elif kind == "ap1":
                e.wait_ge(SEM["st"], TP[f"bc1_{q}"])
                for c in range(NCH):
                    e.tensor_mul(tmp32a[:, 0:QB], xsl(c, q), ps_i[:, :])
                    last = e.tensor_add(xsl(c, q), tmp32a[:, 0:QB],
                                        ps_m[:, :])
                last.then_inc(SEM["sv"], 1)
            elif kind == "ap2":
                e.wait_ge(SEM["st"], TP[f"bc2_{q}"])
                for c in range(NCH):
                    e.tensor_mul(tmp32a[:, 0:QB], osl(c, q), ps_i2[:, :])
                    last = e.tensor_add(osl(c, q), tmp32a[:, 0:QB],
                                        ps_m2[:, :])
                last.then_inc(SEM["sv"], 1)
    CNT["sv"] += len(VORD)

    def lf_scalar(e):
        for nm in SORD:
            kind, q = nm.rsplit("_", 1)
            if kind.startswith("sf"):
                q, b = int(nm.split("_")[1]), int(nm.split("_")[2])
                e.wait_ge(SEM["st"], TP[nm.replace("sf", "ffn")])
                j = TORD.index(nm.replace("sf", "ffn"))
                jj = [x for x in TORD if x.startswith("ffn")].index(
                    nm.replace("sf", "ffn"))
                e.activation(out=osl(b, q), in_=pp[jj % 2][:, :],
                             func=AF.Identity, scale=1.0,
                             bias=bf_sb[:, b:b + 1]).then_inc(SEM["ss"], 1)
            else:
                q = int(q)
                e.wait_ge(SEM["sv"], VP[("v1a" if kind == "rs1" else "v1b")
                                        + f"_{q}"])
                e.activation(out=s_inv[0:1, :], in_=s_var[0:1, :],
                             func=AF.Ln, scale=1.0, bias=eps1[0:1, 0:1])
                e.activation(out=s_inv[0:1, :], in_=s_inv[0:1, :],
                             func=AF.Exp, scale=-0.5,
                             bias=zero1[0:1, 0:1]).then_inc(SEM["ss"], 1)
    CNT["ss"] += len(SORD)

    def lf_sync(e):
        for q in range(NQB):
            e.wait_ge(SEM["sv"], VP[f"ap2_{q}"])
            for c in range(NCH):
                dma(e, yT[c * 128:(c + 1) * 128, q * QB:(q + 1) * QB],
                    out32[:, c * T + q * QB:c * T + (q + 1) * QB], "sd")

    phase("post", tensor=lf_tensor, vector=lf_vector, scalar=lf_scalar,
          sync=lf_sync)
    lf_ctx.close()

    if dbg:
        p = SP()

        def f_dbg(e):
            if "M16" in dbg:
                dma(e, dbg["M16"][:, :], M16[:, :], "sg")
            if "xs" in dbg:
                dma(e, dbg["xs"][:, :], xs[:, :], "sg")
            if "expbias" in dbg:
                dma(e, dbg["expbias"][:, :], expbias[:, :], "sg")
        p.step("gpsimd", f_dbg)
        p.emit("dbg")

    p = SP()
    p.step("sync", lambda e: e.wait_ge(SEM["sd"], CNT["sd"]))
    p.emit("fin")

    return nc


_NC_CACHE = {}


def kernel(L0_attn_h, L0_attn_wq_w, L0_attn_wq_b, L0_attn_wk_w, L0_attn_wk_b,
           L0_attn_wv_w, L0_attn_wv_b, L0_ffn_w_freq_real, L0_ffn_w_freq_imag,
           L0_ffn_bias):
    h = np.asarray(L0_attn_h, dtype=np.float32)
    wq = np.asarray(L0_attn_wq_w, dtype=np.float32)
    wk = np.asarray(L0_attn_wk_w, dtype=np.float32)
    wv = np.asarray(L0_attn_wv_w, dtype=np.float32)
    wfr = np.asarray(L0_ffn_w_freq_real, dtype=np.float32)
    wfi = np.asarray(L0_ffn_w_freq_imag, dtype=np.float32)
    bq = np.asarray(L0_attn_wq_b, dtype=np.float32)
    bk = np.asarray(L0_attn_wk_b, dtype=np.float32)
    bv = np.asarray(L0_attn_wv_b, dtype=np.float32)
    bf = np.asarray(L0_ffn_bias, dtype=np.float32)

    wqT = np.ascontiguousarray(wq.T)
    wkT = np.ascontiguousarray(wk.T)
    wvT = np.ascontiguousarray(wv.T)
    wfrT = np.ascontiguousarray(wfr.T)
    wfiT = np.ascontiguousarray(wfi.T)
    bq2 = np.ascontiguousarray(bq.reshape(8, 128).T)
    bk2 = np.ascontiguousarray(np.tile(bk.reshape(4, 64), (1, 2)).T)
    bv2 = np.ascontiguousarray(bv.reshape(2, 128).T)
    bf2 = np.ascontiguousarray(bf.reshape(8, 128).T)

    if "nc" not in _NC_CACHE:
        _NC_CACHE["nc"] = build_program()
    nc = _NC_CACHE["nc"]

    in_maps = []
    for b in range(8):
        in_maps.append({
            "hT": np.ascontiguousarray(h[b].T),
            "wqT": wqT, "wkT": wkT, "wvT": wvT,
            "wfrT": wfrT, "wfiT": wfiT,
            "bq_in": bq2, "bk_in": bk2, "bv_in": bv2, "bf_in": bf2,
        })
    res = run_bass_kernel_spmd(nc, in_maps, core_ids=list(range(8)))
    out = np.stack([np.asarray(r["yT"]).T for r in res.results], axis=0)
    return np.ascontiguousarray(out.astype(np.float32))


# version 19
# speedup vs baseline: 1.0440x; 1.0059x over previous
from contextlib import ExitStack

import numpy as np
import ml_dtypes
import concourse.bass as bass
import concourse.mybir as mybir
from concourse.bass_utils import run_bass_kernel_spmd

F32 = mybir.dt.float32
F16 = mybir.dt.float16
BF16 = mybir.dt.bfloat16
I32 = mybir.dt.int32
AF = mybir.ActivationFunctionType
ALU = mybir.AluOpType
AXX = mybir.AxisListType.X

D = 1024
T = 2048
NCH = 8
NKT = 16
QB = 512
NQB = T // QB
QB2 = 1024
EPS = 1e-5


def _load_act_table(nc, e, set_name):
    from concourse.hw_specs import get_activation_tables
    for idx, nm in enumerate(get_activation_tables(nc.m.arch)):
        if nm == set_name:
            return e.add_instruction(
                mybir.InstLoadActFuncSet(
                    name=nc.get_next_instruction_name(),
                    act_func_set_id=idx, ins=[], outs=[]))
    raise ValueError(set_name)


def build_program(debug=()):
    nc = bass.Bass("TRN2", target_bir_lowering=False,
                   detect_race_conditions=False)

    hT = nc.dram_tensor("hT", [D, T], F32, kind="ExternalInput").ap()
    wqT = nc.dram_tensor("wqT", [D, D], F32, kind="ExternalInput").ap()
    wkT = nc.dram_tensor("wkT", [D, 256], F32, kind="ExternalInput").ap()
    wvT = nc.dram_tensor("wvT", [D, 256], F32, kind="ExternalInput").ap()
    wfrT = nc.dram_tensor("wfrT", [D, D], F32, kind="ExternalInput").ap()
    wfiT = nc.dram_tensor("wfiT", [D, D], F32, kind="ExternalInput").ap()
    bq_in = nc.dram_tensor("bq_in", [128, 8], F32, kind="ExternalInput").ap()
    bk_in = nc.dram_tensor("bk_in", [128, 4], F32, kind="ExternalInput").ap()
    bv_in = nc.dram_tensor("bv_in", [128, 2], F32, kind="ExternalInput").ap()
    bf_in = nc.dram_tensor("bf_in", [128, 8], F32, kind="ExternalInput").ap()
    yT = nc.dram_tensor("yT", [D, T], F32, kind="ExternalOutput").ap()

    dbg = {}
    for name, shape, dt in debug:
        dbg[name] = nc.dram_tensor("dbg_" + name, shape, dt,
                                   kind="ExternalOutput").ap()

    iota_np = np.arange(D, dtype=np.float32)[None, :]
    ident32_np = np.eye(128, dtype=np.float32)
    identB_np = np.eye(128, dtype=np.float32).astype(ml_dtypes.bfloat16)
    selq_np = np.zeros((128, 128), dtype=np.float32)
    for c in range(8):
        selq_np[0:64, c * 16 + 2 * c] = 1.0
        selq_np[64:128, c * 16 + 2 * c + 1] = 1.0
    selq_np = selq_np.astype(np.float16)
    onestop_np = np.zeros((128, 1), dtype=np.float32)
    onestop_np[0:64, 0] = 1.0
    onestop_np = onestop_np.astype(np.float16)

    iota_d = nc.inline_tensor(iota_np, name="iota_c").ap()
    ident32_d = nc.inline_tensor(ident32_np, name="ident32_c").ap()
    identB_d = nc.inline_tensor(identB_np, name="identB_c").ap()
    selq_d = nc.inline_tensor(selq_np, name="selq_c").ap()
    onestop_d = nc.inline_tensor(onestop_np, name="onestop_c").ap()

    ctx = ExitStack()
    SEM = {}
    for s in ("sd", "sg", "st", "sv", "ss", "ph", "tok"):
        SEM[s] = ctx.enter_context(nc.semaphore(s))
    CNT = {k: 0 for k in SEM}
    PHN = [0]
    TOK = [0]

    def dma(e, out, in_, sem):
        e.dma_start(out=out, in_=in_).then_inc(SEM[sem], 16)
        CNT[sem] += 16

    def phase(name, sync=None, tensor=None, vector=None, scalar=None,
              gpsimd=None):
        k = PHN[0]

        def wrap(fn):
            def inner(e):
                e.wait_ge(SEM["ph"], 5 * k)
                if fn is not None:
                    fn(e)
                e.drain().then_inc(SEM["ph"], 1)
            return inner
        with nc.named_scope(name), nc.Block(name) as blk:
            blk.sync(wrap(sync))
            blk.tensor(wrap(tensor))
            blk.vector(wrap(vector))
            blk.scalar(wrap(scalar))
            blk.gpsimd(wrap(gpsimd))
        PHN[0] += 1

    class SP:

        def __init__(self):
            self.steps = []

        def step(self, eng, fn):
            self.steps.append((eng, fn))

        def emit(self, name):
            base = TOK[0]
            steps = list(self.steps)

            def mk(engname):
                def fn(e):
                    for i, (en, f) in enumerate(steps):
                        if en == engname:
                            e.wait_ge(SEM["tok"], base + i)
                            f(e)
                            e.drain().then_inc(SEM["tok"], 1)
                return fn
            phase(name, sync=mk("sync"), tensor=mk("tensor"),
                  vector=mk("vector"), scalar=mk("scalar"),
                  gpsimd=mk("gpsimd"))
            TOK[0] += len(steps)

    def pipe_phase(name, jobs, gpsimd=None, scalar=None, pre_tensor=None,
                   pre_vector=None, ps_shape=(128, QB), ps_dtype=F32):
        t0 = CNT["st"]
        v0 = CNT["sv"]
        pvi = 1 if pre_vector is not None else 0
        pp_ctx = ExitStack()
        pss = [pp_ctx.enter_context(
            nc.psum_tensor(f"pp_{name}{i}", list(ps_shape), ps_dtype))
            for i in range(2)]

        def ft(e):
            if pre_tensor is not None:
                pre_tensor(e)
            for n, (mm_fn, _) in enumerate(jobs):
                if n >= 2:
                    e.wait_ge(SEM["sv"], v0 + pvi + n - 1)
                elif pvi:
                    e.wait_ge(SEM["sv"], v0 + 1)
                mm_fn(e, pss[n % 2]).then_inc(SEM["st"], 1)

        def fv(e):
            if pre_vector is not None:
                pre_vector(e)
                e.drain().then_inc(SEM["sv"], 1)
            for n, (_, ev_fn) in enumerate(jobs):
                e.wait_ge(SEM["st"], t0 + n + 1)
                ev_fn(e, pss[n % 2]).then_inc(SEM["sv"], 1)

        CNT["st"] += len(jobs)
        CNT["sv"] += len(jobs) + pvi
        phase(name, tensor=ft, vector=fv, gpsimd=gpsimd, scalar=scalar)
        pp_ctx.close()

    xs = nc.alloc_sbuf_tensor("xs", [128, NCH * T], F16)
    M16 = nc.alloc_sbuf_tensor("M16", [128, NCH * D], F16)
    iota_sb = nc.alloc_sbuf_tensor("iota_sb", [1, D], F32)
    ident32 = nc.alloc_sbuf_tensor("ident32", [128, 128], F32)
    identB = nc.alloc_sbuf_tensor("identB", [128, 128], BF16)
    selq = nc.alloc_sbuf_tensor("selq", [128, 128], F16)
    onestop = nc.alloc_sbuf_tensor("onestop", [128, 1], F16)
    bq_sb = nc.alloc_sbuf_tensor("bq_sb", [128, 8], F32)
    bk_sb = nc.alloc_sbuf_tensor("bk_sb", [128, 4], F32)
    bv_sb = nc.alloc_sbuf_tensor("bv_sb", [128, 2], F32)
    bf_sb = nc.alloc_sbuf_tensor("bf_sb", [128, 8], F32)
    expbias = nc.alloc_sbuf_tensor("expbias", [128, 1], F32)
    negpi = nc.alloc_sbuf_tensor("negpi", [128, 1], F32)
    eps1 = nc.alloc_sbuf_tensor("eps1", [1, 1], F32)
    zero1 = nc.alloc_sbuf_tensor("zero1", [1, 1], F32)
    ones1_32 = nc.alloc_sbuf_tensor("ones1_32", [1, 128], F32)
    ones1_16 = nc.alloc_sbuf_tensor("ones1_16", [1, 128], F16)
    ones128_16 = nc.alloc_sbuf_tensor("ones128_16", [128, 1], F16)
    tmp32a = nc.alloc_sbuf_tensor("tmp32a", [128, QB2], F32)
    q2s = nc.alloc_sbuf_tensor("q2s", [128, NCH * QB], BF16)
    s_mean = nc.alloc_sbuf_tensor("s_mean", [1, QB], F32)
    s_var = nc.alloc_sbuf_tensor("s_var", [1, QB], F32)
    s_msq = tmp32a[0:1, 0:QB]
    s_minv = tmp32a[0:1, QB:2 * QB]
    s_inv = nc.alloc_sbuf_tensor("s_inv", [1, QB], F32)
    r_inv = nc.alloc_sbuf_tensor("r_inv", [1, QB], F16)
    r_nminv = nc.alloc_sbuf_tensor("r_nminv", [1, QB], F16)
    r_inv2 = nc.alloc_sbuf_tensor("r_inv2", [1, QB], F16)
    r_nminv2 = nc.alloc_sbuf_tensor("r_nminv2", [1, QB], F16)
    qmax16 = nc.alloc_sbuf_tensor("qmax16", [16, 1], F32)
    rmax16 = nc.alloc_sbuf_tensor("rmax16", [16, 1], F32)
    kmax1 = nc.alloc_sbuf_tensor("kmax1", [1, 1], F32)
    krow1 = nc.alloc_sbuf_tensor("krow1", [1, 1], F32)
    qM = nc.alloc_sbuf_tensor("qM", [1, 1], F32)
    b2 = nc.alloc_sbuf_tensor("b2", [1, 1], F32)
    bnd = nc.alloc_sbuf_tensor("bnd", [1, 1], F32)
    bval = nc.alloc_sbuf_tensor("bval", [1, 1], F32)

    pre_ctx = ExitStack()
    C16 = pre_ctx.enter_context(nc.sbuf_tensor("C16", [128, NCH * D], F16))
    S16 = pre_ctx.enter_context(nc.sbuf_tensor("S16", [128, NCH * D], F16))
    Sn16 = pre_ctx.enter_context(nc.sbuf_tensor("Sn16", [128, NCH * D], F16))
    A_ctx = ExitStack()
    A16r = A_ctx.enter_context(nc.sbuf_tensor("A16r", [128, NCH * D], F16))
    A16i = A_ctx.enter_context(nc.sbuf_tensor("A16i", [128, NCH * D], F16))
    w_ctx = ExitStack()
    wr16 = w_ctx.enter_context(nc.sbuf_tensor("wr16", [128, NCH * D], F16))
    wi16 = w_ctx.enter_context(nc.sbuf_tensor("wi16", [128, NCH * D], F16))
    tw_ctx = ExitStack()
    ik32 = tw_ctx.enter_context(nc.sbuf_tensor("ik32", [128, D], I32))
    yci = tw_ctx.enter_context(nc.sbuf_tensor("yci", [128, D], I32))
    ycf = tw_ctx.enter_context(nc.sbuf_tensor("ycf", [128, D], F32))
    ysf = tw_ctx.enter_context(nc.sbuf_tensor("ysf", [128, D], F32))
    ps_tw_ctx = ExitStack()
    ps_dk = [ps_tw_ctx.enter_context(
        nc.psum_tensor(f"ps_dk{hh}", [128, QB], F32)) for hh in range(4)]

    sd0 = CNT["sd"]
    st0 = CNT["st"]
    sv0 = CNT["sv"]
    ss0 = CNT["ss"]

    def init_sync(e):
        dma(e, iota_sb[:, :], iota_d[:, :], "sd")
        dma(e, ident32[:, :], ident32_d[:, :], "sd")
        dma(e, identB[:, :], identB_d[:, :], "sd")
        dma(e, selq[:, :], selq_d[:, :], "sd")
        dma(e, onestop[:, :], onestop_d[:, :], "sd")
        dma(e, bq_sb[:, :], bq_in[:, :], "sd")
        dma(e, bk_sb[:, :], bk_in[:, :], "sd")
        dma(e, bv_sb[:, :], bv_in[:, :], "sd")
        dma(e, bf_sb[:, :], bf_in[:, :], "sd")

    def init_gpsimd(e):
        for c in range(NCH):
            dma(e, xs[:, c * T:(c + 1) * T], hT[c * 128:(c + 1) * 128, :],
                "sg")
        for c in range(NCH):
            dma(e, wr16[:, c * D:(c + 1) * D],
                wfrT[c * 128:(c + 1) * 128, :], "sg")
            dma(e, wi16[:, c * D:(c + 1) * D],
                wfiT[c * 128:(c + 1) * 128, :], "sg")

    def init_tensor(e):
        e.wait_ge(SEM["sd"], sd0 + 16)
        for c in range(NCH):
            if c >= 2:
                e.wait_ge(SEM["sv"], sv0 + c)
            for half in range(2):
                mm = e.matmul(ps_dk[(c % 2) * 2 + half][:, :],
                              iota_sb[0:1, c * 128:(c + 1) * 128],
                              iota_sb[0:1, half * QB:(half + 1) * QB],
                              start=True, stop=True)
            mm.then_inc(SEM["st"], 1)
    CNT["st"] += NCH

    def init_vector(e):
        e.memset(negpi[:, :], float(-np.pi))
        e.memset(eps1[:, :], float(EPS))
        e.memset(zero1[:, :], 0.0)
        e.memset(ones1_32[:, :], 1.0)
        e.memset(ones1_16[:, :], 1.0)
        e.memset(ones128_16[:, :], 1.0)
        e.memset(qmax16[:, :], 0.0)
        e.memset(kmax1[:, :], 0.0)
        e.drain().then_inc(SEM["sv"], 1)
        for c in range(NCH):
            e.wait_ge(SEM["st"], st0 + c + 1)
            if c >= 1:
                e.wait_ge(SEM["ss"], ss0 + c)
            e.tensor_copy(ik32[:, 0:QB], ps_dk[(c % 2) * 2][:, :])
            e.tensor_copy(ik32[:, QB:D], ps_dk[(c % 2) * 2 + 1][:, :])
            e.tensor_single_scalar(yci[:, :], ik32[:, :], 768, ALU.add)
            e.tensor_single_scalar(yci[:, :], yci[:, :], 1023, ALU.bitwise_and)
            e.tensor_copy(ycf[:, :], yci[:, :])
            e.tensor_single_scalar(yci[:, :], ik32[:, :], 512, ALU.add)
            e.tensor_single_scalar(yci[:, :], yci[:, :], 1023, ALU.bitwise_and)
            e.tensor_copy(ysf[:, :], yci[:, :])
            e.drain().then_inc(SEM["sv"], 1)
        e.wait_ge(SEM["ss"], ss0 + NCH)
        e.tensor_single_scalar(Sn16[:, :], S16[:, :], -1.0, ALU.mult)
    CNT["sv"] += NCH + 1

    def init_scalar(e):
        _load_act_table(nc, e, "trig_and_small")
        sc = float(2 * np.pi / D)
        for c in range(NCH):
            e.wait_ge(SEM["sv"], sv0 + c + 2)
            e.activation(out=C16[:, c * D:(c + 1) * D], in_=ycf[:, :],
                         func=AF.Sin, scale=sc, bias=negpi[:, 0:1])
            e.activation(out=S16[:, c * D:(c + 1) * D], in_=ysf[:, :],
                         func=AF.Sin, scale=sc,
                         bias=negpi[:, 0:1]).then_inc(SEM["ss"], 1)
        _load_act_table(nc, e, "natural_log_exp_and_others")
    CNT["ss"] += NCH

    phase("init", sync=init_sync, gpsimd=init_gpsimd, tensor=init_tensor,
          vector=init_vector, scalar=init_scalar)
    sd_base = CNT["sd"]
    sg_h0 = CNT["sg"]
    sg_h = sg_h0 - 16 * 16
    sg_w = CNT["sg"]
    ps_tw_ctx.close()
    tw_ctx.close()

    jobs = []
    for dst in range(2):
        for dchunk in range(NCH):
            for half in range(2):
                def mm_fn(e, ps, dst=dst, dchunk=dchunk, half=half,
                          first=(dst == 0 and dchunk == 0 and half == 0)):
                    if first:
                        e.wait_ge(SEM["sg"], sg_w)
                    t1 = C16 if dst == 0 else S16
                    t2 = Sn16 if dst == 0 else C16
                    for o in range(NCH):
                        e.matmul(ps[:, :],
                                 wr16[:, o * D + dchunk * 128:
                                      o * D + (dchunk + 1) * 128],
                                 t1[:, o * D + half * QB:
                                    o * D + (half + 1) * QB],
                                 start=(o == 0), stop=False)
                    for o in range(NCH):
                        mm = e.matmul(ps[:, :],
                                      wi16[:, o * D + dchunk * 128:
                                           o * D + (dchunk + 1) * 128],
                                      t2[:, o * D + half * QB:
                                         o * D + (half + 1) * QB],
                                      start=False, stop=(o == NCH - 1))
                    return mm

                def ev_fn(e, ps, dst=dst, dchunk=dchunk, half=half):
                    A = A16r if dst == 0 else A16i
                    return e.tensor_single_scalar(
                        A[:, dchunk * D + half * QB:
                          dchunk * D + (half + 1) * QB],
                        ps[:, :], float(1.0 / D), ALU.mult)
                jobs.append((mm_fn, ev_fn))
    pipe_phase("pre_A", jobs)
    w_ctx.close()

    jobs = []
    for achunk in range(NCH):
        for half in range(2):
            def mm_fn(e, ps, achunk=achunk, half=half):
                for dd in range(NCH):
                    e.matmul(ps[:, :],
                             C16[:, dd * D + achunk * 128:
                                 dd * D + (achunk + 1) * 128],
                             A16r[:, dd * D + half * QB:
                                  dd * D + (half + 1) * QB],
                             start=(dd == 0), stop=False)
                for dd in range(NCH):
                    mm = e.matmul(ps[:, :],
                                  S16[:, dd * D + achunk * 128:
                                      dd * D + (achunk + 1) * 128],
                                  A16i[:, dd * D + half * QB:
                                       dd * D + (half + 1) * QB],
                                  start=False, stop=(dd == NCH - 1))
                return mm

            def ev_fn(e, ps, achunk=achunk, half=half):
                last = e.tensor_copy(
                    M16[:, achunk * D + half * QB:
                        achunk * D + (half + 1) * QB], ps[:, :])
                if achunk // 4 == half:
                    off = achunk * D + achunk * 128
                    last = e.tensor_add(M16[:, off:off + 128],
                                        M16[:, off:off + 128], ident32[:, :])
                return last
            jobs.append((mm_fn, ev_fn))
    pipe_phase("pre_M", jobs)
    A_ctx.close()
    pre_ctx.close()

    attn_ctx = ExitStack()
    Q16 = attn_ctx.enter_context(nc.sbuf_tensor("Q16", [128, NCH * T], F16))
    K2 = attn_ctx.enter_context(nc.sbuf_tensor("K2", [128, 4 * T], F16))
    V2 = attn_ctx.enter_context(
        nc.sbuf_tensor("V2", [128, 4 * NKT * 128], BF16))
    PT = attn_ctx.enter_context(nc.sbuf_tensor("PT", [128, 8 * QB2], BF16))
    pw_ctx = ExitStack()
    wq16 = pw_ctx.enter_context(nc.sbuf_tensor("wq16", [128, NCH * D], F16))
    wk1 = pw_ctx.enter_context(nc.sbuf_tensor("wk1", [128, NCH * 256], F16))
    wk2 = pw_ctx.enter_context(nc.sbuf_tensor("wk2", [128, NCH * 512], F16))
    wv16 = pw_ctx.enter_context(nc.sbuf_tensor("wv16", [128, NCH * 256], BF16))
    V1 = pw_ctx.enter_context(nc.sbuf_tensor("V1", [128, 2 * T], BF16))

    ps5_ctx = ExitStack()
    ps_n = ps5_ctx.enter_context(nc.psum_tensor("ps_n", [16, QB], F32))
    ps_k2 = [ps5_ctx.enter_context(nc.psum_tensor(f"ps_k{i}", [1, QB], F32))
             for i in range(2)]
    ps_tqeb = ps5_ctx.enter_context(nc.psum_tensor("ps_tqeb", [128, 32], F32))
    sg_wl0 = CNT["sg"]
    t0p = CNT["st"]
    v0p = CNT["sv"]
    s0p = CNT["ss"]

    def f_wl(e):
        for c in range(NCH):
            dma(e, wk1[:, c * 256:(c + 1) * 256],
                wkT[c * 128:(c + 1) * 128, :], "sg")
        for c in range(NCH):
            dma(e, wq16[:, c * D:(c + 1) * D],
                wqT[c * 128:(c + 1) * 128, :], "sg")
        for c in range(NCH):
            dma(e, wv16[:, c * 256:(c + 1) * 256],
                wvT[c * 128:(c + 1) * 128, :], "sg")

    def emit_dup(e, g):
        for f in range(NCH):
            src = wk1[:, f * 256 + g * 64:f * 256 + (g + 1) * 64]
            e.tensor_copy(
                wk2[:, f * 512 + g * 128:f * 512 + g * 128 + 64], src)
            last = e.tensor_copy(
                wk2[:, f * 512 + g * 128 + 64:f * 512 + (g + 1) * 128], src)
        return last

    def f_dup(e):
        e.wait_ge(SEM["sg"], sg_wl0 + 16 * 8)
        emit_dup(e, 0)

    jspec = ([("k", g, qb) for g in range(4) for qb in range(NQB)]
             + [("q", c, qb) for qb in range(NQB) for c in range(NCH)]
             + [("v", vc, qb) for vc in range(2) for qb in range(NQB)])

    jobs = []
    for n, (kind, c, qb) in enumerate(jspec):
        def mm_fn(e, ps, n=n, kind=kind, c=c, qb=qb):
            if n == 0:
                e.wait_ge(SEM["sg"], sg_h)
            if n == 16:
                e.wait_ge(SEM["sg"], sg_wl0 + 16 * 16)
            if n == 48:
                e.wait_ge(SEM["sg"], sg_wl0 + 16 * 24)
            for g in range(4):
                qb2 = n - (4 * g + 5)
                if 0 <= qb2 <= 3:
                    sl = (g % 2) * 4 + qb2
                    e.matmul(ps_k2[qb2 % 2][:, :], onestop[:, 0:1],
                             q2s[:, sl * QB:(sl + 1) * QB],
                             start=True, stop=True)
            if n >= 26 and (n - 26) % 8 == 0 and (n - 26) // 8 <= 3:
                qb3 = (n - 26) // 8
                for c2 in range(NCH):
                    e.matmul(ps_n[:, :], selq[:, c2 * 16:(c2 + 1) * 16],
                             q2s[:, c2 * QB:(c2 + 1) * QB],
                             start=(c2 == 0), stop=(c2 == NCH - 1))
            if n == 52:
                e.transpose(ps_tqeb[0:1, 0:16], qmax16[:, 0:1],
                            ident32[0:16, 0:16])
            if n == 54:
                e.wait_ge(SEM["sv"], v0p + 2 + 53)
                e.matmul(ps_tqeb[:, 16:17], ones1_32[0:1, :], bval[0:1, 0:1],
                         start=True, stop=True)
            for f in range(NCH):
                if kind == "q":
                    statn = wq16[:, f * D + c * 128:f * D + (c + 1) * 128]
                elif kind == "k":
                    statn = wk2[:, f * 512 + c * 128:
                                f * 512 + (c + 1) * 128]
                else:
                    statn = wv16[:, f * 256 + c * 128:
                                 f * 256 + (c + 1) * 128]
                mm = e.matmul(ps[:, :], statn,
                              xs[:, f * T + qb * QB:f * T + (qb + 1) * QB],
                              start=(f == 0), stop=(f == NCH - 1))
            return mm

        def ev_fn(e, ps, n=n, kind=kind, c=c, qb=qb):
            if kind == "q":
                dst = Q16[:, c * T + qb * QB:c * T + (qb + 1) * QB]
                bias = bq_sb[:, c:c + 1]
            elif kind == "k":
                dst = K2[:, c * T + qb * QB:c * T + (qb + 1) * QB]
                bias = bk_sb[:, c:c + 1]
            else:
                dst = V1[:, c * T + qb * QB:c * T + (qb + 1) * QB]
                bias = bv_sb[:, c:c + 1]
            last = e.tensor_scalar_add(dst, ps[:, :], bias)
            if n < 3:
                last = emit_dup(e, n + 1)
            for g in range(4):
                if n == 4 * g + 3:
                    for j in range(NQB):
                        s_ = K2[:, g * T + j * QB:g * T + (j + 1) * QB]
                        sl = (g % 2) * 4 + j
                        last = e.tensor_mul(q2s[:, sl * QB:(sl + 1) * QB],
                                            s_, s_)
                qb2 = n - (4 * g + 5)
                if 0 <= qb2 <= 3:
                    e.reduce_max(krow1[:, 0:1], ps_k2[qb2 % 2][0:1, :],
                                 axis=AXX)
                    last = e.tensor_max(kmax1[:, 0:1], kmax1[:, 0:1],
                                        krow1[:, 0:1])
            if 17 <= n <= 48:
                kp, cp, qp = jspec[n - 1]
                if kp == "q":
                    s_ = Q16[:, cp * T + qp * QB:cp * T + (qp + 1) * QB]
                    last = e.tensor_mul(q2s[:, cp * QB:(cp + 1) * QB],
                                        s_, s_)
            if n >= 26 and (n - 26) % 8 == 0 and (n - 26) // 8 <= 3:
                e.reduce_max(rmax16[:, 0:1], ps_n[:, :], axis=AXX)
                last = e.tensor_max(qmax16[:, 0:1], qmax16[:, 0:1],
                                    rmax16[:, 0:1])
            if n == 52:
                e.reduce_max(qM[0:1, 0:1], ps_tqeb[0:1, 0:16], axis=AXX)
                last = e.tensor_mul(b2[0:1, 0:1], qM[0:1, 0:1],
                                    kmax1[0:1, 0:1])
            if n == 53:
                e.wait_ge(SEM["ss"], s0p + 1)
                e.tensor_single_scalar(bval[0:1, 0:1], bnd[0:1, 0:1],
                                       float(-1.001 / 8.0), ALU.mult)
                last = e.tensor_single_scalar(bval[0:1, 0:1], bval[0:1, 0:1],
                                              float(-0.001 / 8.0), ALU.add)
            if n == 54:
                last = e.tensor_copy(expbias[:, 0:1], ps_tqeb[:, 16:17])
            return last
        jobs.append((mm_fn, ev_fn))

    def f_bnd(e):
        e.wait_ge(SEM["sv"], v0p + 2 + 52)
        e.activation(out=bnd[0:1, 0:1], in_=b2[0:1, 0:1], func=AF.Ln,
                     scale=1.0, bias=zero1[0:1, 0:1])
        e.activation(out=bnd[0:1, 0:1], in_=bnd[0:1, 0:1], func=AF.Exp,
                     scale=0.5, bias=zero1[0:1, 0:1]).then_inc(SEM["ss"], 1)
    CNT["ss"] += 1

    pipe_phase("proj", jobs, gpsimd=f_wl, pre_vector=f_dup, scalar=f_bnd)
    ps5_ctx.close()

    jobs = []
    for vc in range(2):
        for kt in range(NKT):
            def mm_fn(e, ps, vc=vc, kt=kt):
                return e.transpose(
                    ps[:, :],
                    V1[:, vc * T + kt * 128:vc * T + (kt + 1) * 128],
                    identB[:, :])

            def ev_fn(e, ps, vc=vc, kt=kt):
                for gg in range(2):
                    g = 2 * vc + gg
                    base = (g * NKT + kt) * 128
                    e.tensor_copy(V2[:, base:base + 64],
                                  ps[:, gg * 64:gg * 64 + 64])
                    ins = e.memset(V2[:, base + 64:base + 128], 1.0)
                return ins
            jobs.append((mm_fn, ev_fn))
    pipe_phase("vtrans", jobs, ps_shape=(128, 128), ps_dtype=BF16)
    pw_ctx.close()

    evac_ctx = ExitStack()
    rA = evac_ctx.enter_context(nc.sbuf_tensor("rA", [128, 2 * QB2], F32))
    rB = evac_ctx.enter_context(nc.sbuf_tensor("rB", [128, 2 * QB2], F32))
    rC = evac_ctx.enter_context(nc.sbuf_tensor("rC", [128, 2 * QB2], F32))


    ps6_ctx = ExitStack()
    ps_s = [ps6_ctx.enter_context(nc.psum_tensor(f"ps_s{h}", [128, QB2], F32))
            for h in range(2)]
    ps_o = [ps6_ctx.enter_context(nc.psum_tensor(f"ps_o{h}", [128, QB2], F32))
            for h in range(2)]
    st0 = CNT["st"]
    sv0 = CNT["sv"]
    ss0 = CNT["ss"]
    sg0 = CNT["sg"]
    NIT = 16

    def attn_tensor(e):
        DL = 3
        for it in range(NIT):
            pr, j = it // 2, it % 2
            g = pr // 2
            itss = ss0 + it * 32
            for kt in range(NKT + DL):
                if kt < NKT:
                    def smm(e, h, half):
                        return e.matmul(
                            ps_s[h][:, half * QB:(half + 1) * QB],
                            K2[h * 64:(h + 1) * 64,
                               g * T + kt * 128:g * T + (kt + 1) * 128],
                            Q16[h * 64:(h + 1) * 64,
                                pr * T + j * QB2 + half * QB:
                                pr * T + j * QB2 + (half + 1) * QB],
                            start=True, stop=True,
                            tile_position=(h * 64, 0))
                    if kt > 0:
                        e.wait_ge(SEM["ss"], itss + 2 * (kt - 1) + 1)
                    else:
                        e.wait_ge(SEM["ss"], itss - 1 if it > 0 else 0)
                    smm(e, 0, 0)
                    smm(e, 0, 1).then_inc(SEM["st"], 1)
                    if kt > 0:
                        e.wait_ge(SEM["ss"], itss + 2 * (kt - 1) + 2)
                    else:
                        e.wait_ge(SEM["ss"], itss)
                    smm(e, 1, 0)
                    smm(e, 1, 1).then_inc(SEM["st"], 1)
                    CNT["st"] += 2
                if kt >= DL:
                    kc = kt - DL
                    e.wait_ge(SEM["ss"], itss + 2 * kc + 2)
                    vb = (g * NKT + kc) * 128
                    for h in range(2):
                        if kc == 0 and it > 0:
                            e.wait_ge(SEM["sv"], sv0 + 3 * it - 2 + h)
                        pt = PT[:, (h * 4 + kc % 4) * QB2:
                                (h * 4 + kc % 4 + 1) * QB2]
                        for half in range(2):
                            mm = e.matmul(
                                ps_o[h][:, half * QB:(half + 1) * QB],
                                V2[:, vb:vb + 128],
                                pt[:, half * QB:(half + 1) * QB],
                                start=(kc == 0), stop=(kc == NKT - 1))
                    mm.then_inc(SEM["st"], 1)
                    CNT["st"] += 1

    def attn_scalar(e):
        DL = 3
        for it in range(NIT):
            itst = st0 + it * 48
            for kt in range(NKT):
                for h in range(2):
                    if kt < DL:
                        e.wait_ge(SEM["st"], itst + 2 * kt + h + 1)
                    else:
                        e.wait_ge(SEM["st"], itst + 3 * kt - DL + h + 1)
                    base = (h * 4 + kt % 4) * QB2
                    e.activation(out=PT[:, base:base + QB2],
                                 in_=ps_s[h][:, :], func=AF.Exp,
                                 scale=0.125,
                                 bias=expbias[:, 0:1]).then_inc(SEM["ss"], 1)
    CNT["ss"] += NIT * 32

    def emit_final(e, it):
        pr, j = it // 2, it % 2
        sl = slice(pr * T + j * QB2, pr * T + (j + 1) * QB2)
        par = (it % 2) * QB2
        e.wait_ge(SEM["sg"], sg0 + 32 * (it + 1))
        e.tensor_mul(tmp32a[0:64, :], rA[0:64, par:par + QB2],
                     rC[0:64, par:par + QB2])
        e.tensor_add(xs[0:64, sl], tmp32a[0:64, :], xs[0:64, sl])
        e.tensor_mul(tmp32a[64:128, :], rC[64:128, par:par + QB2],
                     rB[64:128, par:par + QB2])
        e.tensor_add(xs[64:128, sl], tmp32a[64:128, :], xs[64:128, sl])

    def attn_vector(e):
        for it in range(NIT):
            par = (it % 2) * QB2
            e.wait_ge(SEM["st"], st0 + it * 48 + 48)
            e.tensor_copy(rA[:, par:par + QB2],
                          ps_o[0][:, :]).then_inc(SEM["sv"], 1)
            e.tensor_copy(rB[:, par:par + QB2],
                          ps_o[1][:, :]).then_inc(SEM["sv"], 1)
            e.reciprocal(rA[64:128, par:par + QB2], rA[64:128, par:par + QB2])
            e.reciprocal(rB[64:128, par:par + QB2], rB[64:128, par:par + QB2])
            e.drain().then_inc(SEM["sv"], 1)
            if it > 0:
                emit_final(e, it - 1)
        emit_final(e, NIT - 1)
    CNT["sv"] += NIT * 3

    def attn_gpsimd(e):
        for it in range(NIT):
            par = (it % 2) * QB2
            e.wait_ge(SEM["sv"], sv0 + 3 * it + 3)
            dma(e, rC[0:64, par:par + QB2],
                rA[64:128, par:par + QB2], "sg")
            dma(e, rC[64:128, par:par + QB2],
                rB[0:64, par:par + QB2], "sg")

    phase("attn", tensor=attn_tensor, scalar=attn_scalar, vector=attn_vector,
          gpsimd=attn_gpsimd)
    ps6_ctx.close()
    evac_ctx.close()
    attn_ctx.close()

    post_ctx = ExitStack()
    out32 = post_ctx.enter_context(
        nc.sbuf_tensor("out32", [128, NCH * T], F32))
    xsq = post_ctx.enter_context(
        nc.sbuf_tensor("xsq", [128, NCH * T], BF16))

    def emit_ln(tag, src16, srcapply, dstget, store=False):
        lnctx = ExitStack()
        ps_a = lnctx.enter_context(nc.psum_tensor(f"ps_{tag}a", [1, QB], F32))
        ps_b = lnctx.enter_context(nc.psum_tensor(f"ps_{tag}b", [1, QB], F32))
        ps_i = lnctx.enter_context(
            nc.psum_tensor(f"ps_{tag}i", [128, QB], F32))
        ps_m = lnctx.enter_context(
            nc.psum_tensor(f"ps_{tag}m", [128, QB], F32))
        t0 = CNT["st"]
        v0 = CNT["sv"]
        s0 = CNT["ss"]

        def f_tensor(e):
            for qb in range(NQB):
                e.wait_ge(SEM["sv"], v0 + (1 if qb == 0 else 3 * qb - 1))
                for c in range(NCH):
                    e.matmul(ps_a[:, :], ones128_16[:, 0:1], src16(c, qb),
                             start=(c == 0), stop=(c == NCH - 1))
                for c in range(NCH):
                    mm = e.matmul(ps_b[:, :], ones128_16[:, 0:1],
                                  xsq[:, c * T + qb * QB:c * T + (qb + 1) * QB],
                                  start=(c == 0), stop=(c == NCH - 1))
                mm.then_inc(SEM["st"], 1)
                e.wait_ge(SEM["sv"], v0 + 3 * qb + 3)
                e.matmul(ps_i[:, :], ones1_16[0:1, :], r_inv[0:1, :],
                         start=True, stop=True)
                e.matmul(ps_m[:, :], ones1_16[0:1, :], r_nminv[0:1, :],
                         start=True, stop=True).then_inc(SEM["st"], 1)
        CNT["st"] += 2 * NQB

        def f_vector(e):
            for c in range(NCH):
                for qb in range(NQB):
                    s = src16(c, qb)
                    e.tensor_mul(
                        xsq[:, c * T + qb * QB:c * T + (qb + 1) * QB], s, s)
            e.drain().then_inc(SEM["sv"], 1)
            for qb in range(NQB):
                e.wait_ge(SEM["st"], t0 + 2 * qb + 1)
                e.tensor_single_scalar(s_mean[0:1, :], ps_a[0:1, :],
                                       float(1.0 / D), ALU.mult)
                e.tensor_mul(s_msq[0:1, :], s_mean[0:1, :], s_mean[0:1, :])
                e.tensor_single_scalar(s_var[0:1, :], ps_b[0:1, :],
                                       float(1.0 / D), ALU.mult)
                e.tensor_sub(s_var[0:1, :], s_var[0:1, :],
                             s_msq[0:1, :]).then_inc(SEM["sv"], 1)
                e.wait_ge(SEM["ss"], s0 + qb + 1)
                e.tensor_mul(s_minv[0:1, :], s_mean[0:1, :], s_inv[0:1, :])
                e.tensor_copy(r_inv[0:1, :], s_inv[0:1, :])
                e.tensor_single_scalar(
                    r_nminv[0:1, :], s_minv[0:1, :], -1.0,
                    ALU.mult).then_inc(SEM["sv"], 1)
                e.wait_ge(SEM["st"], t0 + 2 * qb + 2)
                last = None
                for c in range(NCH):
                    e.tensor_mul(tmp32a[:, 0:QB], srcapply(c, qb), ps_i[:, :])
                    last = e.tensor_add(dstget(c, qb), tmp32a[:, 0:QB],
                                        ps_m[:, :])
                last.then_inc(SEM["sv"], 1)
        CNT["sv"] += 1 + 3 * NQB

        def f_scalar(e):
            for qb in range(NQB):
                e.wait_ge(SEM["sv"], v0 + 3 * qb + 2)
                e.activation(out=s_inv[0:1, :], in_=s_var[0:1, :], func=AF.Ln,
                             scale=1.0, bias=eps1[0:1, 0:1])
                e.activation(out=s_inv[0:1, :], in_=s_inv[0:1, :],
                             func=AF.Exp, scale=-0.5,
                             bias=zero1[0:1, 0:1]).then_inc(SEM["ss"], 1)
        CNT["ss"] += NQB

        f_sync = None
        if store:
            def f_sync(e):
                for qb in range(NQB):
                    e.wait_ge(SEM["sv"], v0 + 3 * qb + 4)
                    for c in range(NCH):
                        dma(e, yT[c * 128:(c + 1) * 128,
                                  qb * QB:(qb + 1) * QB],
                            out32[:, c * T + qb * QB:c * T + (qb + 1) * QB],
                            "sd")
        phase(f"ln_{tag}", tensor=f_tensor, vector=f_vector, scalar=f_scalar,
              sync=f_sync)
        lnctx.close()

    lf_ctx = ExitStack()
    ps_a = lf_ctx.enter_context(nc.psum_tensor("ps_lfa", [1, QB], F32))
    ps_b = lf_ctx.enter_context(nc.psum_tensor("ps_lfb", [1, QB], F32))
    ps_i = lf_ctx.enter_context(nc.psum_tensor("ps_lfi", [128, QB], F32))
    ps_i2 = lf_ctx.enter_context(nc.psum_tensor("ps_lfi2", [128, QB], F32))
    ps_m2 = lf_ctx.enter_context(nc.psum_tensor("ps_lfm2", [128, QB], F32))
    pp = [lf_ctx.enter_context(nc.psum_tensor(f"pp_lf{i}", [128, QB], F32))
          for i in range(2)]
    t0 = CNT["st"]
    v0 = CNT["sv"]
    s0 = CNT["ss"]
    q2f = q2s.bitcast(F16)
    sbc = tmp32a[:, QB:2 * QB]

    TORD = (["cs_0", "cs_1", "st1_0", "st1_1", "ffn_0_0", "ffn_0_1",
             "bc1_0"] + [f"ffn_0_{b}" for b in range(2, NCH)]
            + ["st1_2", "ffn_1_0", "ffn_1_1", "bc1_1"]
            + [f"ffn_1_{b}" for b in range(2, NCH)]
            + ["st1_3", "ffn_2_0", "ffn_2_1", "bc1_2", "ffn_2_2", "ffn_2_3",
               "st2_0"] + [f"ffn_2_{b}" for b in range(4, NCH)]
            + ["ffn_3_0", "ffn_3_1", "bc1_3", "bc2_0"]
            + [f"ffn_3_{b}" for b in range(2, NCH)]
            + ["st2_1", "bc2_1", "st2_2", "bc2_2", "st2_3", "bc2_3"])
    VORD = (["csv_0", "csv_1", "sq1_0", "v1a_0", "v2a_0", "sq1_1", "v1a_1",
             "v2a_1", "sq1_2", "bcc_0"] + [f"vm_0_{b}" for b in range(NCH)]
            + ["v1a_2", "v2a_2", "sq1_3", "bcc_1"]
            + [f"vm_1_{b}" for b in range(NCH)]
            + ["v1a_3", "v2a_3", "bcc_2"]
            + [f"vm_2_{b}" for b in range(NCH)]
            + ["v1b_0", "v2b_0", "bcc_3"]
            + [f"vm_3_{b}" for b in range(NCH)]
            + ["v1b_1", "v2b_1", "ap2_0", "v1b_2", "v2b_2", "ap2_1",
               "v1b_3", "v2b_3", "ap2_2", "ap2_3"])
    SORD = (["rs1_0", "rs1_1"] + [f"sf_0_{b}" for b in range(NCH)]
            + ["cast_0", "sq2_0", "rs1_2"]
            + [f"sf_1_{b}" for b in range(NCH)]
            + ["cast_1", "sq2_1", "rs1_3", "rs2_0"]
            + [f"sf_2_{b}" for b in range(NCH)]
            + ["cast_2", "sq2_2", "rs2_1"]
            + [f"sf_3_{b}" for b in range(NCH)]
            + ["cast_3", "sq2_3", "rs2_2", "rs2_3"])
    TP = {nm: t0 + i + 1 for i, nm in enumerate(TORD)}
    VP = {nm: v0 + i + 1 for i, nm in enumerate(VORD)}
    SP_ = {nm: s0 + i + 1 for i, nm in enumerate(SORD)}
    AB_READER = {"st1_1": "v1a_0", "st1_2": "v1a_1", "st1_3": "v1a_2",
                 "st2_0": "v1a_3", "st2_1": "v1b_0", "st2_2": "v1b_1",
                 "st2_3": "v1b_2"}

    def xsl(c, qb):
        return xs[:, c * T + qb * QB:c * T + (qb + 1) * QB]

    def osl(c, qb):
        return out32[:, c * T + qb * QB:c * T + (qb + 1) * QB]

    def lf_tensor(e):
        ffjobs = []
        for nm in TORD:
            parts = nm.split("_")
            kind, q = parts[0], int(parts[1])
            if kind == "cs":
                dst = ps_i if q == 0 else ps_i2
                for a in range(NCH):
                    mm = e.matmul(dst[0:1, :], ones128_16[:, 0:1],
                                  M16[:, a * D + q * QB:a * D + (q + 1) * QB],
                                  start=(a == 0), stop=(a == NCH - 1))
                mm.then_inc(SEM["st"], 1)
            elif kind in ("st1", "st2"):
                if kind == "st1":
                    e.wait_ge(SEM["sv"], VP[f"sq1_{q}"])
                else:
                    e.wait_ge(SEM["ss"], SP_[f"sq2_{q}"])
                e.wait_ge(SEM["sv"], VP[AB_READER[nm]] if nm in AB_READER
                          else VP["csv_1"])
                for c in range(NCH):
                    e.matmul(ps_a[:, :], ones128_16[:, 0:1], xsl(c, q),
                             start=(c == 0), stop=(c == NCH - 1))
                for c in range(NCH):
                    mm = e.matmul(ps_b[:, :], ones128_16[:, 0:1],
                                  xsq[:, c * T + q * QB:c * T + (q + 1) * QB],
                                  start=(c == 0), stop=(c == NCH - 1))
                mm.then_inc(SEM["st"], 1)
            elif kind == "bc1":
                e.wait_ge(SEM["sv"], VP[f"v2a_{q}"])
                e.wait_ge(SEM["sv"], VP[f"bcc_{q - 1}"] if q > 0
                          else VP["csv_0"])
                e.matmul(ps_i[:, :], ones1_16[0:1, :], r_inv[0:1, :],
                         start=True, stop=True).then_inc(SEM["st"], 1)
            elif kind == "bc2":
                e.wait_ge(SEM["sv"], VP[f"v2b_{q}"])
                if q > 0:
                    e.wait_ge(SEM["sv"], VP[f"ap2_{q - 1}"])
                e.matmul(ps_i2[:, :], ones1_16[0:1, :], r_inv2[0:1, :],
                         start=True, stop=True)
                e.matmul(ps_m2[:, :], ones1_16[0:1, :], r_nminv2[0:1, :],
                         start=True, stop=True).then_inc(SEM["st"], 1)
            else:
                b = int(parts[2])
                j = len(ffjobs)
                ffjobs.append(nm)
                if b == 0:
                    e.wait_ge(SEM["sv"], VP[f"v1a_{q}"])
                if j >= 2:
                    e.wait_ge(SEM["sv"], VP[ffjobs[j - 2].replace(
                        "ffn", "vm")])
                e.matmul(pp[j % 2][:, :],
                         q2f[0:1, b * 128:(b + 1) * 128],
                         q2f[0:1, 2048 + q * QB:2048 + (q + 1) * QB],
                         start=True, stop=False)
                for a in range(NCH):
                    mm = e.matmul(
                        pp[j % 2][:, :],
                        M16[:, a * D + b * 128:a * D + (b + 1) * 128],
                        xsl(a, q),
                        start=False, stop=(a == NCH - 1))
                mm.then_inc(SEM["st"], 1)
    CNT["st"] += len(TORD)

    def lf_vector(e):
        ffj = [f"ffn_{q}_{b}" for q in range(NQB) for b in range(NCH)]
        for nm in VORD:
            parts = nm.split("_")
            kind, q = parts[0], int(parts[1])
            if kind == "csv":
                e.wait_ge(SEM["st"], TP[f"cs_{q}"])
                src_ = (ps_i if q == 0 else ps_i2)[0:1, :]
                e.tensor_copy(q2f[0:1, q * QB:(q + 1) * QB],
                              src_).then_inc(SEM["sv"], 1)
            elif kind == "sq1":
                for c in range(NCH):
                    s = xsl(c, q)
                    mm = e.tensor_mul(
                        xsq[:, c * T + q * QB:c * T + (q + 1) * QB], s, s)
                mm.then_inc(SEM["sv"], 1)
            elif kind in ("v1a", "v1b"):
                e.wait_ge(SEM["st"], TP[("st1" if kind == "v1a" else "st2")
                                        + f"_{q}"])
                e.tensor_single_scalar(s_mean[0:1, :], ps_a[0:1, :],
                                       float(1.0 / D), ALU.mult)
                e.tensor_mul(s_msq[0:1, :], s_mean[0:1, :], s_mean[0:1, :])
                e.tensor_single_scalar(s_var[0:1, :], ps_b[0:1, :],
                                       float(1.0 / D), ALU.mult)
                last = e.tensor_sub(s_var[0:1, :], s_var[0:1, :],
                                    s_msq[0:1, :])
                if kind == "v1a":
                    last = e.tensor_single_scalar(
                        q2f[0:1, 2048 + q * QB:2048 + (q + 1) * QB],
                        s_mean[0:1, :], -1.0, ALU.mult)
                last.then_inc(SEM["sv"], 1)
            elif kind in ("v2a", "v2b"):
                e.wait_ge(SEM["ss"], SP_[("rs1" if kind == "v2a" else "rs2")
                                         + f"_{q}"])
                if kind == "v2a":
                    e.tensor_copy(r_inv[0:1, :],
                                  s_inv[0:1, :]).then_inc(SEM["sv"], 1)
                else:
                    e.tensor_mul(s_minv[0:1, :], s_mean[0:1, :],
                                 s_inv[0:1, :])
                    e.tensor_copy(r_inv2[0:1, :], s_inv[0:1, :])
                    e.tensor_single_scalar(
                        r_nminv2[0:1, :], s_minv[0:1, :], -1.0,
                        ALU.mult).then_inc(SEM["sv"], 1)
            elif kind == "bcc":
                e.wait_ge(SEM["st"], TP[f"bc1_{q}"])
                e.tensor_copy(sbc, ps_i[:, :]).then_inc(SEM["sv"], 1)
            elif kind == "vm":
                b = int(parts[2])
                e.wait_ge(SEM["st"], TP[f"ffn_{q}_{b}"])
                j = ffj.index(f"ffn_{q}_{b}")
                e.tensor_mul(osl(b, q), pp[j % 2][:, :],
                             sbc).then_inc(SEM["sv"], 1)
            else:
                e.wait_ge(SEM["st"], TP[f"bc2_{q}"])
                for c in range(NCH):
                    e.tensor_mul(tmp32a[:, 0:QB], osl(c, q), ps_i2[:, :])
                    last = e.tensor_add(osl(c, q), tmp32a[:, 0:QB],
                                        ps_m2[:, :])
                last.then_inc(SEM["sv"], 1)
    CNT["sv"] += len(VORD)

    def lf_scalar(e):
        for nm in SORD:
            parts = nm.split("_")
            kind, q = parts[0], int(parts[1])
            if kind == "cast":
                for c in range(NCH):
                    mm = e.activation(out=xsl(c, q), in_=osl(c, q),
                                      func=AF.Copy, scale=1.0, bias=0.0)
                mm.then_inc(SEM["ss"], 1)
            elif kind == "sq2":
                for c in range(NCH):
                    mm = e.activation(
                        out=xsq[:, c * T + q * QB:c * T + (q + 1) * QB],
                        in_=xsl(c, q), func=AF.Square, scale=1.0, bias=0.0)
                mm.then_inc(SEM["ss"], 1)
            elif kind == "sf":
                b = int(parts[2])
                e.wait_ge(SEM["sv"], VP[f"vm_{q}_{b}"])
                e.activation(out=osl(b, q), in_=osl(b, q),
                             func=AF.Identity, scale=1.0,
                             bias=bf_sb[:, b:b + 1]).then_inc(SEM["ss"], 1)
            else:
                e.wait_ge(SEM["sv"], VP[("v1a" if kind == "rs1" else "v1b")
                                        + f"_{q}"])
                e.activation(out=s_inv[0:1, :], in_=s_var[0:1, :],
                             func=AF.Ln, scale=1.0, bias=eps1[0:1, 0:1])
                e.activation(out=s_inv[0:1, :], in_=s_inv[0:1, :],
                             func=AF.Exp, scale=-0.5,
                             bias=zero1[0:1, 0:1]).then_inc(SEM["ss"], 1)
    CNT["ss"] += len(SORD)

    def lf_sync(e):
        for q in range(NQB):
            e.wait_ge(SEM["sv"], VP[f"ap2_{q}"])
            for c in range(NCH):
                dma(e, yT[c * 128:(c + 1) * 128, q * QB:(q + 1) * QB],
                    out32[:, c * T + q * QB:c * T + (q + 1) * QB], "sd")

    phase("post", tensor=lf_tensor, vector=lf_vector, scalar=lf_scalar,
          sync=lf_sync)
    lf_ctx.close()

    if dbg:
        p = SP()

        def f_dbg(e):
            if "M16" in dbg:
                dma(e, dbg["M16"][:, :], M16[:, :], "sg")
            if "xs" in dbg:
                dma(e, dbg["xs"][:, :], xs[:, :], "sg")
            if "expbias" in dbg:
                dma(e, dbg["expbias"][:, :], expbias[:, :], "sg")
        p.step("gpsimd", f_dbg)
        p.emit("dbg")

    p = SP()
    p.step("sync", lambda e: e.wait_ge(SEM["sd"], CNT["sd"]))
    p.emit("fin")

    return nc


_NC_CACHE = {}


def kernel(L0_attn_h, L0_attn_wq_w, L0_attn_wq_b, L0_attn_wk_w, L0_attn_wk_b,
           L0_attn_wv_w, L0_attn_wv_b, L0_ffn_w_freq_real, L0_ffn_w_freq_imag,
           L0_ffn_bias):
    h = np.asarray(L0_attn_h, dtype=np.float32)
    wq = np.asarray(L0_attn_wq_w, dtype=np.float32)
    wk = np.asarray(L0_attn_wk_w, dtype=np.float32)
    wv = np.asarray(L0_attn_wv_w, dtype=np.float32)
    wfr = np.asarray(L0_ffn_w_freq_real, dtype=np.float32)
    wfi = np.asarray(L0_ffn_w_freq_imag, dtype=np.float32)
    bq = np.asarray(L0_attn_wq_b, dtype=np.float32)
    bk = np.asarray(L0_attn_wk_b, dtype=np.float32)
    bv = np.asarray(L0_attn_wv_b, dtype=np.float32)
    bf = np.asarray(L0_ffn_bias, dtype=np.float32)

    wqT = np.ascontiguousarray(wq.T)
    wkT = np.ascontiguousarray(wk.T)
    wvT = np.ascontiguousarray(wv.T)
    wfrT = np.ascontiguousarray(wfr.T)
    wfiT = np.ascontiguousarray(wfi.T)
    bq2 = np.ascontiguousarray(bq.reshape(8, 128).T)
    bk2 = np.ascontiguousarray(np.tile(bk.reshape(4, 64), (1, 2)).T)
    bv2 = np.ascontiguousarray(bv.reshape(2, 128).T)
    bf2 = np.ascontiguousarray(bf.reshape(8, 128).T)

    if "nc" not in _NC_CACHE:
        _NC_CACHE["nc"] = build_program()
    nc = _NC_CACHE["nc"]

    in_maps = []
    for b in range(8):
        in_maps.append({
            "hT": np.ascontiguousarray(h[b].T),
            "wqT": wqT, "wkT": wkT, "wvT": wvT,
            "wfrT": wfrT, "wfiT": wfiT,
            "bq_in": bq2, "bk_in": bk2, "bv_in": bv2, "bf_in": bf2,
        })
    res = run_bass_kernel_spmd(nc, in_maps, core_ids=list(range(8)))
    out = np.stack([np.asarray(r["yT"]).T for r in res.results], axis=0)
    return np.ascontiguousarray(out.astype(np.float32))


# version 21
# speedup vs baseline: 1.0673x; 1.0223x over previous
from contextlib import ExitStack

import numpy as np
import ml_dtypes
import concourse.bass as bass
import concourse.mybir as mybir
from concourse.bass_utils import run_bass_kernel_spmd

F32 = mybir.dt.float32
F16 = mybir.dt.float16
BF16 = mybir.dt.bfloat16
I32 = mybir.dt.int32
AF = mybir.ActivationFunctionType
ALU = mybir.AluOpType
AXX = mybir.AxisListType.X

D = 1024
T = 2048
NCH = 8
NKT = 16
QB = 512
NQB = T // QB
QB2 = 1024
EPS = 1e-5


def _load_act_table(nc, e, set_name):
    from concourse.hw_specs import get_activation_tables
    for idx, nm in enumerate(get_activation_tables(nc.m.arch)):
        if nm == set_name:
            return e.add_instruction(
                mybir.InstLoadActFuncSet(
                    name=nc.get_next_instruction_name(),
                    act_func_set_id=idx, ins=[], outs=[]))
    raise ValueError(set_name)


def build_program(debug=()):
    nc = bass.Bass("TRN2", target_bir_lowering=False,
                   detect_race_conditions=False)

    hT = nc.dram_tensor("hT", [D, T], F32, kind="ExternalInput").ap()
    wqT = nc.dram_tensor("wqT", [D, D], F32, kind="ExternalInput").ap()
    wkT = nc.dram_tensor("wkT", [D, 256], F32, kind="ExternalInput").ap()
    wvT = nc.dram_tensor("wvT", [D, 256], F32, kind="ExternalInput").ap()
    wfrT = nc.dram_tensor("wfrT", [D, D], F32, kind="ExternalInput").ap()
    wfiT = nc.dram_tensor("wfiT", [D, D], F32, kind="ExternalInput").ap()
    bq_in = nc.dram_tensor("bq_in", [128, 8], F32, kind="ExternalInput").ap()
    bk_in = nc.dram_tensor("bk_in", [128, 4], F32, kind="ExternalInput").ap()
    bv_in = nc.dram_tensor("bv_in", [128, 2], F32, kind="ExternalInput").ap()
    bf_in = nc.dram_tensor("bf_in", [128, 8], F32, kind="ExternalInput").ap()
    yT = nc.dram_tensor("yT", [D, T], F32, kind="ExternalOutput").ap()

    dbg = {}
    for name, shape, dt in debug:
        dbg[name] = nc.dram_tensor("dbg_" + name, shape, dt,
                                   kind="ExternalOutput").ap()

    iota_np = np.arange(D, dtype=np.float32)[None, :]
    ident32_np = np.eye(128, dtype=np.float32)
    identB_np = np.eye(128, dtype=np.float32).astype(ml_dtypes.bfloat16)
    selq_np = np.zeros((128, 128), dtype=np.float32)
    for c in range(8):
        selq_np[0:64, c * 16 + 2 * c] = 1.0
        selq_np[64:128, c * 16 + 2 * c + 1] = 1.0
    selq_np = selq_np.astype(np.float16)
    onestop_np = np.zeros((128, 1), dtype=np.float32)
    onestop_np[0:64, 0] = 1.0
    onestop_np = onestop_np.astype(np.float16)

    iota_d = nc.inline_tensor(iota_np, name="iota_c").ap()
    ident32_d = nc.inline_tensor(ident32_np, name="ident32_c").ap()
    identB_d = nc.inline_tensor(identB_np, name="identB_c").ap()
    selq_d = nc.inline_tensor(selq_np, name="selq_c").ap()
    onestop_d = nc.inline_tensor(onestop_np, name="onestop_c").ap()

    ctx = ExitStack()
    SEM = {}
    for s in ("sd", "sg", "st", "sv", "ss", "ph", "tok"):
        SEM[s] = ctx.enter_context(nc.semaphore(s))
    CNT = {k: 0 for k in SEM}
    PHN = [0]
    TOK = [0]

    def dma(e, out, in_, sem):
        e.dma_start(out=out, in_=in_).then_inc(SEM[sem], 16)
        CNT[sem] += 16

    def phase(name, sync=None, tensor=None, vector=None, scalar=None,
              gpsimd=None):
        k = PHN[0]

        def wrap(fn):
            def inner(e):
                e.wait_ge(SEM["ph"], 5 * k)
                if fn is not None:
                    fn(e)
                e.drain().then_inc(SEM["ph"], 1)
            return inner
        with nc.named_scope(name), nc.Block(name) as blk:
            blk.sync(wrap(sync))
            blk.tensor(wrap(tensor))
            blk.vector(wrap(vector))
            blk.scalar(wrap(scalar))
            blk.gpsimd(wrap(gpsimd))
        PHN[0] += 1

    class SP:

        def __init__(self):
            self.steps = []

        def step(self, eng, fn):
            self.steps.append((eng, fn))

        def emit(self, name):
            base = TOK[0]
            steps = list(self.steps)

            def mk(engname):
                def fn(e):
                    for i, (en, f) in enumerate(steps):
                        if en == engname:
                            e.wait_ge(SEM["tok"], base + i)
                            f(e)
                            e.drain().then_inc(SEM["tok"], 1)
                return fn
            phase(name, sync=mk("sync"), tensor=mk("tensor"),
                  vector=mk("vector"), scalar=mk("scalar"),
                  gpsimd=mk("gpsimd"))
            TOK[0] += len(steps)

    def pipe_phase(name, jobs, gpsimd=None, scalar=None, pre_tensor=None,
                   pre_vector=None, ps_shape=(128, QB), ps_dtype=F32):
        t0 = CNT["st"]
        v0 = CNT["sv"]
        pvi = 1 if pre_vector is not None else 0
        pp_ctx = ExitStack()
        pss = [pp_ctx.enter_context(
            nc.psum_tensor(f"pp_{name}{i}", list(ps_shape), ps_dtype))
            for i in range(2)]

        def ft(e):
            if pre_tensor is not None:
                pre_tensor(e)
            for n, (mm_fn, _) in enumerate(jobs):
                if n >= 2:
                    e.wait_ge(SEM["sv"], v0 + pvi + n - 1)
                elif pvi:
                    e.wait_ge(SEM["sv"], v0 + 1)
                mm_fn(e, pss[n % 2]).then_inc(SEM["st"], 1)

        def fv(e):
            if pre_vector is not None:
                pre_vector(e)
                e.drain().then_inc(SEM["sv"], 1)
            for n, (_, ev_fn) in enumerate(jobs):
                e.wait_ge(SEM["st"], t0 + n + 1)
                ev_fn(e, pss[n % 2]).then_inc(SEM["sv"], 1)

        CNT["st"] += len(jobs)
        CNT["sv"] += len(jobs) + pvi
        phase(name, tensor=ft, vector=fv, gpsimd=gpsimd, scalar=scalar)
        pp_ctx.close()

    xs = nc.alloc_sbuf_tensor("xs", [128, NCH * T], F16)
    M16 = nc.alloc_sbuf_tensor("M16", [128, NCH * D], F16)
    iota_sb = nc.alloc_sbuf_tensor("iota_sb", [1, D], F32)
    ident32 = nc.alloc_sbuf_tensor("ident32", [128, 128], F32)
    identB = nc.alloc_sbuf_tensor("identB", [128, 128], BF16)
    selq = nc.alloc_sbuf_tensor("selq", [128, 128], F16)
    onestop = nc.alloc_sbuf_tensor("onestop", [128, 1], F16)
    bq_sb = nc.alloc_sbuf_tensor("bq_sb", [128, 8], F32)
    bk_sb = nc.alloc_sbuf_tensor("bk_sb", [128, 4], F32)
    bv_sb = nc.alloc_sbuf_tensor("bv_sb", [128, 2], F32)
    bf_sb = nc.alloc_sbuf_tensor("bf_sb", [128, 8], F32)
    expbias = nc.alloc_sbuf_tensor("expbias", [128, 1], F32)
    negpi = nc.alloc_sbuf_tensor("negpi", [128, 1], F32)
    eps1 = nc.alloc_sbuf_tensor("eps1", [1, 1], F32)
    zero1 = nc.alloc_sbuf_tensor("zero1", [1, 1], F32)
    ones1_32 = nc.alloc_sbuf_tensor("ones1_32", [1, 128], F32)
    ones1_16 = nc.alloc_sbuf_tensor("ones1_16", [1, 128], F16)
    ones128_16 = nc.alloc_sbuf_tensor("ones128_16", [128, 1], F16)
    tmp32a = nc.alloc_sbuf_tensor("tmp32a", [128, QB2], F32)
    q2s = nc.alloc_sbuf_tensor("q2s", [128, NCH * QB], BF16)
    s_mean = nc.alloc_sbuf_tensor("s_mean", [1, QB], F32)
    s_var = nc.alloc_sbuf_tensor("s_var", [1, QB], F32)
    s_msq = tmp32a[0:1, 0:QB]
    s_minv = tmp32a[0:1, QB:2 * QB]
    s_inv = nc.alloc_sbuf_tensor("s_inv", [1, QB], F32)
    r_inv = nc.alloc_sbuf_tensor("r_inv", [1, QB], F16)
    r_nminv = nc.alloc_sbuf_tensor("r_nminv", [1, QB], F16)
    r_inv2 = nc.alloc_sbuf_tensor("r_inv2", [1, QB], F16)
    r_nminv2 = nc.alloc_sbuf_tensor("r_nminv2", [1, QB], F16)
    qmax16 = nc.alloc_sbuf_tensor("qmax16", [16, 1], F32)
    rmax16 = nc.alloc_sbuf_tensor("rmax16", [16, 1], F32)
    kmax1 = nc.alloc_sbuf_tensor("kmax1", [1, 1], F32)
    krow1 = nc.alloc_sbuf_tensor("krow1", [1, 1], F32)
    qM = nc.alloc_sbuf_tensor("qM", [1, 1], F32)
    b2 = nc.alloc_sbuf_tensor("b2", [1, 1], F32)
    bnd = nc.alloc_sbuf_tensor("bnd", [1, 1], F32)
    bval = nc.alloc_sbuf_tensor("bval", [1, 1], F32)

    pre_ctx = ExitStack()
    C16 = pre_ctx.enter_context(nc.sbuf_tensor("C16", [128, NCH * D], F16))
    S16 = pre_ctx.enter_context(nc.sbuf_tensor("S16", [128, NCH * D], F16))
    Sn16 = pre_ctx.enter_context(nc.sbuf_tensor("Sn16", [128, NCH * D], F16))
    A_ctx = ExitStack()
    A16r = A_ctx.enter_context(nc.sbuf_tensor("A16r", [128, NCH * D], F16))
    A16i = A_ctx.enter_context(nc.sbuf_tensor("A16i", [128, NCH * D], F16))
    w_ctx = ExitStack()
    wr16 = w_ctx.enter_context(nc.sbuf_tensor("wr16", [128, NCH * D], F16))
    wi16 = w_ctx.enter_context(nc.sbuf_tensor("wi16", [128, NCH * D], F16))
    tw_ctx = ExitStack()
    ik32 = tw_ctx.enter_context(nc.sbuf_tensor("ik32", [128, D], I32))
    yci = tw_ctx.enter_context(nc.sbuf_tensor("yci", [128, D], I32))
    ycf = tw_ctx.enter_context(nc.sbuf_tensor("ycf", [128, D], F32))
    ysf = tw_ctx.enter_context(nc.sbuf_tensor("ysf", [128, D], F32))
    ps_tw_ctx = ExitStack()
    ps_dk = [ps_tw_ctx.enter_context(
        nc.psum_tensor(f"ps_dk{hh}", [128, QB], F32)) for hh in range(4)]

    sd0 = CNT["sd"]
    st0 = CNT["st"]
    sv0 = CNT["sv"]
    ss0 = CNT["ss"]

    def init_sync(e):
        dma(e, iota_sb[:, :], iota_d[:, :], "sd")
        dma(e, ident32[:, :], ident32_d[:, :], "sd")
        dma(e, identB[:, :], identB_d[:, :], "sd")
        dma(e, selq[:, :], selq_d[:, :], "sd")
        dma(e, onestop[:, :], onestop_d[:, :], "sd")
        dma(e, bq_sb[:, :], bq_in[:, :], "sd")
        dma(e, bk_sb[:, :], bk_in[:, :], "sd")
        dma(e, bv_sb[:, :], bv_in[:, :], "sd")
        dma(e, bf_sb[:, :], bf_in[:, :], "sd")

    def init_gpsimd(e):
        for c in range(NCH):
            dma(e, xs[:, c * T:(c + 1) * T], hT[c * 128:(c + 1) * 128, :],
                "sg")
        for c in range(NCH):
            dma(e, wr16[:, c * D:(c + 1) * D],
                wfrT[c * 128:(c + 1) * 128, :], "sg")
            dma(e, wi16[:, c * D:(c + 1) * D],
                wfiT[c * 128:(c + 1) * 128, :], "sg")

    def init_tensor(e):
        e.wait_ge(SEM["sd"], sd0 + 16)
        for c in range(NCH):
            if c >= 2:
                e.wait_ge(SEM["sv"], sv0 + c)
            for half in range(2):
                mm = e.matmul(ps_dk[(c % 2) * 2 + half][:, :],
                              iota_sb[0:1, c * 128:(c + 1) * 128],
                              iota_sb[0:1, half * QB:(half + 1) * QB],
                              start=True, stop=True)
            mm.then_inc(SEM["st"], 1)
    CNT["st"] += NCH

    def init_vector(e):
        e.memset(negpi[:, :], float(-np.pi))
        e.memset(eps1[:, :], float(EPS))
        e.memset(zero1[:, :], 0.0)
        e.memset(ones1_32[:, :], 1.0)
        e.memset(ones1_16[:, :], 1.0)
        e.memset(ones128_16[:, :], 1.0)
        e.memset(qmax16[:, :], 0.0)
        e.memset(kmax1[:, :], 0.0)
        e.drain().then_inc(SEM["sv"], 1)
        for c in range(NCH):
            e.wait_ge(SEM["st"], st0 + c + 1)
            if c >= 1:
                e.wait_ge(SEM["ss"], ss0 + c)
            e.tensor_copy(ik32[:, 0:QB], ps_dk[(c % 2) * 2][:, :])
            e.tensor_copy(ik32[:, QB:D], ps_dk[(c % 2) * 2 + 1][:, :])
            e.tensor_single_scalar(yci[:, :], ik32[:, :], 768, ALU.add)
            e.tensor_single_scalar(yci[:, :], yci[:, :], 1023, ALU.bitwise_and)
            e.tensor_copy(ycf[:, :], yci[:, :])
            e.tensor_single_scalar(yci[:, :], ik32[:, :], 512, ALU.add)
            e.tensor_single_scalar(yci[:, :], yci[:, :], 1023, ALU.bitwise_and)
            e.tensor_copy(ysf[:, :], yci[:, :])
            e.drain().then_inc(SEM["sv"], 1)
        e.wait_ge(SEM["ss"], ss0 + NCH)
        e.tensor_single_scalar(Sn16[:, :], S16[:, :], -1.0, ALU.mult)
    CNT["sv"] += NCH + 1

    def init_scalar(e):
        _load_act_table(nc, e, "trig_and_small")
        sc = float(2 * np.pi / D)
        for c in range(NCH):
            e.wait_ge(SEM["sv"], sv0 + c + 2)
            e.activation(out=C16[:, c * D:(c + 1) * D], in_=ycf[:, :],
                         func=AF.Sin, scale=sc, bias=negpi[:, 0:1])
            e.activation(out=S16[:, c * D:(c + 1) * D], in_=ysf[:, :],
                         func=AF.Sin, scale=sc,
                         bias=negpi[:, 0:1]).then_inc(SEM["ss"], 1)
        _load_act_table(nc, e, "natural_log_exp_and_others")
    CNT["ss"] += NCH

    phase("init", sync=init_sync, gpsimd=init_gpsimd, tensor=init_tensor,
          vector=init_vector, scalar=init_scalar)
    sd_base = CNT["sd"]
    sg_h0 = CNT["sg"]
    sg_h = sg_h0 - 16 * 16
    sg_w = CNT["sg"]
    ps_tw_ctx.close()
    tw_ctx.close()

    jobs = []
    for dst in range(2):
        for dchunk in range(NCH):
            for half in range(2):
                def mm_fn(e, ps, dst=dst, dchunk=dchunk, half=half,
                          first=(dst == 0 and dchunk == 0 and half == 0)):
                    if first:
                        e.wait_ge(SEM["sg"], sg_w)
                    t1 = C16 if dst == 0 else S16
                    t2 = Sn16 if dst == 0 else C16
                    for o in range(NCH):
                        e.matmul(ps[:, :],
                                 wr16[:, o * D + dchunk * 128:
                                      o * D + (dchunk + 1) * 128],
                                 t1[:, o * D + half * QB:
                                    o * D + (half + 1) * QB],
                                 start=(o == 0), stop=False)
                    for o in range(NCH):
                        mm = e.matmul(ps[:, :],
                                      wi16[:, o * D + dchunk * 128:
                                           o * D + (dchunk + 1) * 128],
                                      t2[:, o * D + half * QB:
                                         o * D + (half + 1) * QB],
                                      start=False, stop=(o == NCH - 1))
                    return mm

                def ev_fn(e, ps, dst=dst, dchunk=dchunk, half=half):
                    A = A16r if dst == 0 else A16i
                    return e.tensor_single_scalar(
                        A[:, dchunk * D + half * QB:
                          dchunk * D + (half + 1) * QB],
                        ps[:, :], float(1.0 / D), ALU.mult)
                jobs.append((mm_fn, ev_fn))
    pipe_phase("pre_A", jobs)
    w_ctx.close()

    jobs = []
    for achunk in range(NCH):
        for half in range(2):
            def mm_fn(e, ps, achunk=achunk, half=half):
                for dd in range(NCH):
                    e.matmul(ps[:, :],
                             C16[:, dd * D + achunk * 128:
                                 dd * D + (achunk + 1) * 128],
                             A16r[:, dd * D + half * QB:
                                  dd * D + (half + 1) * QB],
                             start=(dd == 0), stop=False)
                for dd in range(NCH):
                    mm = e.matmul(ps[:, :],
                                  S16[:, dd * D + achunk * 128:
                                      dd * D + (achunk + 1) * 128],
                                  A16i[:, dd * D + half * QB:
                                       dd * D + (half + 1) * QB],
                                  start=False, stop=(dd == NCH - 1))
                return mm

            def ev_fn(e, ps, achunk=achunk, half=half):
                last = e.tensor_copy(
                    M16[:, achunk * D + half * QB:
                        achunk * D + (half + 1) * QB], ps[:, :])
                if achunk // 4 == half:
                    off = achunk * D + achunk * 128
                    last = e.tensor_add(M16[:, off:off + 128],
                                        M16[:, off:off + 128], ident32[:, :])
                return last
            jobs.append((mm_fn, ev_fn))
    pipe_phase("pre_M", jobs)
    A_ctx.close()
    pre_ctx.close()

    attn_ctx = ExitStack()
    Q16 = attn_ctx.enter_context(nc.sbuf_tensor("Q16", [128, NCH * T], F16))
    K2 = attn_ctx.enter_context(nc.sbuf_tensor("K2", [128, 4 * T], F16))
    V2 = attn_ctx.enter_context(
        nc.sbuf_tensor("V2", [128, 4 * NKT * 128], BF16))
    PT = attn_ctx.enter_context(nc.sbuf_tensor("PT", [128, 8 * QB2], BF16))
    pw_ctx = ExitStack()
    wq16 = pw_ctx.enter_context(nc.sbuf_tensor("wq16", [128, NCH * D], F16))
    wk1 = pw_ctx.enter_context(nc.sbuf_tensor("wk1", [128, NCH * 256], F16))
    wk2 = pw_ctx.enter_context(nc.sbuf_tensor("wk2", [128, NCH * 512], F16))
    wv16 = pw_ctx.enter_context(nc.sbuf_tensor("wv16", [128, NCH * 256], BF16))
    V1 = pw_ctx.enter_context(nc.sbuf_tensor("V1", [128, 2 * T], BF16))

    ps5_ctx = ExitStack()
    ps_n = ps5_ctx.enter_context(nc.psum_tensor("ps_n", [16, QB], F32))
    ps_k2 = [ps5_ctx.enter_context(nc.psum_tensor(f"ps_k{i}", [1, QB], F32))
             for i in range(2)]
    ps_tqeb = ps5_ctx.enter_context(nc.psum_tensor("ps_tqeb", [128, 32], F32))
    sg_wl0 = CNT["sg"]
    t0p = CNT["st"]
    v0p = CNT["sv"]
    s0p = CNT["ss"]

    def f_wl(e):
        for c in range(NCH):
            dma(e, wk1[:, c * 256:(c + 1) * 256],
                wkT[c * 128:(c + 1) * 128, :], "sg")
        for c in range(NCH):
            dma(e, wq16[:, c * D:(c + 1) * D],
                wqT[c * 128:(c + 1) * 128, :], "sg")
        for c in range(NCH):
            dma(e, wv16[:, c * 256:(c + 1) * 256],
                wvT[c * 128:(c + 1) * 128, :], "sg")

    def emit_dup(e, g):
        for f in range(NCH):
            src = wk1[:, f * 256 + g * 64:f * 256 + (g + 1) * 64]
            e.tensor_copy(
                wk2[:, f * 512 + g * 128:f * 512 + g * 128 + 64], src)
            last = e.tensor_copy(
                wk2[:, f * 512 + g * 128 + 64:f * 512 + (g + 1) * 128], src)
        return last

    def f_dup(e):
        e.wait_ge(SEM["sg"], sg_wl0 + 16 * 8)
        emit_dup(e, 0)

    jspec = ([("k", g, qb) for g in range(4) for qb in range(NQB)]
             + [("q", c, qb) for qb in range(NQB) for c in range(NCH)]
             + [("v", vc, qb) for vc in range(2) for qb in range(NQB)])

    jobs = []
    for n, (kind, c, qb) in enumerate(jspec):
        def mm_fn(e, ps, n=n, kind=kind, c=c, qb=qb):
            if n == 0:
                e.wait_ge(SEM["sg"], sg_h)
            if n == 16:
                e.wait_ge(SEM["sg"], sg_wl0 + 16 * 16)
            if n == 48:
                e.wait_ge(SEM["sg"], sg_wl0 + 16 * 24)
            for g in range(4):
                qb2 = n - (4 * g + 5)
                if 0 <= qb2 <= 3:
                    sl = (g % 2) * 4 + qb2
                    e.matmul(ps_k2[qb2 % 2][:, :], onestop[:, 0:1],
                             q2s[:, sl * QB:(sl + 1) * QB],
                             start=True, stop=True)
            if n >= 25 and (n - 25) % 8 == 0 and (n - 25) // 8 <= 3:
                qb3 = (n - 25) // 8
                e.wait_ge(SEM["sv"], v0p + 2 + 24 + 8 * qb3)
                for c2 in range(NCH):
                    e.matmul(ps_n[:, :], selq[:, c2 * 16:(c2 + 1) * 16],
                             q2s[:, c2 * QB:(c2 + 1) * QB],
                             start=(c2 == 0), stop=(c2 == NCH - 1))
            if n == 51:
                e.transpose(ps_tqeb[0:1, 0:16], qmax16[:, 0:1],
                            ident32[0:16, 0:16])
            if n == 53:
                e.wait_ge(SEM["sv"], v0p + 2 + 52)
                e.matmul(ps_tqeb[:, 16:17], ones1_32[0:1, :], bval[0:1, 0:1],
                         start=True, stop=True)
            for f in range(NCH):
                if kind == "q":
                    statn = wq16[:, f * D + c * 128:f * D + (c + 1) * 128]
                elif kind == "k":
                    statn = wk2[:, f * 512 + c * 128:
                                f * 512 + (c + 1) * 128]
                else:
                    statn = wv16[:, f * 256 + c * 128:
                                 f * 256 + (c + 1) * 128]
                mm = e.matmul(ps[:, :], statn,
                              xs[:, f * T + qb * QB:f * T + (qb + 1) * QB],
                              start=(f == 0), stop=(f == NCH - 1))
            return mm

        def ev_fn(e, ps, n=n, kind=kind, c=c, qb=qb):
            if kind == "q":
                dst = Q16[:, c * T + qb * QB:c * T + (qb + 1) * QB]
                bias = bq_sb[:, c:c + 1]
            elif kind == "k":
                dst = K2[:, c * T + qb * QB:c * T + (qb + 1) * QB]
                bias = bk_sb[:, c:c + 1]
            else:
                dst = V1[:, c * T + qb * QB:c * T + (qb + 1) * QB]
                bias = bv_sb[:, c:c + 1]
            last = e.tensor_scalar_add(dst, ps[:, :], bias)
            if n < 3:
                last = emit_dup(e, n + 1)
            for g in range(4):
                if n == 4 * g + 3:
                    for j in range(NQB):
                        s_ = K2[:, g * T + j * QB:g * T + (j + 1) * QB]
                        sl = (g % 2) * 4 + j
                        last = e.tensor_mul(q2s[:, sl * QB:(sl + 1) * QB],
                                            s_, s_)
                qb2 = n - (4 * g + 5)
                if 0 <= qb2 <= 3:
                    e.reduce_max(krow1[:, 0:1], ps_k2[qb2 % 2][0:1, :],
                                 axis=AXX)
                    last = e.tensor_max(kmax1[:, 0:1], kmax1[:, 0:1],
                                        krow1[:, 0:1])
            if 17 <= n <= 48:
                kp, cp, qp = jspec[n - 1]
                if kp == "q":
                    s_ = Q16[:, cp * T + qp * QB:cp * T + (qp + 1) * QB]
                    last = e.tensor_mul(q2s[:, cp * QB:(cp + 1) * QB],
                                        s_, s_)
            if n >= 25 and (n - 25) % 8 == 0 and (n - 25) // 8 <= 3:
                e.reduce_max(rmax16[:, 0:1], ps_n[:, :], axis=AXX)
                last = e.tensor_max(qmax16[:, 0:1], qmax16[:, 0:1],
                                    rmax16[:, 0:1])
            if n == 51:
                e.reduce_max(qM[0:1, 0:1], ps_tqeb[0:1, 0:16], axis=AXX)
                last = e.tensor_mul(b2[0:1, 0:1], qM[0:1, 0:1],
                                    kmax1[0:1, 0:1])
            if n == 52:
                e.wait_ge(SEM["ss"], s0p + 1)
                e.tensor_single_scalar(bval[0:1, 0:1], bnd[0:1, 0:1],
                                       float(-1.001 / 8.0), ALU.mult)
                last = e.tensor_single_scalar(bval[0:1, 0:1], bval[0:1, 0:1],
                                              float(-0.001 / 8.0), ALU.add)
            if n == 53:
                last = e.tensor_copy(expbias[:, 0:1], ps_tqeb[:, 16:17])
            return last
        jobs.append((mm_fn, ev_fn))

    def f_bnd(e):
        e.wait_ge(SEM["sv"], v0p + 2 + 51)
        e.activation(out=bnd[0:1, 0:1], in_=b2[0:1, 0:1], func=AF.Ln,
                     scale=1.0, bias=zero1[0:1, 0:1])
        e.activation(out=bnd[0:1, 0:1], in_=bnd[0:1, 0:1], func=AF.Exp,
                     scale=0.5, bias=zero1[0:1, 0:1]).then_inc(SEM["ss"], 1)
    CNT["ss"] += 1

    pipe_phase("proj", jobs, gpsimd=f_wl, pre_vector=f_dup, scalar=f_bnd)
    ps5_ctx.close()

    jobs = []
    for vc in range(2):
        for kt in range(NKT):
            def mm_fn(e, ps, vc=vc, kt=kt):
                return e.transpose(
                    ps[:, :],
                    V1[:, vc * T + kt * 128:vc * T + (kt + 1) * 128],
                    identB[:, :])

            def ev_fn(e, ps, vc=vc, kt=kt):
                for gg in range(2):
                    g = 2 * vc + gg
                    base = (g * NKT + kt) * 128
                    e.tensor_copy(V2[:, base:base + 64],
                                  ps[:, gg * 64:gg * 64 + 64])
                    ins = e.memset(V2[:, base + 64:base + 128], 1.0)
                return ins
            jobs.append((mm_fn, ev_fn))
    pipe_phase("vtrans", jobs, ps_shape=(128, 128), ps_dtype=BF16)
    pw_ctx.close()

    evac_ctx = ExitStack()
    rA = evac_ctx.enter_context(nc.sbuf_tensor("rA", [128, 2 * QB2], F32))
    rB = evac_ctx.enter_context(nc.sbuf_tensor("rB", [128, 2 * QB2], F32))
    rC = evac_ctx.enter_context(nc.sbuf_tensor("rC", [128, 2 * QB2], F32))


    ps6_ctx = ExitStack()
    ps_s = [ps6_ctx.enter_context(nc.psum_tensor(f"ps_s{h}", [128, QB2], F32))
            for h in range(2)]
    ps_o = [ps6_ctx.enter_context(nc.psum_tensor(f"ps_o{h}", [128, QB2], F32))
            for h in range(2)]
    st0 = CNT["st"]
    sv0 = CNT["sv"]
    ss0 = CNT["ss"]
    sg0 = CNT["sg"]
    NIT = 16

    def attn_tensor(e):
        DL = 1
        for it in range(NIT):
            pr, j = it // 2, it % 2
            g = pr // 2
            itss = ss0 + it * 32
            for kt in range(NKT + DL):
                if kt < NKT:
                    def smm(e, h, half):
                        return e.matmul(
                            ps_s[h][:, half * QB:(half + 1) * QB],
                            K2[h * 64:(h + 1) * 64,
                               g * T + kt * 128:g * T + (kt + 1) * 128],
                            Q16[h * 64:(h + 1) * 64,
                                pr * T + j * QB2 + half * QB:
                                pr * T + j * QB2 + (half + 1) * QB],
                            start=True, stop=True,
                            tile_position=(h * 64, 0))
                    if kt > 0:
                        e.wait_ge(SEM["ss"], itss + 2 * (kt - 1) + 1)
                    else:
                        e.wait_ge(SEM["ss"], itss - 1 if it > 0 else 0)
                    smm(e, 0, 0)
                    smm(e, 0, 1).then_inc(SEM["st"], 1)
                    if kt > 0:
                        e.wait_ge(SEM["ss"], itss + 2 * (kt - 1) + 2)
                    else:
                        e.wait_ge(SEM["ss"], itss)
                    smm(e, 1, 0)
                    smm(e, 1, 1).then_inc(SEM["st"], 1)
                    CNT["st"] += 2
                if kt >= DL:
                    kc = kt - DL
                    e.wait_ge(SEM["ss"], itss + 2 * kc + 2)
                    vb = (g * NKT + kc) * 128
                    for h in range(2):
                        if kc == 0 and it > 0:
                            e.wait_ge(SEM["sv"], sv0 + 3 * it - 2 + h)
                        pt = PT[:, (h * 4 + kc % 4) * QB2:
                                (h * 4 + kc % 4 + 1) * QB2]
                        for half in range(2):
                            mm = e.matmul(
                                ps_o[h][:, half * QB:(half + 1) * QB],
                                V2[:, vb:vb + 128],
                                pt[:, half * QB:(half + 1) * QB],
                                start=(kc == 0), stop=(kc == NKT - 1))
                    mm.then_inc(SEM["st"], 1)
                    CNT["st"] += 1

    def attn_scalar(e):
        DL = 1
        for it in range(NIT):
            itst = st0 + it * 48
            for kt in range(NKT):
                for h in range(2):
                    if kt < DL:
                        e.wait_ge(SEM["st"], itst + 2 * kt + h + 1)
                    else:
                        e.wait_ge(SEM["st"], itst + 3 * kt - DL + h + 1)
                    base = (h * 4 + kt % 4) * QB2
                    e.activation(out=PT[:, base:base + QB2],
                                 in_=ps_s[h][:, :], func=AF.Exp,
                                 scale=0.125,
                                 bias=expbias[:, 0:1]).then_inc(SEM["ss"], 1)
    CNT["ss"] += NIT * 32

    def emit_final(e, it):
        pr, j = it // 2, it % 2
        sl = slice(pr * T + j * QB2, pr * T + (j + 1) * QB2)
        par = (it % 2) * QB2
        e.wait_ge(SEM["sg"], sg0 + 32 * (it + 1))
        e.tensor_mul(tmp32a[0:64, :], rA[0:64, par:par + QB2],
                     rC[0:64, par:par + QB2])
        e.tensor_add(xs[0:64, sl], tmp32a[0:64, :], xs[0:64, sl])
        e.tensor_mul(tmp32a[64:128, :], rC[64:128, par:par + QB2],
                     rB[64:128, par:par + QB2])
        e.tensor_add(xs[64:128, sl], tmp32a[64:128, :], xs[64:128, sl])

    def attn_vector(e):
        for it in range(NIT):
            par = (it % 2) * QB2
            e.wait_ge(SEM["st"], st0 + it * 48 + 48)
            e.tensor_copy(rA[:, par:par + QB2],
                          ps_o[0][:, :]).then_inc(SEM["sv"], 1)
            e.tensor_copy(rB[:, par:par + QB2],
                          ps_o[1][:, :]).then_inc(SEM["sv"], 1)
            e.reciprocal(rA[64:128, par:par + QB2], rA[64:128, par:par + QB2])
            e.reciprocal(rB[64:128, par:par + QB2], rB[64:128, par:par + QB2])
            e.drain().then_inc(SEM["sv"], 1)
            if it > 0:
                emit_final(e, it - 1)
        emit_final(e, NIT - 1)
    CNT["sv"] += NIT * 3

    def attn_gpsimd(e):
        for it in range(NIT):
            par = (it % 2) * QB2
            e.wait_ge(SEM["sv"], sv0 + 3 * it + 3)
            dma(e, rC[0:64, par:par + QB2],
                rA[64:128, par:par + QB2], "sg")
            dma(e, rC[64:128, par:par + QB2],
                rB[0:64, par:par + QB2], "sg")

    phase("attn", tensor=attn_tensor, scalar=attn_scalar, vector=attn_vector,
          gpsimd=attn_gpsimd)
    ps6_ctx.close()
    evac_ctx.close()
    attn_ctx.close()

    post_ctx = ExitStack()
    out32 = post_ctx.enter_context(
        nc.sbuf_tensor("out32", [128, NCH * T], F32))
    xsq = post_ctx.enter_context(
        nc.sbuf_tensor("xsq", [128, NCH * T], BF16))

    def emit_ln(tag, src16, srcapply, dstget, store=False):
        lnctx = ExitStack()
        ps_a = lnctx.enter_context(nc.psum_tensor(f"ps_{tag}a", [1, QB], F32))
        ps_b = lnctx.enter_context(nc.psum_tensor(f"ps_{tag}b", [1, QB], F32))
        ps_i = lnctx.enter_context(
            nc.psum_tensor(f"ps_{tag}i", [128, QB], F32))
        ps_m = lnctx.enter_context(
            nc.psum_tensor(f"ps_{tag}m", [128, QB], F32))
        t0 = CNT["st"]
        v0 = CNT["sv"]
        s0 = CNT["ss"]

        def f_tensor(e):
            for qb in range(NQB):
                e.wait_ge(SEM["sv"], v0 + (1 if qb == 0 else 3 * qb - 1))
                for c in range(NCH):
                    e.matmul(ps_a[:, :], ones128_16[:, 0:1], src16(c, qb),
                             start=(c == 0), stop=(c == NCH - 1))
                for c in range(NCH):
                    mm = e.matmul(ps_b[:, :], ones128_16[:, 0:1],
                                  xsq[:, c * T + qb * QB:c * T + (qb + 1) * QB],
                                  start=(c == 0), stop=(c == NCH - 1))
                mm.then_inc(SEM["st"], 1)
                e.wait_ge(SEM["sv"], v0 + 3 * qb + 3)
                e.matmul(ps_i[:, :], ones1_16[0:1, :], r_inv[0:1, :],
                         start=True, stop=True)
                e.matmul(ps_m[:, :], ones1_16[0:1, :], r_nminv[0:1, :],
                         start=True, stop=True).then_inc(SEM["st"], 1)
        CNT["st"] += 2 * NQB

        def f_vector(e):
            for c in range(NCH):
                for qb in range(NQB):
                    s = src16(c, qb)
                    e.tensor_mul(
                        xsq[:, c * T + qb * QB:c * T + (qb + 1) * QB], s, s)
            e.drain().then_inc(SEM["sv"], 1)
            for qb in range(NQB):
                e.wait_ge(SEM["st"], t0 + 2 * qb + 1)
                e.tensor_single_scalar(s_mean[0:1, :], ps_a[0:1, :],
                                       float(1.0 / D), ALU.mult)
                e.tensor_mul(s_msq[0:1, :], s_mean[0:1, :], s_mean[0:1, :])
                e.tensor_single_scalar(s_var[0:1, :], ps_b[0:1, :],
                                       float(1.0 / D), ALU.mult)
                e.tensor_sub(s_var[0:1, :], s_var[0:1, :],
                             s_msq[0:1, :]).then_inc(SEM["sv"], 1)
                e.wait_ge(SEM["ss"], s0 + qb + 1)
                e.tensor_mul(s_minv[0:1, :], s_mean[0:1, :], s_inv[0:1, :])
                e.tensor_copy(r_inv[0:1, :], s_inv[0:1, :])
                e.tensor_single_scalar(
                    r_nminv[0:1, :], s_minv[0:1, :], -1.0,
                    ALU.mult).then_inc(SEM["sv"], 1)
                e.wait_ge(SEM["st"], t0 + 2 * qb + 2)
                last = None
                for c in range(NCH):
                    e.tensor_mul(tmp32a[:, 0:QB], srcapply(c, qb), ps_i[:, :])
                    last = e.tensor_add(dstget(c, qb), tmp32a[:, 0:QB],
                                        ps_m[:, :])
                last.then_inc(SEM["sv"], 1)
        CNT["sv"] += 1 + 3 * NQB

        def f_scalar(e):
            for qb in range(NQB):
                e.wait_ge(SEM["sv"], v0 + 3 * qb + 2)
                e.activation(out=s_inv[0:1, :], in_=s_var[0:1, :], func=AF.Ln,
                             scale=1.0, bias=eps1[0:1, 0:1])
                e.activation(out=s_inv[0:1, :], in_=s_inv[0:1, :],
                             func=AF.Exp, scale=-0.5,
                             bias=zero1[0:1, 0:1]).then_inc(SEM["ss"], 1)
        CNT["ss"] += NQB

        f_sync = None
        if store:
            def f_sync(e):
                for qb in range(NQB):
                    e.wait_ge(SEM["sv"], v0 + 3 * qb + 4)
                    for c in range(NCH):
                        dma(e, yT[c * 128:(c + 1) * 128,
                                  qb * QB:(qb + 1) * QB],
                            out32[:, c * T + qb * QB:c * T + (qb + 1) * QB],
                            "sd")
        phase(f"ln_{tag}", tensor=f_tensor, vector=f_vector, scalar=f_scalar,
              sync=f_sync)
        lnctx.close()

    lf_ctx = ExitStack()
    ps_a = lf_ctx.enter_context(nc.psum_tensor("ps_lfa", [1, QB], F32))
    ps_b = lf_ctx.enter_context(nc.psum_tensor("ps_lfb", [1, QB], F32))
    ps_i = lf_ctx.enter_context(nc.psum_tensor("ps_lfi", [128, QB], F32))
    ps_i2 = lf_ctx.enter_context(nc.psum_tensor("ps_lfi2", [128, QB], F32))
    ps_m2 = lf_ctx.enter_context(nc.psum_tensor("ps_lfm2", [128, QB], F32))
    pp = [lf_ctx.enter_context(nc.psum_tensor(f"pp_lf{i}", [128, QB], F32))
          for i in range(2)]
    t0 = CNT["st"]
    v0 = CNT["sv"]
    s0 = CNT["ss"]
    q2f = q2s.bitcast(F16)
    sbc = tmp32a[:, QB:2 * QB]

    TORD = (["cs_0", "cs_1", "st1_0", "st1_1", "ffn_0_0", "ffn_0_1",
             "bc1_0"] + [f"ffn_0_{b}" for b in range(2, NCH)]
            + ["st1_2", "ffn_1_0", "ffn_1_1", "bc1_1"]
            + [f"ffn_1_{b}" for b in range(2, NCH)]
            + ["st1_3", "ffn_2_0", "ffn_2_1", "bc1_2", "ffn_2_2", "ffn_2_3",
               "st2_0"] + [f"ffn_2_{b}" for b in range(4, NCH)]
            + ["ffn_3_0", "ffn_3_1", "bc1_3", "bc2_0"]
            + [f"ffn_3_{b}" for b in range(2, NCH)]
            + ["st2_1", "bc2_1", "st2_2", "bc2_2", "st2_3", "bc2_3"])
    VORD = (["csv_0", "csv_1", "sq1_0", "v1a_0", "v2a_0", "sq1_1", "v1a_1",
             "v2a_1", "sq1_2", "bcc_0"] + [f"vm_0_{b}" for b in range(NCH)]
            + ["v1a_2", "v2a_2", "sq1_3", "bcc_1"]
            + [f"vm_1_{b}" for b in range(NCH)]
            + ["v1a_3", "v2a_3", "bcc_2"]
            + [f"vm_2_{b}" for b in range(NCH)]
            + ["v1b_0", "v2b_0", "bcc_3"]
            + [f"vm_3_{b}" for b in range(NCH)]
            + ["v1b_1", "v2b_1", "ap2_0", "v1b_2", "v2b_2", "ap2_1",
               "v1b_3", "v2b_3", "ap2_2", "ap2_3"])
    SORD = (["rs1_0", "rs1_1"] + [f"sf_0_{b}" for b in range(NCH)]
            + ["cast_0", "sq2_0", "rs1_2"]
            + [f"sf_1_{b}" for b in range(NCH)]
            + ["cast_1", "sq2_1", "rs1_3", "rs2_0"]
            + [f"sf_2_{b}" for b in range(NCH)]
            + ["cast_2", "sq2_2", "rs2_1"]
            + [f"sf_3_{b}" for b in range(NCH)]
            + ["cast_3", "sq2_3", "rs2_2", "rs2_3"])
    TP = {nm: t0 + i + 1 for i, nm in enumerate(TORD)}
    VP = {nm: v0 + i + 1 for i, nm in enumerate(VORD)}
    SP_ = {nm: s0 + i + 1 for i, nm in enumerate(SORD)}
    AB_READER = {"st1_1": "v1a_0", "st1_2": "v1a_1", "st1_3": "v1a_2",
                 "st2_0": "v1a_3", "st2_1": "v1b_0", "st2_2": "v1b_1",
                 "st2_3": "v1b_2"}

    def xsl(c, qb):
        return xs[:, c * T + qb * QB:c * T + (qb + 1) * QB]

    def osl(c, qb):
        return out32[:, c * T + qb * QB:c * T + (qb + 1) * QB]

    def lf_tensor(e):
        ffjobs = []
        for nm in TORD:
            parts = nm.split("_")
            kind, q = parts[0], int(parts[1])
            if kind == "cs":
                dst = ps_i if q == 0 else ps_i2
                for a in range(NCH):
                    mm = e.matmul(dst[0:1, :], ones128_16[:, 0:1],
                                  M16[:, a * D + q * QB:a * D + (q + 1) * QB],
                                  start=(a == 0), stop=(a == NCH - 1))
                mm.then_inc(SEM["st"], 1)
            elif kind in ("st1", "st2"):
                if kind == "st1":
                    e.wait_ge(SEM["sv"], VP[f"sq1_{q}"])
                else:
                    e.wait_ge(SEM["ss"], SP_[f"sq2_{q}"])
                e.wait_ge(SEM["sv"], VP[AB_READER[nm]] if nm in AB_READER
                          else VP["csv_1"])
                for c in range(NCH):
                    e.matmul(ps_a[:, :], ones128_16[:, 0:1], xsl(c, q),
                             start=(c == 0), stop=(c == NCH - 1))
                for c in range(NCH):
                    mm = e.matmul(ps_b[:, :], ones128_16[:, 0:1],
                                  xsq[:, c * T + q * QB:c * T + (q + 1) * QB],
                                  start=(c == 0), stop=(c == NCH - 1))
                mm.then_inc(SEM["st"], 1)
            elif kind == "bc1":
                e.wait_ge(SEM["sv"], VP[f"v2a_{q}"])
                e.wait_ge(SEM["sv"], VP[f"bcc_{q - 1}"] if q > 0
                          else VP["csv_0"])
                e.matmul(ps_i[:, :], ones1_16[0:1, :], r_inv[0:1, :],
                         start=True, stop=True).then_inc(SEM["st"], 1)
            elif kind == "bc2":
                e.wait_ge(SEM["sv"], VP[f"v2b_{q}"])
                if q > 0:
                    e.wait_ge(SEM["sv"], VP[f"ap2_{q - 1}"])
                e.matmul(ps_i2[:, :], ones1_16[0:1, :], r_inv2[0:1, :],
                         start=True, stop=True)
                e.matmul(ps_m2[:, :], ones1_16[0:1, :], r_nminv2[0:1, :],
                         start=True, stop=True).then_inc(SEM["st"], 1)
            else:
                b = int(parts[2])
                j = len(ffjobs)
                ffjobs.append(nm)
                if b == 0:
                    e.wait_ge(SEM["sv"], VP[f"v1a_{q}"])
                if j >= 2:
                    e.wait_ge(SEM["sv"], VP[ffjobs[j - 2].replace(
                        "ffn", "vm")])
                e.matmul(pp[j % 2][:, :],
                         q2f[0:1, b * 128:(b + 1) * 128],
                         q2f[0:1, 2048 + q * QB:2048 + (q + 1) * QB],
                         start=True, stop=False)
                for a in range(NCH):
                    mm = e.matmul(
                        pp[j % 2][:, :],
                        M16[:, a * D + b * 128:a * D + (b + 1) * 128],
                        xsl(a, q),
                        start=False, stop=(a == NCH - 1))
                mm.then_inc(SEM["st"], 1)
    CNT["st"] += len(TORD)

    def lf_vector(e):
        ffj = [f"ffn_{q}_{b}" for q in range(NQB) for b in range(NCH)]
        for nm in VORD:
            parts = nm.split("_")
            kind, q = parts[0], int(parts[1])
            if kind == "csv":
                e.wait_ge(SEM["st"], TP[f"cs_{q}"])
                src_ = (ps_i if q == 0 else ps_i2)[0:1, :]
                e.tensor_copy(q2f[0:1, q * QB:(q + 1) * QB],
                              src_).then_inc(SEM["sv"], 1)
            elif kind == "sq1":
                for c in range(NCH):
                    s = xsl(c, q)
                    mm = e.tensor_mul(
                        xsq[:, c * T + q * QB:c * T + (q + 1) * QB], s, s)
                mm.then_inc(SEM["sv"], 1)
            elif kind in ("v1a", "v1b"):
                e.wait_ge(SEM["st"], TP[("st1" if kind == "v1a" else "st2")
                                        + f"_{q}"])
                e.tensor_single_scalar(s_mean[0:1, :], ps_a[0:1, :],
                                       float(1.0 / D), ALU.mult)
                e.tensor_mul(s_msq[0:1, :], s_mean[0:1, :], s_mean[0:1, :])
                e.tensor_single_scalar(s_var[0:1, :], ps_b[0:1, :],
                                       float(1.0 / D), ALU.mult)
                last = e.tensor_sub(s_var[0:1, :], s_var[0:1, :],
                                    s_msq[0:1, :])
                if kind == "v1a":
                    last = e.tensor_single_scalar(
                        q2f[0:1, 2048 + q * QB:2048 + (q + 1) * QB],
                        s_mean[0:1, :], -1.0, ALU.mult)
                last.then_inc(SEM["sv"], 1)
            elif kind in ("v2a", "v2b"):
                e.wait_ge(SEM["ss"], SP_[("rs1" if kind == "v2a" else "rs2")
                                         + f"_{q}"])
                if kind == "v2a":
                    e.tensor_copy(r_inv[0:1, :],
                                  s_inv[0:1, :]).then_inc(SEM["sv"], 1)
                else:
                    e.tensor_mul(s_minv[0:1, :], s_mean[0:1, :],
                                 s_inv[0:1, :])
                    e.tensor_copy(r_inv2[0:1, :], s_inv[0:1, :])
                    e.tensor_single_scalar(
                        r_nminv2[0:1, :], s_minv[0:1, :], -1.0,
                        ALU.mult).then_inc(SEM["sv"], 1)
            elif kind == "bcc":
                e.wait_ge(SEM["st"], TP[f"bc1_{q}"])
                e.tensor_copy(sbc, ps_i[:, :]).then_inc(SEM["sv"], 1)
            elif kind == "vm":
                b = int(parts[2])
                e.wait_ge(SEM["st"], TP[f"ffn_{q}_{b}"])
                j = ffj.index(f"ffn_{q}_{b}")
                e.tensor_mul(osl(b, q), pp[j % 2][:, :],
                             sbc).then_inc(SEM["sv"], 1)
            else:
                e.wait_ge(SEM["st"], TP[f"bc2_{q}"])
                for c in range(NCH):
                    e.tensor_mul(tmp32a[:, 0:QB], osl(c, q), ps_i2[:, :])
                    last = e.tensor_add(osl(c, q), tmp32a[:, 0:QB],
                                        ps_m2[:, :])
                last.then_inc(SEM["sv"], 1)
    CNT["sv"] += len(VORD)

    def lf_scalar(e):
        for nm in SORD:
            parts = nm.split("_")
            kind, q = parts[0], int(parts[1])
            if kind == "cast":
                for c in range(NCH):
                    mm = e.activation(out=xsl(c, q), in_=osl(c, q),
                                      func=AF.Copy, scale=1.0, bias=0.0)
                mm.then_inc(SEM["ss"], 1)
            elif kind == "sq2":
                for c in range(NCH):
                    mm = e.activation(
                        out=xsq[:, c * T + q * QB:c * T + (q + 1) * QB],
                        in_=xsl(c, q), func=AF.Square, scale=1.0, bias=0.0)
                mm.then_inc(SEM["ss"], 1)
            elif kind == "sf":
                b = int(parts[2])
                e.wait_ge(SEM["sv"], VP[f"vm_{q}_{b}"])
                e.activation(out=osl(b, q), in_=osl(b, q),
                             func=AF.Identity, scale=1.0,
                             bias=bf_sb[:, b:b + 1]).then_inc(SEM["ss"], 1)
            else:
                e.wait_ge(SEM["sv"], VP[("v1a" if kind == "rs1" else "v1b")
                                        + f"_{q}"])
                e.activation(out=s_inv[0:1, :], in_=s_var[0:1, :],
                             func=AF.Ln, scale=1.0, bias=eps1[0:1, 0:1])
                e.activation(out=s_inv[0:1, :], in_=s_inv[0:1, :],
                             func=AF.Exp, scale=-0.5,
                             bias=zero1[0:1, 0:1]).then_inc(SEM["ss"], 1)
    CNT["ss"] += len(SORD)

    def lf_sync(e):
        for q in range(NQB):
            e.wait_ge(SEM["sv"], VP[f"ap2_{q}"])
            for c in range(NCH):
                dma(e, yT[c * 128:(c + 1) * 128, q * QB:(q + 1) * QB],
                    out32[:, c * T + q * QB:c * T + (q + 1) * QB], "sd")

    phase("post", tensor=lf_tensor, vector=lf_vector, scalar=lf_scalar,
          sync=lf_sync)
    lf_ctx.close()

    if dbg:
        p = SP()

        def f_dbg(e):
            if "M16" in dbg:
                dma(e, dbg["M16"][:, :], M16[:, :], "sg")
            if "xs" in dbg:
                dma(e, dbg["xs"][:, :], xs[:, :], "sg")
            if "expbias" in dbg:
                dma(e, dbg["expbias"][:, :], expbias[:, :], "sg")
        p.step("gpsimd", f_dbg)
        p.emit("dbg")

    p = SP()
    p.step("sync", lambda e: e.wait_ge(SEM["sd"], CNT["sd"]))
    p.emit("fin")

    return nc


_NC_CACHE = {}


def kernel(L0_attn_h, L0_attn_wq_w, L0_attn_wq_b, L0_attn_wk_w, L0_attn_wk_b,
           L0_attn_wv_w, L0_attn_wv_b, L0_ffn_w_freq_real, L0_ffn_w_freq_imag,
           L0_ffn_bias):
    h = np.asarray(L0_attn_h, dtype=np.float32)
    wq = np.asarray(L0_attn_wq_w, dtype=np.float32)
    wk = np.asarray(L0_attn_wk_w, dtype=np.float32)
    wv = np.asarray(L0_attn_wv_w, dtype=np.float32)
    wfr = np.asarray(L0_ffn_w_freq_real, dtype=np.float32)
    wfi = np.asarray(L0_ffn_w_freq_imag, dtype=np.float32)
    bq = np.asarray(L0_attn_wq_b, dtype=np.float32)
    bk = np.asarray(L0_attn_wk_b, dtype=np.float32)
    bv = np.asarray(L0_attn_wv_b, dtype=np.float32)
    bf = np.asarray(L0_ffn_bias, dtype=np.float32)

    wqT = np.ascontiguousarray(wq.T)
    wkT = np.ascontiguousarray(wk.T)
    wvT = np.ascontiguousarray(wv.T)
    wfrT = np.ascontiguousarray(wfr.T)
    wfiT = np.ascontiguousarray(wfi.T)
    bq2 = np.ascontiguousarray(bq.reshape(8, 128).T)
    bk2 = np.ascontiguousarray(np.tile(bk.reshape(4, 64), (1, 2)).T)
    bv2 = np.ascontiguousarray(bv.reshape(2, 128).T)
    bf2 = np.ascontiguousarray(bf.reshape(8, 128).T)

    if "nc" not in _NC_CACHE:
        _NC_CACHE["nc"] = build_program()
    nc = _NC_CACHE["nc"]

    in_maps = []
    for b in range(8):
        in_maps.append({
            "hT": np.ascontiguousarray(h[b].T),
            "wqT": wqT, "wkT": wkT, "wvT": wvT,
            "wfrT": wfrT, "wfiT": wfiT,
            "bq_in": bq2, "bk_in": bk2, "bv_in": bv2, "bf_in": bf2,
        })
    res = run_bass_kernel_spmd(nc, in_maps, core_ids=list(range(8)))
    out = np.stack([np.asarray(r["yT"]).T for r in res.results], axis=0)
    return np.ascontiguousarray(out.astype(np.float32))
